# version 03
# baseline (speedup 1.0000x reference)
import sys
sys.path.insert(0, "/opt/trn_rl_repo")
import contextlib
import numpy as np
import concourse.bass as bass
import concourse.mybir as mybir

F32 = mybir.dt.float32
BF16 = mybir.dt.bfloat16
AF = mybir.ActivationFunctionType
ALU = mybir.AluOpType
AX = mybir.AxisListType

ENGINES = ("sync", "vector", "scalar", "tensor", "gpsimd")


class Prog:
    def __init__(self, nc, es, loop_n=1):
        self.nc = nc
        self.es = es
        self.loop_n = loop_n
        self.seg = "pre"
        self.ops = {e: [] for e in ENGINES}
        self.n_dma = {e: 0 for e in ENGINES}
        self.n_cmp = {e: 0 for e in ENGINES}
        self.last_writer = {}
        self.readers = {}
        self.cur_barrier = []
        self.loop_var = {}
        self.sem_c = {e: es.enter_context(nc.semaphore(f"sc_{e}")) for e in ENGINES}
        nds = {"sync": 8, "gpsimd": 4, "scalar": 4, "vector": 2, "tensor": 2}
        self.ndsem = nds
        self.sem_d = {e: [es.enter_context(nc.semaphore(f"sd_{e}{i}"))
                          for i in range(nds[e])] for e in ENGINES}

    def op(self, eng, fn, reads=(), writes=(), dma=False, tag="", barrier_op=False,
           extra_deps=()):
        rec = dict(eng=eng, fn=fn, dma=dma, deps=[], inc=False, tag=tag,
                   seg=self.seg, bar=barrier_op)
        if dma:
            rec["ord"] = self.n_dma[eng]
            self.n_dma[eng] += 1
        else:
            rec["ord"] = self.n_cmp[eng]
            self.n_cmp[eng] += 1
        deps = list(extra_deps)
        for k in reads:
            lw = self.last_writer.get(k)
            if lw is not None:
                deps.append(lw)
        for k in writes:
            lw = self.last_writer.get(k)
            if lw is not None:
                deps.append(lw)
            for r in self.readers.get(k, ()):
                deps.append(r)
        for b in self.cur_barrier:
            if b["eng"] != eng:
                deps.append(b)
        seen = set()
        for d in deps:
            i = id(d)
            if i in seen or d is rec:
                continue
            seen.add(i)
            if (d["eng"] == eng == "tensor") and not d["dma"] and not dma:
                continue
            rec["deps"].append(d)
            d["inc"] = True
        for k in reads:
            self.readers.setdefault(k, []).append(rec)
        for k in writes:
            self.readers[k] = []
            self.last_writer[k] = rec
        self.ops[eng].append(rec)
        return rec

    def dma(self, eng, out, in_, reads=(), writes=(), tag="", **kw):
        return self.op(eng, lambda e: e.dma_start(out=out, in_=in_, **kw),
                       reads=reads, writes=writes, dma=True, tag=tag)

    def barrier(self, tag="bar"):
        lasts = []
        for e in ENGINES:
            for rec in reversed(self.ops[e]):
                lasts.append(rec)
                break
        slot_last = {}
        for e in ENGINES:
            for rec in self.ops[e]:
                if rec["dma"]:
                    slot_last[(e, rec["ord"] % self.ndsem[e])] = rec
        lasts += list(slot_last.values())
        bar_ops = []
        for eng in ENGINES:
            deps = [d for d in lasts]
            rec = self.op(eng,
                          lambda e, eng=eng: e.sem_inc(self.sem_c[eng], 1),
                          dma=False, tag=tag, barrier_op=True, extra_deps=deps)
            rec["inc"] = False
            bar_ops.append(rec)
        self.cur_barrier = bar_ops
        return bar_ops

    def begin_loop(self):
        self.barrier(tag="bar.preloop")
        self.seg = "body"

    def end_loop(self):
        self.barrier(tag="bar.bodyend")
        self.seg = "post"

    def emit(self):
        nc = self.nc
        block = self.es.enter_context(nc.Block())
        for e in ENGINES:
            for rec in self.ops[e]:
                if rec["dma"]:
                    rec["inc"] = True
            for rec in reversed(self.ops[e]):
                if not rec["bar"]:
                    rec["inc"] = True
                break
            for rec in reversed(self.ops[e]):
                if rec["seg"] == "body" and not rec["bar"]:
                    rec["inc"] = True
                    break

        for e in ENGINES:
            cum = 0
            for rec in self.ops[e]:
                if not rec["dma"] and (rec["inc"] or rec["bar"]):
                    cum += 1
                rec["cum"] = cum

        def key_of(rec):
            if rec["dma"]:
                return ("d", rec["eng"], rec["ord"] % self.ndsem[rec["eng"]])
            return ("c", rec["eng"])

        body_prod = {}
        for e in ENGINES:
            for rec in self.ops[e]:
                if rec["seg"] == "body" and (rec["inc"] or rec["bar"]):
                    k = key_of(rec)
                    body_prod[k] = body_prod.get(k, 0) + (16 if rec["dma"] else 1)

        def dep_val(rec, d):
            if d["dma"]:
                pn = self.ndsem[d["eng"]]
                v = 16 * (d["ord"] // pn + 1)
            else:
                v = d["cum"]
            if rec["seg"] == "post" and d["seg"] in ("body", "post"):
                v += (self.loop_n - 1) * body_prod.get(key_of(d), 0)
            return v

        def emit_stream(engobj, eng):
            nds = self.ndsem[eng]
            keys = set()
            for rec in self.ops[eng]:
                for d in rec["deps"]:
                    keys.add(key_of(d))
                if rec["dma"]:
                    keys.add(("d", eng, rec["ord"] % nds))
            keys = sorted(keys)
            regs = {}
            reg_guards = []
            for k in keys:
                g = engobj.register(f"r_{eng}_{'_'.join(map(str, k))}")
                reg_guards.append(g)
                regs[k] = g.__enter__()
                engobj.reg_mov(regs[k], 0)
            sguard = engobj.register(f"roff_{eng}")
            reg_guards.append(sguard)
            self_off = sguard.__enter__()
            self.loop_var[eng] = None
            self.off_reg = getattr(self, "off_reg", {})
            self.off_reg[eng] = self_off

            guaranteed = {k: 0 for k in keys}

            def emit_ops(op_list):
                for rec in op_list:
                    needs = {}
                    for d in rec["deps"]:
                        k = key_of(d)
                        v = dep_val(rec, d)
                        if v > needs.get(k, 0):
                            needs[k] = v
                    if rec["dma"]:
                        slot = rec["ord"] % nds
                        prev = rec["ord"] // nds
                        if prev > 0:
                            k = ("d", eng, slot)
                            need = 16 * prev
                            if rec["seg"] == "post":
                                need += (self.loop_n - 1) * body_prod.get(k, 0)
                            if need > needs.get(k, 0):
                                needs[k] = need
                    for k, v in sorted(needs.items()):
                        delta = v - guaranteed[k]
                        if delta <= 0:
                            continue
                        sem = (self.sem_d[k[1]][k[2]] if k[0] == "d"
                               else self.sem_c[k[1]])
                        engobj.reg_add(regs[k], regs[k], delta)
                        engobj.wait_ge(sem, regs[k])
                        guaranteed[k] = v
                    instr = rec["fn"](engobj)
                    if rec["inc"] and not rec["bar"]:
                        if rec["dma"]:
                            instr.then_inc(self.sem_d[eng][rec["ord"] % nds], 16)
                        else:
                            instr.then_inc(self.sem_c[eng], 1)

            pre = [r for r in self.ops[eng] if r["seg"] == "pre"]
            body = [r for r in self.ops[eng] if r["seg"] == "body"]
            post = [r for r in self.ops[eng] if r["seg"] == "post"]
            emit_ops(pre)
            if body:
                g0 = dict(guaranteed)
                with engobj.Fori(0, self.loop_n) as li:
                    self.loop_var[eng] = li
                    emit_ops(body)
                self.loop_var[eng] = None
                for k in keys:
                    guaranteed[k] = g0[k] + self.loop_n * (guaranteed[k] - g0[k])
            emit_ops(post)
            if eng == "sync":
                for slot in range(nds):
                    total = 0
                    for rec in self.ops[eng]:
                        if rec["dma"] and rec["ord"] % nds == slot:
                            total += 16 * (self.loop_n if rec["seg"] == "body" else 1)
                    k = ("d", eng, slot)
                    if total > 0 and k in regs and total > guaranteed.get(k, 0):
                        engobj.reg_add(regs[k], regs[k], total - guaranteed[k])
                        engobj.wait_ge(self.sem_d[eng][slot], regs[k])
                        guaranteed[k] = total
            for g in reg_guards:
                g.__exit__(None, None, None)

        for eng in ENGINES:
            if not self.ops[eng] and eng != "sync":
                continue
            getattr(block, eng)(lambda engobj, eng=eng: emit_stream(engobj, eng))


def r3(ap_tensor, f_count, f_step, c0, c1, p0=0, p1=128):
    return ap_tensor.ap().rearrange("p (f t) -> p f t", f=f_count)[p0:p1, :, c0:c1]


_name_counter = [0]


def uname(name):
    _name_counter[0] += 1
    return f"{name}_{_name_counter[0]}"


def dyn_dma(P, eng, out_ap, hbm_t, static_off, per_iter_off, ap2d,
            reads=(), writes=(), tag=""):
    def fn(e):
        try:
            lv = P.loop_var.get(eng)
            if lv is None or per_iter_off == 0:
                return e.dma_start(out=out_ap,
                                   in_=bass.AP(hbm_t, static_off, ap2d))
            off = P.off_reg[eng]
            e.reg_alu(off, lv, per_iter_off, ALU.mult)
            if static_off:
                e.reg_add(off, off, static_off)
            return e.dma_start(out=out_ap, in_=bass.AP(hbm_t, off, ap2d))
        except Exception:
            print("dyn_dma failed:", tag, hbm_t, static_off, per_iter_off, ap2d)
            raise
    return P.op(eng, fn, reads=reads, writes=writes, dma=True, tag=tag)


def check_prog(P):
    loop_n = P.loop_n

    def key_of(rec):
        if rec["dma"]:
            return ("d", rec["eng"], rec["ord"] % P.ndsem[rec["eng"]])
        return ("c", rec["eng"])

    for e in ENGINES:
        for rec in P.ops[e]:
            if rec["dma"]:
                rec["inc"] = True
        for rec in reversed(P.ops[e]):
            if not rec["bar"]:
                rec["inc"] = True
            break
        for rec in reversed(P.ops[e]):
            if rec["seg"] == "body" and not rec["bar"]:
                rec["inc"] = True
                break
    for e in ENGINES:
        cum = 0
        for rec in P.ops[e]:
            if not rec["dma"] and (rec["inc"] or rec["bar"]):
                cum += 1
            rec["cum"] = cum
    body_prod = {}
    for e in ENGINES:
        for rec in P.ops[e]:
            if rec["seg"] == "body" and (rec["inc"] or rec["bar"]):
                k = key_of(rec)
                body_prod[k] = body_prod.get(k, 0) + (16 if rec["dma"] else 1)

    def dep_val(rec, d):
        if d["dma"]:
            pn = P.ndsem[d["eng"]]
            v = 16 * (d["ord"] // pn + 1)
        else:
            v = d["cum"]
        if rec["seg"] == "post" and d["seg"] in ("body", "post"):
            v += (loop_n - 1) * body_prod.get(key_of(d), 0)
        return v

    streams = {}
    for eng in ENGINES:
        nds = P.ndsem[eng]
        keys = set()
        for rec in P.ops[eng]:
            for d in rec["deps"]:
                keys.add(key_of(d))
            if rec["dma"]:
                keys.add(("d", eng, rec["ord"] % nds))
        regs = {k: 0 for k in keys}
        guaranteed = {k: 0 for k in keys}
        ev = []

        def emit_ops(op_list):
            for rec in op_list:
                needs = {}
                for d in rec["deps"]:
                    k = key_of(d)
                    v = dep_val(rec, d)
                    if v > needs.get(k, 0):
                        needs[k] = v
                if rec["dma"]:
                    slot = rec["ord"] % nds
                    prev = rec["ord"] // nds
                    if prev > 0:
                        k = ("d", eng, slot)
                        need = 16 * prev
                        if rec["seg"] == "post":
                            need += (loop_n - 1) * body_prod.get(k, 0)
                        if need > needs.get(k, 0):
                            needs[k] = need
                for k, v in sorted(needs.items()):
                    delta = v - guaranteed[k]
                    if delta <= 0:
                        continue
                    ev.append(("radd", k, delta))
                    ev.append(("wait", k))
                    guaranteed[k] = v
                if rec["inc"] or rec["bar"]:
                    ev.append(("inc", key_of(rec), 16 if rec["dma"] else 1,
                               rec["tag"]))
                else:
                    ev.append(("noinc",))

        pre = [r for r in P.ops[eng] if r["seg"] == "pre"]
        body = [r for r in P.ops[eng] if r["seg"] == "body"]
        post = [r for r in P.ops[eng] if r["seg"] == "post"]
        emit_ops(pre)
        if body:
            g0 = dict(guaranteed)
            ev.append(("loopstart",))
            emit_ops(body)
            ev.append(("loopend",))
            for k in keys:
                guaranteed[k] = g0[k] + loop_n * (guaranteed[k] - g0[k])
        emit_ops(post)
        if eng == "sync":
            for slot in range(nds):
                total = 0
                for rec in P.ops[eng]:
                    if rec["dma"] and rec["ord"] % nds == slot:
                        total += 16 * (loop_n if rec["seg"] == "body" else 1)
                k = ("d", eng, slot)
                if total > 0 and k in regs and total > guaranteed.get(k, 0):
                    ev.append(("radd", k, total - guaranteed[k]))
                    ev.append(("wait", k))
                    guaranteed[k] = total
        flat = []
        if ("loopstart",) in ev:
            i0 = ev.index(("loopstart",))
            i1 = ev.index(("loopend",))
            flat = ev[:i0] + ev[i0 + 1:i1] * loop_n + ev[i1 + 1:]
        else:
            flat = ev
        streams[eng] = flat

    sems = {}
    regs = {e: {} for e in ENGINES}
    pc = {e: 0 for e in ENGINES}
    progress = True
    while progress:
        progress = False
        for e in ENGINES:
            while pc[e] < len(streams[e]):
                evn = streams[e][pc[e]]
                if evn[0] == "radd":
                    regs[e][evn[1]] = regs[e].get(evn[1], 0) + evn[2]
                elif evn[0] == "wait":
                    k = evn[1]
                    if sems.get(k, 0) < regs[e].get(k, 0):
                        break
                elif evn[0] == "inc":
                    sems[evn[1]] = sems.get(evn[1], 0) + evn[2]
                pc[e] += 1
                progress = True
    stuck = {e: (len(streams[e]) - pc[e]) for e in ENGINES}
    if any(v > 0 for v in stuck.values()):
        print("DEADLOCK; remaining events:", stuck)
        for e in ENGINES:
            if pc[e] < len(streams[e]):
                evn = streams[e][pc[e]]
                k = evn[1]
                print(f"  {e} stuck at ev#{pc[e]} waiting {k}: "
                      f"target={regs[e].get(k, 0)} sem={sems.get(k, 0)}")
        return False
    print("check_prog: OK, no deadlock")
    return True


"""Block emitters for the mk_sandwich_d2 megakernel. See mk_lib.Prog.

PSUM bank key convention: every physical bank has exactly one tracker key,
("psum", name). Never reuse a bank under two keys.
"""



def alloc(es, nc, name, shape, dtype):
    return es.enter_context(nc.sbuf_tensor(name, shape, dtype))


def psum1(es, nc, name, dtype=F32):
    n = 512 if dtype == F32 else 1024
    return es.enter_context(nc.psum_tensor(name, [128, n], dtype))


def rr(t, groups):
    return t.ap().rearrange("p (f t) -> p f t", f=groups)


class LNHelper:
    def __init__(self, P, es, nc, NB, tag="", xc_dtype=F32):
        self.P = P
        self.NB = NB
        self.xb = alloc(es, nc, f"ln_xb{tag}", [128, 8 * NB], BF16)
        self.xc = alloc(es, nc, f"ln_xc{tag}", [128, 8 * NB], xc_dtype)
        self.stat1 = alloc(es, nc, f"ln_st1{tag}", [1, NB], F32)
        self.stat2 = alloc(es, nc, f"ln_st2{tag}", [1, NB], F32)
        self.kxb = ("ln" + tag, "xb")
        self.kxc = ("ln" + tag, "xc")
        self.kst1 = ("ln" + tag, "st1")
        self.kst2 = ("ln" + tag, "st2")

    def run(self, C, pstat, kps, prep, kpr, src3=None, src_keys=None, inplace=False):
        P, NB = self.P, self.NB
        xb, xc = self.xb, self.xc
        if inplace:
            src3 = rr(xc, 8)
            src_keys = [self.kxc]
        P.op("vector", lambda v: v.tensor_copy(rr(xb, 8), src3),
             reads=list(src_keys), writes=[self.kxb], tag="ln.cast")
        for f in range(8):
            P.op("tensor", lambda t, f=f: t.matmul(
                pstat[0:1, 0:NB], C["onescol_b"][:], xb[:, f * NB:(f + 1) * NB],
                start=(f == 0), stop=(f == 7)),
                reads=[self.kxb, ("c", "onescol_b")], writes=[kps], tag="ln.sum")
        P.op("scalar", lambda s: s.copy(self.stat1[:], pstat[0:1, 0:NB]),
             reads=[kps], writes=[self.kst1], tag="ln.sumev")
        P.op("tensor", lambda t: t.matmul(
            prep[:, 0:NB], C["c1024"][:], self.stat1[:], start=True, stop=True),
            reads=[self.kst1, ("c", "c1024")], writes=[kpr], tag="ln.murep")
        P.op("vector", lambda v: v.tensor_tensor(
            rr(xc, 8), src3,
            prep[:, 0:NB].unsqueeze(1).broadcast_to([128, 8, NB]),
            op=ALU.subtract),
            reads=list(src_keys) + [kpr], writes=[self.kxc], tag="ln.xc")
        P.op("scalar", lambda s: s.activation(xb[:], xc[:], AF.Square),
             reads=[self.kxc], writes=[self.kxb], tag="ln.sq")
        for f in range(8):
            P.op("tensor", lambda t, f=f: t.matmul(
                pstat[0:1, 0:NB], C["onescol_b"][:], xb[:, f * NB:(f + 1) * NB],
                start=(f == 0), stop=(f == 7)),
                reads=[self.kxb, ("c", "onescol_b")], writes=[kps], tag="ln.sumsq")
        P.op("scalar", lambda s: s.activation(
            self.stat1[:], pstat[0:1, 0:NB], AF.Sqrt, bias=1e-5, scale=1.0 / 1024.0),
            reads=[kps, ("c", "fc")], writes=[self.kst1], tag="ln.std")
        P.op("vector", lambda v: v.reciprocal(self.stat2[:], self.stat1[:]),
             reads=[self.kst1], writes=[self.kst2], tag="ln.rstd")
        P.op("tensor", lambda t: t.matmul(
            prep[:, 0:NB], C["c1"][:], self.stat2[:], start=True, stop=True),
            reads=[self.kst2, ("c", "c1")], writes=[kpr], tag="ln.rrep")
        return prep[:, 0:NB].unsqueeze(1).broadcast_to([128, 8, NB])

    def run_inplace(self, C, pstat, kps, prep, kpr):
        return self.run(C, pstat, kps, prep, kpr, rr(self.xc, 8), [self.kxc],
                        inplace=True)

    def scale_inplace(self, rrep, kpr):
        self.P.op("vector", lambda v: v.tensor_tensor(
            rr(self.xc, 8), rr(self.xc, 8), rrep, op=ALU.mult),
            reads=[self.kxc, kpr], writes=[self.kxc], tag="ln.sclip")

    def to_bf16(self, out3, out_keys, rrep, kpr):
        self.P.op("vector", lambda v: v.tensor_tensor(
            out3, rr(self.xc, 8), rrep, op=ALU.mult),
            reads=[self.kxc, kpr], writes=list(out_keys), tag="ln.out16")

    def add_into(self, dst3, dst_keys, rrep, kpr):
        P = self.P
        P.op("vector", lambda v: v.tensor_tensor(
            rr(self.xc, 8), rr(self.xc, 8), rrep, op=ALU.mult),
            reads=[self.kxc, kpr], writes=[self.kxc], tag="ln.scale")
        P.op("vector", lambda v: v.tensor_tensor(
            dst3, dst3, rr(self.xc, 8), op=ALU.add),
            reads=list(dst_keys) + [self.kxc], writes=list(dst_keys), tag="ln.resid")

    def mul_to(self, out3, out_keys, rrep, kpr):
        self.P.op("vector", lambda v: v.tensor_tensor(
            out3, rr(self.xc, 8), rrep, op=ALU.mult),
            reads=[self.kxc, kpr], writes=list(out_keys), tag="ln.outf")


def load_consts(P, es, nc, specs):
    C = {}
    for name, shape in specs:
        hbm = nc.dram_tensor(name, list(shape), F32, kind="ExternalInput").ap()
        sb = alloc(es, nc, "c_" + name, list(shape), F32)
        P.dma("sync", sb[:], hbm, writes=[("c", name)], tag=f"load.{name}")
        C[name] = sb
    return C


def load_bf16_via_stage(P, es, nc, C, stage, name, shape):
    hbm = nc.dram_tensor(name, list(shape), F32, kind="ExternalInput").ap()
    P.dma("sync", stage[0:shape[0], 0:shape[1]], hbm,
          writes=[("stage",)], tag=f"load.{name}")
    sb = alloc(es, nc, "cb_" + name, list(shape), BF16)
    P.op("vector", lambda v: v.tensor_copy(sb[:], stage[0:shape[0], 0:shape[1]]),
         reads=[("stage",)], writes=[("c", name + "_b")], tag=f"cast.{name}")
    C[name + "_b"] = sb
    return sb


def cast_c(P, es, nc, C, name, shape):
    sb = alloc(es, nc, "cb_" + name, list(shape), BF16)
    P.op("vector", lambda v: v.tensor_copy(sb[:], C[name][:]),
         reads=[("c", name)], writes=[("c", name + "_b")], tag=f"cast.{name}")
    C[name + "_b"] = sb
    return sb


COMMON_CONSTS = [
    ("idf", (128, 128)),
    ("c1024", (1, 128)), ("c1", (1, 128)),
    ("onescol", (128, 1)), ("fc", (1, 8)),
]

FC_VALUES = [1e-5, 1e-6, 1.0, 0.0, 0.0, 0.0, 0.0, 0.0]


def register_fc(nc, C):
    for i, v in enumerate(FC_VALUES):
        if v not in (0.0, 1.0):
            nc.const_aps.register(F32, v, C["fc"][0:1, i:i + 1])


def common_const_arrays():
    return {
        "idf": np.eye(128, dtype=np.float32),
        "c1024": np.full((1, 128), 1.0 / 1024.0, dtype=np.float32),
        "c1": np.ones((1, 128), dtype=np.float32),
        "onescol": np.ones((128, 1), dtype=np.float32),
        "fc": np.array([FC_VALUES], dtype=np.float32),
    }


def emit_entry_transpose(P, C, x_h, s, xt, T, xtm, pb, NB):
    for tb in range(T // 128):
        buf = xtm[tb % 2]
        kb = ("stage",) if xtm[0] is xtm[1] else ("xtm", tb % 2)
        P.dma("sync", buf[:], x_h[s, tb * 128:(tb + 1) * 128, :],
              writes=[kb], tag="x.load")
        for f in range(8):
            P.op("tensor", lambda t, f=f, buf=buf: t.transpose(
                pb[:, f * 128:(f + 1) * 128],
                buf[:, f * 128:(f + 1) * 128], C["idf"][:]),
                reads=[kb, ("c", "idf")],
                writes=[("psum", "pb0" if f < 4 else "pb1")], tag="x.tp")
        P.op("vector", lambda v, tb=tb: v.tensor_copy(
            r3(xt, 8, T, tb * 128, (tb + 1) * 128), rr(pb, 8)),
            reads=[("psum", "pb0"), ("psum", "pb1")],
            writes=[("xt", tb * 128 // NB)], tag="x.ev")


def emit_exit_transpose(P, C, xt, out_h, s, T, pb, idb_key=("c", "idf_b")):
    otm = P._otm
    for tb in range(T // 128):
        kb = ("otm",)
        for f in range(8):
            P.op("tensor", lambda t, f=f, tb=tb: t.transpose(
                pb[:, f * 128:(f + 1) * 128],
                xt.ap()[:, f * T + tb * 128: f * T + (tb + 1) * 128],
                C["idf"][:]),
                reads=[("xt", tb * 128 // 512), ("c", "idf")],
                writes=[("psum", "pb0" if f < 4 else "pb1")], tag="xo.tp")
        P.op("vector", lambda v: v.tensor_copy(otm[:], pb[:, 0:1024]),
             reads=[("psum", "pb0"), ("psum", "pb1")], writes=[kb], tag="xo.ev")
        P.dma("sync", out_h[s, tb * 128:(tb + 1) * 128, :], otm[:],
              reads=[kb], tag="xo.st")


def build_l0attn(nc, T=4096, SEQS=2):
    es = contextlib.ExitStack()
    NB = 512
    nblk = T // NB
    CH = 128
    nch = NB // CH
    with es:
        P = Prog(nc, es)
        x_h = nc.dram_tensor("x", [SEQS, T, 1024], F32, kind="ExternalInput").ap()
        x1_h = nc.dram_tensor("x1", [SEQS, 128, 8 * T], F32, kind="ExternalOutput").ap()
        z0_h = nc.dram_tensor("z0", [SEQS, 128, 8 * T], F32, kind="ExternalOutput").ap()

        C = load_consts(P, es, nc, COMMON_CONSTS + [
            ("dlog", (1, 1)), ("oscale", (1, 1)),
            ("ET", (128, 128)), ("M01T", (128, 128)),
            ("powq2", (1, 256)), ("powk2", (1, 256)),
        ])
        register_fc(nc, C)
        stage = alloc(es, nc, "stage", [128, 1024], F32)
        for nm, sh in [("basisT", (128, 1024)), ("qcT", (128, 256)),
                       ("kcT", (128, 256)), ("vcT", (128, 256)),
                       ("ocT", (128, 256))]:
            load_bf16_via_stage(P, es, nc, C, stage, nm, sh)
        for nm, sh in [("onescol", (128, 1)), ("idf", (128, 128))]:
            cast_c(P, es, nc, C, nm, sh)

        xt = alloc(es, nc, "xt", [128, 8 * T], F32)
        xtm = [stage, stage]
        qw = alloc(es, nc, "qw", [128, 8 * 256], BF16)
        kw = alloc(es, nc, "kw", [128, 8 * 256], BF16)
        vw = alloc(es, nc, "vw", [128, 8 * 256], BF16)
        owT = alloc(es, nc, "owT", [128, 2 * 1024], BF16)
        DTt = alloc(es, nc, "DTt", [128, 128], F32)
        Etmp = alloc(es, nc, "Etmp", [128, 128], F32)
        lamq = alloc(es, nc, "lamq", [128, 256], BF16)
        lamk = alloc(es, nc, "lamk", [128, 256], BF16)
        lamc = alloc(es, nc, "lamc", [128, 1], F32)
        osbc = alloc(es, nc, "osbc", [128, 1], F32)
        srow = alloc(es, nc, "srow", [1, 256], F32)
        srow2 = alloc(es, nc, "srow2", [1, 256], F32)
        lam_t = alloc(es, nc, "lam_t", [1, 4], F32)
        lnl_bc = alloc(es, nc, "lnl_bc", [128, 4], F32)
        qT = alloc(es, nc, "qT", [128, 2 * NB], BF16)
        kT = alloc(es, nc, "kT", [128, 2 * NB], BF16)
        vT = alloc(es, nc, "vT", [128, 2 * NB], BF16)
        qs = alloc(es, nc, "qs", [128, 256], BF16)
        ks = alloc(es, nc, "ks", [128, 256], BF16)
        ab = alloc(es, nc, "ab", [128, 128], BF16)
        vtm = alloc(es, nc, "vtm", [128, 256], BF16)
        ktm = alloc(es, nc, "ktm", [128, 256], BF16)
        state = alloc(es, nc, "state", [128, 512], F32)
        stateb = alloc(es, nc, "stateb", [128, 512], BF16)
        oT = alloc(es, nc, "oT", [128, 2 * NB], BF16)
        ln1 = LNHelper(P, es, nc, NB, tag="1")

        pstat = psum1(es, nc, "pstat")
        prep = psum1(es, nc, "prep")
        pa0 = psum1(es, nc, "pa0")
        pa1 = psum1(es, nc, "pa1")
        pbt = es.enter_context(nc.psum_tensor("pbt", [128, 1024], F32))
        pc = psum1(es, nc, "pc")
        pdb = es.enter_context(nc.psum_tensor("pdb", [128, 1024], BF16))
        kps, kpr = ("psum", "pstat"), ("psum", "prep")

        for nm, dst, kk in [("qcT_b", qw, "q"), ("kcT_b", kw, "k"), ("vcT_b", vw, "v")]:
            for dk in range(8):
                P.op("tensor", lambda t, dk=dk, nm=nm: t.matmul(
                    pa0[:, 0:256], C["basisT_b"][:, dk * 128:(dk + 1) * 128],
                    C[nm][:], start=True, stop=True),
                    reads=[("c", "basisT_b"), ("c", nm)],
                    writes=[("psum", "pa0")], tag="w.mm")
                P.op("scalar", lambda s, dk=dk, dst=dst: s.copy(
                    dst[:, dk * 256:(dk + 1) * 256], pa0[:, 0:256]),
                    reads=[("psum", "pa0")], writes=[("w", kk)], tag="w.evac")
        for ck in range(2):
            for half in range(2):
                P.op("tensor", lambda t, ck=ck, half=half: t.matmul(
                    pa1[:, 0:512], C["ocT_b"][:, ck * 128:(ck + 1) * 128],
                    C["basisT_b"][:, half * 512:(half + 1) * 512],
                    start=True, stop=True),
                    reads=[("c", "basisT_b"), ("c", "ocT_b")],
                    writes=[("psum", "pa1")], tag="w.omm")
                P.op("scalar", lambda s, ck=ck, half=half: s.copy(
                    owT[:, ck * 1024 + half * 512: ck * 1024 + (half + 1) * 512],
                    pa1[:, 0:512]),
                    reads=[("psum", "pa1")], writes=[("w", "o")], tag="w.oevac")

        P.op("vector", lambda v: v.memset(lam_t[:], 0.0),
             writes=[("t", "lam")], tag="d.z")
        P.op("scalar", lambda s: s.activation(lam_t[:, 0:1], C["dlog"][:], AF.Sigmoid),
             reads=[("c", "dlog"), ("t", "lam")], writes=[("t", "lam")], tag="d.sig")
        P.op("scalar", lambda s: s.activation(lam_t[:, 1:2], lam_t[:, 0:1], AF.Ln),
             reads=[("t", "lam")], writes=[("t", "lam")], tag="d.ln")
        P.op("scalar", lambda s: s.activation(lam_t[:, 2:3], lam_t[:, 1:2], AF.Exp,
                                              scale=float(CH)),
             reads=[("t", "lam")], writes=[("t", "lam")], tag="d.lamc")
        P.op("tensor", lambda t: t.matmul(prep[:, 0:4], C["c1"][:], lam_t[:],
                                          start=True, stop=True),
             reads=[("t", "lam"), ("c", "c1")], writes=[kpr], tag="d.rep")
        P.op("scalar", lambda s: s.copy(lnl_bc[:], prep[:, 0:4]),
             reads=[kpr], writes=[("t", "lnl_bc")], tag="d.repev")
        P.op("vector", lambda v: v.tensor_copy(lamc[:], lnl_bc[:, 2:3]),
             reads=[("t", "lnl_bc")], writes=[("t", "lamcbc")], tag="d.lamcbc")
        P.op("tensor", lambda t: t.matmul(prep[:, 0:1], C["c1"][:], C["oscale"][:],
                                          start=True, stop=True),
             reads=[("c", "oscale"), ("c", "c1")], writes=[kpr], tag="d.osrep")
        P.op("scalar", lambda s: s.copy(osbc[:], prep[:, 0:1]),
             reads=[kpr], writes=[("t", "osbc")], tag="d.osev")
        P.op("vector", lambda v: v.tensor_scalar(
            Etmp[:], C["ET"][:], lnl_bc[:, 1:2], None, op0=ALU.mult),
            reads=[("c", "ET"), ("t", "lnl_bc")], writes=[("t", "Etmp")], tag="d.emul")
        P.op("scalar", lambda s: s.activation(Etmp[:], Etmp[:], AF.Exp),
             reads=[("t", "Etmp")], writes=[("t", "Etmp")], tag="d.eexp")
        P.op("vector", lambda v: v.tensor_tensor(DTt[:], Etmp[:], C["M01T"][:],
                                                 op=ALU.mult),
             reads=[("t", "Etmp"), ("c", "M01T")], writes=[("t", "DT")], tag="d.dmask")
        for dstt, powname, kk in [(lamq, "powq2", "lamq"), (lamk, "powk2", "lamk")]:
            P.op("vector", lambda v, powname=powname: v.tensor_scalar(
                srow[:], C[powname][:], lnl_bc[0:1, 1:2], None, op0=ALU.mult),
                reads=[("c", powname), ("t", "lnl_bc")], writes=[("t", "srow")],
                tag="d.pmul")
            P.op("scalar", lambda s: s.activation(srow2[:], srow[:], AF.Exp),
                 reads=[("t", "srow")], writes=[("t", "srow2")], tag="d.pexp")
            P.op("tensor", lambda t: t.matmul(prep[:, 0:256], C["c1"][:], srow2[:],
                                              start=True, stop=True),
                 reads=[("t", "srow2"), ("c", "c1")], writes=[kpr], tag="d.prep")
            P.op("scalar", lambda s, dstt=dstt, kk=kk: s.copy(dstt[:], prep[:, 0:256]),
                 reads=[kpr], writes=[("t", kk)], tag="d.pev")

        for s in range(SEQS):
            emit_entry_transpose(P, C, x_h, s, xt, T, xtm, pbt, NB)
            P.op("vector", lambda v: v.memset(state[:], 0.0),
                 writes=[("t", "state")], tag="st.zero")
            P.op("vector", lambda v: v.memset(stateb[:], 0.0),
                 writes=[("t", "stateb")], tag="st.zero16")

            for blk in reversed(range(nblk)):
                c0, c1_ = blk * NB, (blk + 1) * NB
                kxt = ("xt", blk)
                rrep = ln1.run(C, pstat, kps, prep, kpr, r3(xt, 8, T, c0, c1_), [kxt])
                lnb = ln1.xb
                ln1.to_bf16(rr(lnb, 8), [ln1.kxb], rrep, kpr)
                for wname, wt, dstt, kk in [("q", qw, qT, "qT"), ("k", kw, kT, "kT"),
                                            ("v", vw, vT, "vT")]:
                    for cm in range(2):
                        pban = pa0 if cm == 0 else pa1
                        for dk in range(8):
                            P.op("tensor", lambda t, wt=wt, cm=cm, dk=dk, pban=pban: t.matmul(
                                pban[:, 0:NB],
                                wt[:, dk * 256 + cm * 128: dk * 256 + cm * 128 + 128],
                                lnb[:, dk * NB:(dk + 1) * NB],
                                start=(dk == 0), stop=(dk == 7)),
                                reads=[("w", wname), ln1.kxb],
                                writes=[("psum", "pa" + str(cm))], tag="p.mm")
                        P.op("scalar", lambda sc, dstt=dstt, cm=cm, pban=pban: sc.copy(
                            dstt[:, cm * NB:(cm + 1) * NB], pban[:, 0:NB]),
                            reads=[("psum", "pa" + str(cm))],
                            writes=[("t", kk)], tag="p.evac")
                for ch in reversed(range(nch)):
                    t0 = ch * CH
                    P.op("vector", lambda v, t0=t0: v.tensor_tensor(
                        rr(qs, 2), rr(qT, 2)[:, :, t0:t0 + CH], rr(lamq, 2),
                        op=ALU.mult),
                        reads=[("t", "qT"), ("t", "lamq")], writes=[("t", "qs")],
                        tag="r.qs")
                    P.op("vector", lambda v, t0=t0: v.tensor_tensor(
                        rr(ks, 2), rr(kT, 2)[:, :, t0:t0 + CH], rr(lamk, 2),
                        op=ALU.mult),
                        reads=[("t", "kT"), ("t", "lamk")], writes=[("t", "ks")],
                        tag="r.ks")
                    for ck in range(2):
                        P.op("tensor", lambda t, ck=ck, t0=t0: t.matmul(
                            pc[:, 0:CH],
                            kT[:, ck * NB + t0: ck * NB + t0 + CH],
                            qT[:, ck * NB + t0: ck * NB + t0 + CH],
                            start=(ck == 0), stop=(ck == 1)),
                            reads=[("t", "kT"), ("t", "qT")],
                            writes=[("psum", "pc")], tag="r.amm")
                    P.op("vector", lambda v: v.tensor_tensor(
                        ab[:], pc[:, 0:CH], DTt[:], op=ALU.mult),
                        reads=[("psum", "pc"), ("t", "DT")], writes=[("t", "ab")],
                        tag="r.amask")
                    for ck in range(2):
                        P.op("tensor", lambda t, ck=ck, t0=t0: t.transpose(
                            pdb[:, ck * 128:(ck + 1) * 128],
                            vT[:, ck * NB + t0: ck * NB + t0 + CH],
                            C["idf_b"][:]),
                            reads=[("t", "vT"), ("c", "idf_b")],
                            writes=[("psum", "pdb")], tag="r.vtp")
                    P.op("vector", lambda v: v.tensor_copy(vtm[:], pdb[:, 0:256]),
                         reads=[("psum", "pdb")], writes=[("t", "vtm")], tag="r.vtev")
                    for ck in range(2):
                        P.op("tensor", lambda t, ck=ck: t.transpose(
                            pdb[:, 256 + ck * 128: 256 + (ck + 1) * 128],
                            ks[:, ck * 128:(ck + 1) * 128],
                            C["idf_b"][:]),
                            reads=[("t", "ks"), ("c", "idf_b")],
                            writes=[("psum", "pdb")], tag="r.ktp")
                    P.op("vector", lambda v: v.tensor_copy(ktm[:], pdb[:, 256:512]),
                         reads=[("psum", "pdb")], writes=[("t", "ktm")], tag="r.ktev")
                    for m in range(2):
                        pban = pa0 if m == 0 else pa1
                        P.op("tensor", lambda t, m=m, pban=pban: t.matmul(
                            pban[:, 0:CH],
                            vtm[:, m * 128:(m + 1) * 128], ab[:],
                            start=True, stop=False),
                            reads=[("t", "vtm"), ("t", "ab")],
                            writes=[("psum", "pa" + str(m))], tag="r.omm")
                        for ck in range(2):
                            P.op("tensor", lambda t, m=m, ck=ck, pban=pban: t.matmul(
                                pban[:, 0:CH],
                                stateb[:, ck * 256 + m * 128: ck * 256 + (m + 1) * 128],
                                qs[:, ck * 128:(ck + 1) * 128],
                                start=False, stop=(ck == 1)),
                                reads=[("t", "stateb"), ("t", "qs")],
                                writes=[("psum", "pa" + str(m))], tag="r.cmm")
                    for m in range(2):
                        pban = pa0 if m == 0 else pa1
                        P.op("scalar", lambda sc, t0=t0, m=m, pban=pban: sc.copy(
                            oT[:, m * NB + t0: m * NB + t0 + CH], pban[:, 0:CH]),
                            reads=[("psum", "pa" + str(m))],
                            writes=[("t", "oT")], tag="r.oev")
                    for ck in range(2):
                        P.op("tensor", lambda t, ck=ck: t.matmul(
                            pc[:, ck * 256:(ck + 1) * 256],
                            ktm[:, ck * 128:(ck + 1) * 128], vtm[:],
                            start=True, stop=True),
                            reads=[("t", "ktm"), ("t", "vtm")],
                            writes=[("psum", "pc")], tag="r.smm")
                    P.op("vector", lambda v: v.scalar_tensor_tensor(
                        state[:], state[:], lamc[:], pc[:],
                        op0=ALU.mult, op1=ALU.add),
                        reads=[("t", "state"), ("t", "lamcbc"), ("psum", "pc")],
                        writes=[("t", "state")], tag="r.supd")
                    P.op("vector", lambda v: v.tensor_copy(stateb[:], state[:]),
                         reads=[("t", "state")], writes=[("t", "stateb")], tag="r.s16")
                for dm in range(8):
                    pban = pa0 if dm % 2 == 0 else pa1
                    for ck in range(2):
                        P.op("tensor", lambda t, dm=dm, ck=ck, pban=pban: t.matmul(
                            pban[:, 0:NB],
                            owT[:, ck * 1024 + dm * 128: ck * 1024 + (dm + 1) * 128],
                            oT[:, ck * NB:(ck + 1) * NB],
                            start=(ck == 0), stop=(ck == 1)),
                            reads=[("w", "o"), ("t", "oT")],
                            writes=[("psum", "pa" + str(dm % 2))], tag="o.mm")
                    P.op("vector", lambda v, dm=dm, pban=pban: v.tensor_scalar(
                        ln1.xc[:, dm * NB:(dm + 1) * NB], pban[:, 0:NB],
                        osbc[:], None, op0=ALU.mult),
                        reads=[("psum", "pa" + str(dm % 2)), ("t", "osbc")],
                        writes=[ln1.kxc], tag="o.ev")
                ryrep = ln1.run_inplace(C, pstat, kps, prep, kpr)
                ln1.add_into(r3(xt, 8, T, c0, c1_), [kxt], ryrep, kpr)
                P.dma("sync", x1_h[s].rearrange("p (f t) -> p f t", f=8)[:, :, c0:c1_],
                      r3(xt, 8, T, c0, c1_), reads=[kxt], tag="x1.st")
                rzrep = ln1.run(C, pstat, kps, prep, kpr, r3(xt, 8, T, c0, c1_), [kxt])
                ln1.scale_inplace(rzrep, kpr)
                P.dma("sync", z0_h[s].rearrange("p (f t) -> p f t", f=8)[:, :, c0:c1_],
                      rr(ln1.xc, 8), reads=[ln1.kxc], tag="z0.st")
        P.emit()
    return nc


def l0attn_inputs(x_shard, weights, T=4096):
    CH = 128
    i_idx = np.arange(CH, dtype=np.float32)
    ET = np.maximum(i_idx[:, None] - 1 - i_idx[None, :], 0).astype(np.float32)
    M01T = (i_idx[:, None] > i_idx[None, :]).astype(np.float32)
    powq2 = np.concatenate([CH - 1 - i_idx, CH - 1 - i_idx])[None, :]
    powk2 = np.concatenate([i_idx, i_idx])[None, :]
    d = {
        "x": np.ascontiguousarray(x_shard, dtype=np.float32),
        "basisT": np.ascontiguousarray(np.asarray(weights["L0_attn_basis"]).T),
        "qcT": np.ascontiguousarray(np.asarray(weights["L0_attn_q_coeffs"]).T),
        "kcT": np.ascontiguousarray(np.asarray(weights["L0_attn_k_coeffs"]).T),
        "vcT": np.ascontiguousarray(np.asarray(weights["L0_attn_v_coeffs"]).T),
        "ocT": np.ascontiguousarray(np.asarray(weights["L0_attn_o_coeffs"]).T),
        "dlog": np.array(weights["L0_attn_decay_logit"], dtype=np.float32).reshape(1, 1),
        "oscale": np.array(weights["L0_attn_out_scale"], dtype=np.float32).reshape(1, 1),
        "ET": ET, "M01T": M01T,
        "powq2": powq2.astype(np.float32), "powk2": powk2.astype(np.float32),
    }
    d.update(common_const_arrays())
    return d


def tile_w(W):
    Din, Dout = W.shape
    nk, nm = Din // 128, Dout // 128
    return np.ascontiguousarray(
        W.reshape(nk, 128, nm, 128).transpose(1, 0, 2, 3).reshape(128, nk * nm * 128))


def build_l1attn(nc, T=4096, SEQS=2, debug=False):
    es = contextlib.ExitStack()
    NB = 512
    nblk = T // NB
    CH = 128
    nch = NB // CH
    NH = 16
    with es:
        P = Prog(nc, es)
        x2_h = nc.dram_tensor("x2", [SEQS, 128, 8 * T], F32, kind="ExternalInput").ap()
        x3_h = nc.dram_tensor("x3", [SEQS, 128, 8 * T], F32, kind="ExternalOutput").ap()
        if debug:
            dbg_h = nc.dram_tensor("dbg", [128, 768], F32, kind="ExternalOutput").ap()
            dbg_sb = alloc(es, nc, "dbg_sb", [128, 768], F32)
            P.op("vector", lambda v: v.memset(dbg_sb[:], 0.0), writes=[("t", "dbg")], tag="dbg.z")
        w_h = {}
        for nm_ in ["wq", "wk", "wv", "wo"]:
            w_h[nm_] = nc.dram_tensor(nm_, [128, 8192], F32, kind="ExternalInput").ap()
        C = load_consts(P, es, nc, COMMON_CONSTS + [
            ("bq", (128, 8)), ("bk", (128, 8)), ("bv", (128, 8)), ("bo", (128, 8)),
            ("MINC", (128, 128)),
        ])
        cosr_h = nc.dram_tensor("cosr", [1, T], F32, kind="ExternalInput").ap()
        sinr_h = nc.dram_tensor("sinr", [1, T], F32, kind="ExternalInput").ap()
        register_fc(nc, C)
        cast_c(P, es, nc, C, "onescol", (128, 1))
        cast_c(P, es, nc, C, "idf", (128, 128))

        stage = alloc(es, nc, "stage", [128, 2048], F32)
        wsb = {nm_: alloc(es, nc, nm_ + "b", [128, 8192], BF16)
               for nm_ in ["wq", "wk", "wv", "wo"]}
        for nm_ in ["wq", "wk", "wv", "wo"]:
            for r in range(4):
                P.dma("sync", stage[:], w_h[nm_][:, r * 2048:(r + 1) * 2048],
                      writes=[("stage",)], tag="w.load")
                P.op("vector", lambda v, nm_=nm_, r=r: v.tensor_copy(
                    wsb[nm_][:, r * 2048:(r + 1) * 2048], stage[:]),
                    reads=[("stage",)], writes=[("w", nm_)], tag="w.cast")
        cosw = alloc(es, nc, "cosw", [128, T], BF16)
        sinw = alloc(es, nc, "sinw", [128, T], BF16)
        prep = psum1(es, nc, "prep")
        kpr = ("psum", "prep")
        for tab, rowh, kk in [(cosw, cosr_h, "cosw"), (sinw, sinr_h, "sinw")]:
            for b in range(T // 512):
                P.dma("sync", stage[0:1, 0:512], rowh[0:1, b * 512:(b + 1) * 512],
                      writes=[("stage",)], tag="tab.ld")
                P.op("tensor", lambda t, b=b: t.matmul(
                    prep[:, 0:512], C["c1"][:], stage[0:1, 0:512],
                    start=True, stop=True),
                    reads=[("stage",), ("c", "c1")], writes=[kpr], tag="tab.rep")
                P.op("scalar", lambda s, tab=tab, b=b, kk=kk: s.copy(
                    tab[:, b * 512:(b + 1) * 512], prep[:, 0:512]),
                    reads=[kpr], writes=[("t", kk)], tag="tab.ev")

        x2b = alloc(es, nc, "x2b", [128, 8 * NB], F32)
        qp = alloc(es, nc, "qp", [128, 8 * NB], BF16)
        kp = alloc(es, nc, "kp", [128, 8 * NB], BF16)
        vv = alloc(es, nc, "vv", [128, 8 * NB], BF16)
        qc = alloc(es, nc, "qc", [64, 16 * CH], BF16)
        qs_ = alloc(es, nc, "qs", [64, 16 * CH], BF16)
        kc = alloc(es, nc, "kc", [64, 16 * CH], BF16)
        ks_ = alloc(es, nc, "ks", [64, 16 * CH], BF16)
        vtma = alloc(es, nc, "vtma", [128, NH * 65], BF16)
        kctm = alloc(es, nc, "kctm", [128, 1024], BF16)
        kstm = alloc(es, nc, "kstm", [128, 1024], BF16)
        ab = alloc(es, nc, "ab", [128, CH], BF16)
        state = alloc(es, nc, "state", [64, NH * 130], F32)
        stateb = alloc(es, nc, "stateb", [64, NH * 130], BF16)
        dstripe = alloc(es, nc, "dstripe", [1, CH], F32)
        denr = alloc(es, nc, "denr", [1, CH], F32)
        rrsb = alloc(es, nc, "rrsb", [64, CH], F32)
        outT = alloc(es, nc, "outT", [128, 8 * NB], BF16)
        ln1 = LNHelper(P, es, nc, NB, tag="1")

        pstat = psum1(es, nc, "pstat")
        pa0 = psum1(es, nc, "pa0")
        pa1 = psum1(es, nc, "pa1")
        pA = psum1(es, nc, "pA")
        pS = psum1(es, nc, "pS")
        pdb = es.enter_context(nc.psum_tensor("pdb", [128, 1024], BF16))
        kps = ("psum", "pstat")

        P.op("vector", lambda v: v.memset(vtma[:], 1.0), writes=[("t", "vtma")],
             tag="va.ones")

        for s in range(SEQS):
            P.op("vector", lambda v: v.memset(state[:], 0.0),
                 writes=[("t", "state")], tag="st.z")
            P.op("vector", lambda v: v.memset(stateb[:], 0.0),
                 writes=[("t", "stateb")], tag="st.z16")
            for blk in range(nblk):
                c0, c1_ = blk * NB, (blk + 1) * NB
                P.dma("sync", rr(x2b, 8),
                      x2_h[s].rearrange("p (f t) -> p f t", f=8)[:, :, c0:c1_],
                      writes=[("t", "x2b")], tag="x2.ld")
                rrep = ln1.run(C, pstat, kps, prep, kpr, rr(x2b, 8), [("t", "x2b")])
                lnb = ln1.xb
                ln1.to_bf16(rr(lnb, 8), [ln1.kxb], rrep, kpr)
                for wname, bias, dstt, kk, act in [
                        ("wq", "bq", qp, "qp", AF.Relu), ("wk", "bk", kp, "kp", AF.Relu),
                        ("wv", "bv", vv, "vv", AF.Identity)]:
                    for m in range(8):
                        pban, kpb = (pa0, ("psum", "pa0")) if m % 2 == 0 else (pa1, ("psum", "pa1"))
                        for k in range(8):
                            P.op("tensor", lambda t, wname=wname, m=m, k=k, pban=pban: t.matmul(
                                pban[:, 0:NB],
                                wsb[wname][:, (k * 8 + m) * 128:(k * 8 + m + 1) * 128],
                                lnb[:, k * NB:(k + 1) * NB],
                                start=(k == 0), stop=(k == 7)),
                                reads=[("w", wname), ln1.kxb],
                                writes=[kpb], tag="p.mm")
                        P.op("scalar", lambda sc, m=m, dstt=dstt, bias=bias, act=act, kk=kk, pban=pban: sc.activation(
                            dstt[:, m * NB:(m + 1) * NB], pban[:, 0:NB], act,
                            bias=C[bias][:, m:m + 1]),
                            reads=[kpb, ("psum", "pa" + str(m % 2)), ("c", bias)],
                            writes=[("t", kk)], tag="p.ev")
                for ch in range(nch):
                    t0 = ch * CH
                    g0 = c0 + t0
                    for srctab, tabk, dstt, kk, src, ksrc in [
                            (cosw, "cosw", qc, "qc", qp, "qp"), (sinw, "sinw", qs_, "qs", qp, "qp"),
                            (cosw, "cosw", kc, "kc", kp, "kp"), (sinw, "sinw", ks_, "ks", kp, "kp")]:
                        for par in range(2):
                            P.op("vector", lambda v, srctab=srctab, dstt=dstt, src=src, t0=t0, g0=g0, par=par: v.tensor_tensor(
                                dstt.ap().rearrange("p (f q) -> p f q", f=8)[0:64, :, par * CH:(par + 1) * CH],
                                rr(src, 8)[par * 64:par * 64 + 64, :, t0:t0 + CH],
                                srctab[par * 64:par * 64 + 64, g0:g0 + CH].unsqueeze(1).broadcast_to([64, 8, CH]),
                                op=ALU.mult),
                                reads=[("t", ksrc), ("t", tabk)],
                                writes=[("t", kk)], tag="c.qk")
                    for f in range(8):
                        P.op("tensor", lambda t, f=f, t0=t0: t.transpose(
                            pdb[:, f * 128:(f + 1) * 128],
                            vv[:, f * NB + t0: f * NB + t0 + CH],
                            C["idf_b"][:]),
                            reads=[("t", "vv"), ("c", "idf_b")],
                            writes=[("psum", "pdb")], tag="c.vtp")
                    for h in range(NH):
                        P.op("scalar", lambda sc, h=h: sc.copy(
                            vtma[:, h * 65: h * 65 + 64],
                            pdb[:, h * 64:(h + 1) * 64]),
                            reads=[("psum", "pdb")], writes=[("t", "vtma")], tag="c.vtev")
                    for src, srck, dstt, kk in [(kc, "kc", kctm, "kctm"), (ks_, "ks", kstm, "kstm")]:
                        for h in range(NH):
                            P.op("tensor", lambda t, h=h, src=src, srck=srck: t.transpose(
                                pdb[:, h * 64:(h + 1) * 64],
                                src[0:64, h * CH:(h + 1) * CH],
                                C["idf_b"][0:64, 0:64]),
                                reads=[("t", srck), ("c", "idf_b")],
                                writes=[("psum", "pdb")], tag="c.ktp")
                        P.op("vector", lambda v, dstt=dstt, kk=kk: v.tensor_copy(dstt[:], pdb[:, 0:1024]),
                             reads=[("psum", "pdb")], writes=[("t", kk)], tag="c.ktev")
                    for h in range(NH):
                        fb = h // 2
                        pb_ = (h % 2) * 64
                        pban, kpb = (pa0, ("psum", "pa0")) if h % 2 == 0 else (pa1, ("psum", "pa1"))
                        P.op("tensor", lambda t, h=h: t.matmul(
                            pA[:, 0:CH],
                            kc[0:64, h * CH:(h + 1) * CH],
                            qc[0:64, h * CH:(h + 1) * CH],
                            start=True, stop=False),
                            reads=[("t", "kc"), ("t", "qc")],
                            writes=[("psum", "pA")], tag="h.amm")
                        P.op("tensor", lambda t, h=h: t.matmul(
                            pA[:, 0:CH],
                            ks_[0:64, h * CH:(h + 1) * CH],
                            qs_[0:64, h * CH:(h + 1) * CH],
                            start=False, stop=True),
                            reads=[("t", "ks"), ("t", "qs")],
                            writes=[("psum", "pA")], tag="h.amm2")
                        P.op("vector", lambda v: v.tensor_tensor(
                            ab[:], pA[:, 0:CH], C["MINC"][:], op=ALU.mult),
                            reads=[("psum", "pA"), ("c", "MINC")],
                            writes=[("t", "ab")], tag="h.mask")
                        P.op("tensor", lambda t, h=h, pban=pban: t.matmul(
                            pban[0:65, 0:CH], vtma[:, h * 65:(h + 1) * 65], ab[:],
                            start=True, stop=False),
                            reads=[("t", "vtma"), ("t", "ab")],
                            writes=[kpb], tag="h.omm")
                        P.op("tensor", lambda t, h=h, pban=pban: t.matmul(
                            pban[0:65, 0:CH],
                            stateb[0:64, h * 130: h * 130 + 65],
                            qc[0:64, h * CH:(h + 1) * CH],
                            start=False, stop=False),
                            reads=[("t", "stateb"), ("t", "qc")],
                            writes=[kpb], tag="h.cmm")
                        P.op("tensor", lambda t, h=h, pban=pban: t.matmul(
                            pban[0:65, 0:CH],
                            stateb[0:64, h * 130 + 65: h * 130 + 130],
                            qs_[0:64, h * CH:(h + 1) * CH],
                            start=False, stop=True),
                            reads=[("t", "stateb"), ("t", "qs")],
                            writes=[kpb], tag="h.cmm2")
                        P.op("scalar", lambda sc, pban=pban: sc.copy(
                            dstripe[:], pban[64:65, 0:CH]),
                            reads=[kpb], writes=[("t", "dstripe")], tag="h.den")
                        P.op("vector", lambda v: v.tensor_scalar(
                            denr[:], dstripe[:], 1e-6, None, op0=ALU.max),
                            reads=[("t", "dstripe")], writes=[("t", "denr")], tag="h.dmax")
                        P.op("vector", lambda v: v.reciprocal(denr[:], denr[:]),
                             reads=[("t", "denr")], writes=[("t", "denr")], tag="h.drec")
                        P.op("tensor", lambda t: t.matmul(
                            prep[0:64, 0:CH], C["c1"][0:1, 0:64], denr[:],
                            start=True, stop=True),
                            reads=[("t", "denr"), ("c", "c1")], writes=[kpr], tag="h.drep")
                        P.op("scalar", lambda sc: sc.copy(rrsb[:], prep[0:64, 0:CH]),
                             reads=[kpr], writes=[("t", "rrsb")], tag="h.drev")
                        P.op("vector", lambda v, h=h, fb=fb, pb_=pb_, t0=t0, pban=pban: v.scalar_tensor_tensor(
                            outT[pb_:pb_ + 64, fb * NB + t0: fb * NB + t0 + CH],
                            pban[0:64, 0:CH], 1.0, rrsb[:],
                            op0=ALU.mult, op1=ALU.mult),
                            reads=[kpb, ("t", "rrsb")],
                            writes=[("t", "outT")], tag="h.div")
                        if debug and s == 0 and blk == 0 and ch == 0 and h == 0:
                            P.op("vector", lambda v: v.tensor_copy(dbg_sb[:, 0:128], ab[:]),
                                 reads=[("t", "ab")], writes=[("t", "dbg")], tag="dbg.ab")
                            P.op("vector", lambda v: v.tensor_copy(dbg_sb[0:64, 128:256], qc[0:64, 0:128]),
                                 reads=[("t", "qc")], writes=[("t", "dbg")], tag="dbg.qc")
                            P.op("vector", lambda v: v.tensor_copy(dbg_sb[:, 256:384], vtma[:, 0:128]),
                                 reads=[("t", "vtma")], writes=[("t", "dbg")], tag="dbg.vt")
                            P.op("vector", lambda v: v.tensor_copy(dbg_sb[0:64, 384:512], kc[0:64, 0:128]),
                                 reads=[("t", "kc")], writes=[("t", "dbg")], tag="dbg.kc")
                            P.op("vector", lambda v, pban=pban: v.tensor_copy(dbg_sb[0:65, 512:640], pban[0:65, 0:CH]),
                                 reads=[("psum", "pa0")], writes=[("t", "dbg")], tag="dbg.num")
                            P.op("vector", lambda v: v.tensor_copy(dbg_sb[0:64, 640:704], qs_[0:64, 0:64]),
                                 reads=[("t", "qs")], writes=[("t", "dbg")], tag="dbg.qs")
                            P.op("vector", lambda v: v.tensor_copy(dbg_sb[0:64, 704:768], ks_[0:64, 0:64]),
                                 reads=[("t", "ks")], writes=[("t", "dbg")], tag="dbg.ks")
                        P.op("tensor", lambda t, h=h: t.matmul(
                            pS[0:64, 0:65], kctm[:, h * 64:(h + 1) * 64],
                            vtma[:, h * 65:(h + 1) * 65],
                            start=True, stop=True),
                            reads=[("t", "kctm"), ("t", "vtma")],
                            writes=[("psum", "pS")], tag="h.smm")
                        P.op("tensor", lambda t, h=h: t.matmul(
                            pS[0:64, 65:130], kstm[:, h * 64:(h + 1) * 64],
                            vtma[:, h * 65:(h + 1) * 65],
                            start=True, stop=True),
                            reads=[("t", "kstm"), ("t", "vtma")],
                            writes=[("psum", "pS")], tag="h.smm2")
                        P.op("vector", lambda v, h=h: v.tensor_tensor(
                            state[0:64, h * 130:(h + 1) * 130],
                            state[0:64, h * 130:(h + 1) * 130],
                            pS[0:64, 0:130], op=ALU.add),
                            reads=[("t", "state"), ("psum", "pS")],
                            writes=[("t", "state")], tag="h.supd")
                    P.op("vector", lambda v: v.tensor_copy(stateb[:], state[:]),
                         reads=[("t", "state")], writes=[("t", "stateb")], tag="c.s16")
                for m in range(8):
                    pban, kpb = (pa0, ("psum", "pa0")) if m % 2 == 0 else (pa1, ("psum", "pa1"))
                    for k in range(8):
                        P.op("tensor", lambda t, m=m, k=k, pban=pban: t.matmul(
                            pban[:, 0:NB],
                            wsb["wo"][:, (k * 8 + m) * 128:(k * 8 + m + 1) * 128],
                            outT[:, k * NB:(k + 1) * NB],
                            start=(k == 0), stop=(k == 7)),
                            reads=[("w", "wo"), ("t", "outT")],
                            writes=[kpb], tag="wo.mm")
                    P.op("vector", lambda v, m=m, pban=pban: v.tensor_scalar(
                        ln1.xc[:, m * NB:(m + 1) * NB], pban[:, 0:NB],
                        C["bo"][:, m:m + 1], None, op0=ALU.add),
                        reads=[("psum", "pa" + str(m % 2)), ("c", "bo")],
                        writes=[ln1.kxc], tag="wo.ev")
                ryrep = ln1.run_inplace(C, pstat, kps, prep, kpr)
                ln1.add_into(rr(x2b, 8), [("t", "x2b")], ryrep, kpr)
                P.dma("sync", x3_h[s].rearrange("p (f t) -> p f t", f=8)[:, :, c0:c1_],
                      rr(x2b, 8), reads=[("t", "x2b")], tag="x3.st")
        if debug:
            P.dma("sync", dbg_h, dbg_sb[:], reads=[("t", "dbg")], tag="dbg.st")
        P.emit()
    return nc


def l1attn_inputs(x2raw, weights, T=4096):
    i_idx = np.arange(128, dtype=np.float32)
    MINC = (i_idx[:, None] <= i_idx[None, :]).astype(np.float32)
    angles = np.pi / (2.0 * T) * np.arange(T, dtype=np.float64)
    d = {
        "x2": np.ascontiguousarray(x2raw, dtype=np.float32),
        "wq": tile_w(np.asarray(weights["L1_attn_Wq"], dtype=np.float32)),
        "wk": tile_w(np.asarray(weights["L1_attn_Wk"], dtype=np.float32)),
        "wv": tile_w(np.asarray(weights["L1_attn_Wv"], dtype=np.float32)),
        "wo": tile_w(np.asarray(weights["L1_attn_Wo"], dtype=np.float32)),
        "bq": np.ascontiguousarray(np.asarray(weights["L1_attn_bq"], dtype=np.float32).reshape(8, 128).T),
        "bk": np.ascontiguousarray(np.asarray(weights["L1_attn_bk"], dtype=np.float32).reshape(8, 128).T),
        "bv": np.ascontiguousarray(np.asarray(weights["L1_attn_bv"], dtype=np.float32).reshape(8, 128).T),
        "bo": np.ascontiguousarray(np.asarray(weights["L1_attn_bo"], dtype=np.float32).reshape(8, 128).T),
        "cosr": np.cos(angles)[None, :].astype(np.float32),
        "sinr": np.sin(angles)[None, :].astype(np.float32),
        "MINC": MINC,
    }
    d.update(common_const_arrays())
    return d


def build_l1ffn(nc, T=4096, SEQS=2):
    es = contextlib.ExitStack()
    NB = 512
    nblk = T // NB
    BIG = 1.0e6
    with es:
        P = Prog(nc, es)
        x3_h = nc.dram_tensor("x3", [SEQS, 128, 8 * T], F32, kind="ExternalInput").ap()
        out_h = nc.dram_tensor("out", [SEQS, T, 1024], F32, kind="ExternalOutput").ap()
        cw1_h = nc.dram_tensor("cw1", [128, 16 * 128], F32, kind="ExternalInput").ap()
        C = load_consts(P, es, nc, COMMON_CONSTS + [
            ("sigraw", (64, 1024)),
            ("knots", (1, 8)), ("temp", (1, 1)), ("osc", (1, 1)),
            ("cw2", (128, 4)), ("cb1", (128, 2)), ("cb2", (1, 2)),
            ("tabsp", (64, 65)), ("dirs", (64, 1024)),
            ("gamma", (128, 8)), ("beta", (128, 8)),
            ("grp8", (64, 8)), ("iota8x4", (128, 32)), ("iotaMB", (128, 32)),
            ("iota64", (128, 64)),
        ])
        register_fc(nc, C)
        cast_c(P, es, nc, C, "onescol", (128, 1))
        cast_c(P, es, nc, C, "idf", (128, 128))
        cast_c(P, es, nc, C, "cw2", (128, 4))
        cast_c(P, es, nc, C, "tabsp", (64, 65))
        cast_c(P, es, nc, C, "dirs", (64, 1024))
        cast_c(P, es, nc, C, "grp8", (64, 8))

        stage = alloc(es, nc, "stage", [128, 2048], F32)
        cw1b = alloc(es, nc, "cw1b", [128, 16 * 128], BF16)
        P.dma("sync", stage[:], cw1_h[:, 0:2048], writes=[("stage",)], tag="w.l")
        P.op("vector", lambda v: v.tensor_copy(cw1b[:], stage[:]),
             reads=[("stage",)], writes=[("w", "cw1")], tag="w.c")

        pstat = psum1(es, nc, "pstat")
        prep = psum1(es, nc, "prep")
        pa0 = psum1(es, nc, "pa0")
        pa1 = psum1(es, nc, "pa1")
        pg = psum1(es, nc, "pg")
        pdb = es.enter_context(nc.psum_tensor("pdb", [128, 1024], BF16))
        pdf = psum1(es, nc, "pdf")
        pdf2 = psum1(es, nc, "pdf2")
        kps, kpr = ("psum", "pstat"), ("psum", "prep")

        sigq = alloc(es, nc, "sigq", [64, 1024], F32)
        sigt = alloc(es, nc, "sigt", [64, 1024], F32)
        sigb = alloc(es, nc, "sigb", [64, 1024], BF16)
        csig = alloc(es, nc, "csig", [8, 1024], F32)
        csigb = alloc(es, nc, "csigb", [8, 1024], BF16)
        sigsT = alloc(es, nc, "sigsT", [128, 8 * 64], BF16)
        csigT = alloc(es, nc, "csigT", [128, 8 * 8], BF16)
        P.op("vector", lambda v: v.tensor_scalar(
            sigq[:], C["sigraw"][:], 0.3, None, op0=ALU.is_gt),
            reads=[("c", "sigraw")], writes=[("t", "sigq")], tag="s.gt")
        P.op("vector", lambda v: v.tensor_scalar(
            sigt[:], C["sigraw"][:], -0.3, None, op0=ALU.is_lt),
            reads=[("c", "sigraw")], writes=[("t", "sigt")], tag="s.lt")
        P.op("vector", lambda v: v.tensor_tensor(
            sigq[:], sigq[:], sigt[:], op=ALU.subtract),
            reads=[("t", "sigq"), ("t", "sigt")], writes=[("t", "sigq")], tag="s.sub")
        P.op("vector", lambda v: v.tensor_copy(sigb[:], sigq[:]),
             reads=[("t", "sigq")], writes=[("t", "sigb")], tag="s.cast")
        for half in range(2):
            P.op("tensor", lambda t, half=half: t.matmul(
                pg[0:8, half * 256:half * 256 + 256] if False else pg[0:8, 0:512],
                C["grp8_b"][:], sigb[:, half * 512:(half + 1) * 512],
                start=True, stop=True),
                reads=[("t", "sigb"), ("c", "grp8_b")], writes=[("psum", "pg")],
                tag="s.cmm")
            P.op("scalar", lambda s, half=half: s.activation(
                csig[:, half * 512:(half + 1) * 512], pg[0:8, 0:512], AF.Sign),
                reads=[("psum", "pg")], writes=[("t", "csig")], tag="s.sign")
        P.op("vector", lambda v: v.tensor_copy(csigb[:], csig[:]),
             reads=[("t", "csig")], writes=[("t", "csigb")], tag="s.ccast")
        for f in range(8):
            P.op("tensor", lambda t, f=f: t.transpose(
                pdb[:, f * 64:(f + 1) * 64],
                sigb[:, f * 128:(f + 1) * 128], C["idf_b"][0:64, 0:64]),
                reads=[("t", "sigb"), ("c", "idf_b")], writes=[("psum", "pdb")],
                tag="s.tp")
        P.op("vector", lambda v: v.tensor_copy(sigsT[:], pdb[:, 0:512]),
             reads=[("psum", "pdb")], writes=[("t", "sigsT")], tag="s.tpev")
        for f in range(8):
            P.op("tensor", lambda t, f=f: t.transpose(
                pdb[:, 512 + f * 8: 512 + (f + 1) * 8],
                csigb[:, f * 128:(f + 1) * 128], C["idf_b"][0:8, 0:8]),
                reads=[("t", "csigb"), ("c", "idf_b")], writes=[("psum", "pdb")],
                tag="s.ctp")
        P.op("vector", lambda v: v.tensor_copy(csigT[:], pdb[:, 512:576]),
             reads=[("psum", "pdb")], writes=[("t", "csigT")], tag="s.ctpev")
        slp = alloc(es, nc, "slp", [1, 8], F32)
        slpb = alloc(es, nc, "slpb", [128, 8], F32)
        P.op("vector", lambda v: v.tensor_tensor(
            slp[0:1, 0:7], C["knots"][0:1, 1:8], C["knots"][0:1, 0:7],
            op=ALU.subtract),
            reads=[("c", "knots")], writes=[("t", "slp")], tag="k.slope")
        P.op("vector", lambda v: v.tensor_copy(slp[0:1, 7:8], C["knots"][0:1, 0:1]),
             reads=[("c", "knots")], writes=[("t", "slp")], tag="k.k0")
        P.op("tensor", lambda t: t.matmul(prep[:, 0:8], C["c1"][:], slp[:],
                                          start=True, stop=True),
             reads=[("t", "slp"), ("c", "c1")], writes=[kpr], tag="k.rep")
        P.op("scalar", lambda s: s.copy(slpb[:], prep[:, 0:8]),
             reads=[kpr], writes=[("t", "slpb")], tag="k.repev")
        itp = alloc(es, nc, "itp", [1, 1], F32)
        itpb = alloc(es, nc, "itpb", [128, 1], F32)
        oscb = alloc(es, nc, "oscb", [128, 1], F32)
        P.op("scalar", lambda s: s.activation(itp[:], C["temp"][:], AF.Identity,
                                              bias=1e-6),
             reads=[("c", "temp"), ("c", "fc")], writes=[("t", "itp")], tag="t.add")
        P.op("vector", lambda v: v.reciprocal(itp[:], itp[:]),
             reads=[("t", "itp")], writes=[("t", "itp")], tag="t.rec")
        P.op("tensor", lambda t: t.matmul(prep[:, 0:1], C["c1"][:], itp[:],
                                          start=True, stop=True),
             reads=[("t", "itp"), ("c", "c1")], writes=[kpr], tag="t.rep")
        P.op("scalar", lambda s: s.copy(itpb[:], prep[:, 0:1]),
             reads=[kpr], writes=[("t", "itpb")], tag="t.repev")
        P.op("tensor", lambda t: t.matmul(prep[:, 0:1], C["c1"][:], C["osc"][:],
                                          start=True, stop=True),
             reads=[("c", "osc"), ("c", "c1")], writes=[kpr], tag="o.rep")
        P.op("scalar", lambda s: s.copy(oscb[:], prep[:, 0:1]),
             reads=[kpr], writes=[("t", "oscb")], tag="o.repev")

        x3b = alloc(es, nc, "x3b", [128, 8 * NB], F32)
        xfb = alloc(es, nc, "xfb", [128, 8 * NB], F32)
        csc = alloc(es, nc, "csc", [8, NB], F32)
        tscb = alloc(es, nc, "tscb", [64, NB], BF16)
        gtm = alloc(es, nc, "gtm", [128, 32], F32)
        stm = alloc(es, nc, "stm", [128, 32], F32)
        htmp = alloc(es, nc, "htmp", [128, 32], F32)
        htmp2 = alloc(es, nc, "htmp2", [128, 32], F32)
        acc = alloc(es, nc, "acc", [128, 32], F32)
        mx = alloc(es, nc, "mx", [128, 4], F32)
        cidx = alloc(es, nc, "cidx", [128, 4], F32)
        lidx = alloc(es, nc, "lidx", [128, 4], F32)
        tstm = alloc(es, nc, "tstm", [128, 4 * 64], F32)
        ohtm = alloc(es, nc, "ohtm", [128, 4 * 64], BF16)
        ohT = alloc(es, nc, "ohT", [64, NB], BF16)
        hb = alloc(es, nc, "hb", [128, 2 * NB], BF16)
        comp = alloc(es, nc, "comp", [1, 2 * NB], F32)
        spl = {nm2: alloc(es, nc, "spl_" + nm2, [16, NB], F32)
               for nm2 in ["w1a", "w1b", "b1", "w2"]}
        splb2 = alloc(es, nc, "splb2", [1, NB], F32)
        st1 = alloc(es, nc, "st1", [16, NB], F32)
        st2 = alloc(es, nc, "st2", [16, NB], F32)
        hhb = alloc(es, nc, "hhb", [16, NB], BF16)
        w2bb = alloc(es, nc, "w2bb", [16, NB], BF16)
        magr = alloc(es, nc, "magr", [128, NB], F32)
        dirt = alloc(es, nc, "dirt", [128, NB], F32)
        otm = alloc(es, nc, "otm", [128, 1024], F32)
        onecol16 = alloc(es, nc, "oc16", [16, 1], BF16)
        P.op("vector", lambda v: v.memset(onecol16[:], 1.0),
             writes=[("t", "oc16")], tag="oc16")
        ln1 = LNHelper(P, es, nc, NB, tag="1")

        for s in range(SEQS):
            for blk in range(nblk):
                c0, c1_ = blk * NB, (blk + 1) * NB
                P.dma("sync", rr(x3b, 8),
                      x3_h[s].rearrange("p (f t) -> p f t", f=8)[:, :, c0:c1_],
                      writes=[("t", "x3b")], tag="x3.ld")
                rrep = ln1.run(C, pstat, kps, prep, kpr, rr(x3b, 8), [("t", "x3b")])
                lnb = ln1.xb
                ln1.to_bf16(rr(lnb, 8), [ln1.kxb], rrep, kpr)
                ln1.mul_to(rr(xfb, 8), [("t", "xfb")], rrep, kpr)
                for k in range(8):
                    P.op("tensor", lambda t, k=k: t.matmul(
                        pg[0:8, 0:NB], csigT[:, k * 8:(k + 1) * 8],
                        lnb[:, k * NB:(k + 1) * NB],
                        start=(k == 0), stop=(k == 7)),
                        reads=[("t", "csigT"), ln1.kxb], writes=[("psum", "pg")],
                        tag="b.cmm")
                P.op("scalar", lambda s_: s_.copy(csc[:], pg[0:8, 0:NB]),
                     reads=[("psum", "pg")], writes=[("t", "csc")], tag="b.cev")
                for k in range(8):
                    P.op("tensor", lambda t, k=k: t.matmul(
                        pg[0:64, 0:NB], sigsT[:, k * 64:(k + 1) * 64],
                        lnb[:, k * NB:(k + 1) * NB],
                        start=(k == 0), stop=(k == 7)),
                        reads=[("t", "sigsT"), ln1.kxb], writes=[("psum", "pg")],
                        tag="b.tmm")
                P.op("scalar", lambda s_: s_.copy(tscb[:], pg[0:64, 0:NB]),
                     reads=[("psum", "pg")], writes=[("t", "tscb")], tag="b.tev")
                P.op("vector", lambda v: v.tensor_scalar(
                    csc[:], csc[:], itpb[0:8, 0:1], None, op0=ALU.mult),
                    reads=[("t", "csc"), ("t", "itpb")], writes=[("t", "csc")],
                    tag="b.it")
                P.op("scalar", lambda s_: s_.activation(csc[:], csc[:], AF.Sigmoid),
                     reads=[("t", "csc")], writes=[("t", "csc")], tag="b.sig")
                for st in range(4):
                    P.op("tensor", lambda t, st=st: t.transpose(
                        pdf[:, st * 8:(st + 1) * 8],
                        csc[:, st * 128:(st + 1) * 128], C["idf"][0:8, 0:8]),
                        reads=[("t", "csc"), ("c", "idf")], writes=[("psum", "pdf")],
                        tag="b.gtp")
                P.op("vector", lambda v: v.tensor_copy(gtm[:], pdf[:, 0:32]),
                     reads=[("psum", "pdf")], writes=[("t", "gtm")], tag="b.gtpev")

                def calibrate(src, srck, dst, dstk, width):
                    for j in range(7):
                        P.op("vector", lambda v, j=j: v.tensor_scalar(
                            htmp[:, 0:width], src[:, 0:width], 7.0, float(-j),
                            op0=ALU.mult, op1=ALU.add),
                            reads=[srck], writes=[("t", "htmp")], tag="cal.h")
                        P.op("vector", lambda v: v.tensor_scalar(
                            htmp[:, 0:width], htmp[:, 0:width], 0.0, 1.0,
                            op0=ALU.max, op1=ALU.min),
                            reads=[("t", "htmp")], writes=[("t", "htmp")], tag="cal.c")
                        if j == 0:
                            P.op("vector", lambda v: v.tensor_scalar(
                                dst[:, 0:width], htmp[:, 0:width],
                                slpb[:, 0:1], slpb[:, 7:8],
                                op0=ALU.mult, op1=ALU.add),
                                reads=[("t", "htmp"), ("t", "slpb")],
                                writes=[dstk], tag="cal.a0")
                        else:
                            P.op("vector", lambda v, j=j: v.scalar_tensor_tensor(
                                dst[:, 0:width], htmp[:, 0:width],
                                slpb[:, j:j + 1], dst[:, 0:width],
                                op0=ALU.mult, op1=ALU.add),
                                reads=[("t", "htmp"), ("t", "slpb"), dstk],
                                writes=[dstk], tag="cal.a")

                calibrate(gtm, ("t", "gtm"), acc, ("t", "acc"), 32)

                def argmax8(gsrc, gsrck, idxdst, idxk):
                    P.op("vector", lambda v: v.tensor_reduce(
                        mx[:], gsrc.ap().rearrange("p (s e) -> p s e", s=4),
                        axis=AX.X, op=ALU.max),
                        reads=[gsrck], writes=[("t", "mx")], tag="am.max")
                    P.op("vector", lambda v: v.tensor_tensor(
                        htmp.ap().rearrange("p (s e) -> p s e", s=4),
                        gsrc.ap().rearrange("p (s e) -> p s e", s=4),
                        mx.ap().unsqueeze(2).broadcast_to([128, 4, 8]),
                        op=ALU.is_ge),
                        reads=[gsrck, ("t", "mx")], writes=[("t", "htmp")], tag="am.eq")
                    P.op("vector", lambda v: v.tensor_tensor(
                        htmp2[:], htmp[:], C["iotaMB"][:], op=ALU.mult),
                        reads=[("t", "htmp"), ("c", "iotaMB")],
                        writes=[("t", "htmp2")], tag="am.mb")
                    P.op("vector", lambda v: v.tensor_scalar(
                        htmp2[:], htmp2[:], BIG, None, op0=ALU.add),
                        reads=[("t", "htmp2")], writes=[("t", "htmp2")], tag="am.big")
                    P.op("vector", lambda v: v.tensor_reduce(
                        idxdst[:], htmp2.ap().rearrange("p (s e) -> p s e", s=4),
                        axis=AX.X, op=ALU.min),
                        reads=[("t", "htmp2")], writes=[idxk], tag="am.min")

                argmax8(acc, ("t", "acc"), cidx, ("t", "cidx"))
                for st in range(4):
                    P.op("tensor", lambda t, st=st: t.transpose(
                        pdb[:, st * 64:(st + 1) * 64],
                        tscb[:, st * 128:(st + 1) * 128], C["idf_b"][0:64, 0:64]),
                        reads=[("t", "tscb"), ("c", "idf_b")],
                        writes=[("psum", "pdb")], tag="b.ttp")
                P.op("vector", lambda v: v.tensor_copy(tstm[:], pdb[:, 0:256]),
                     reads=[("psum", "pdb")], writes=[("t", "tstm")], tag="b.ttpev")
                for c in range(8):
                    P.op("vector", lambda v, c=c: v.tensor_scalar(
                        htmp[:, 0:4], cidx[:], float(c), None, op0=ALU.is_equal),
                        reads=[("t", "cidx")], writes=[("t", "htmp")], tag="sel.eq")
                    P.op("vector", lambda v, c=c: v.tensor_tensor(
                        htmp2.ap().rearrange("p (s e) -> p s e", s=4),
                        tstm.ap().rearrange("p (s e) -> p s e", s=4)[:, :, c * 8:(c + 1) * 8],
                        htmp[:, 0:4].unsqueeze(2).broadcast_to([128, 4, 8]),
                        op=ALU.mult),
                        reads=[("t", "tstm"), ("t", "htmp")],
                        writes=[("t", "htmp2")], tag="sel.m")
                    if c == 0:
                        P.op("vector", lambda v: v.tensor_copy(stm[:], htmp2[:]),
                             reads=[("t", "htmp2")], writes=[("t", "stm")], tag="sel.i")
                    else:
                        P.op("vector", lambda v: v.tensor_tensor(
                            stm[:], stm[:], htmp2[:], op=ALU.add),
                            reads=[("t", "stm"), ("t", "htmp2")],
                            writes=[("t", "stm")], tag="sel.a")
                P.op("vector", lambda v: v.tensor_scalar(
                    stm[:], stm[:], itpb[:, 0:1], None, op0=ALU.mult),
                    reads=[("t", "stm"), ("t", "itpb")], writes=[("t", "stm")],
                    tag="sel.it")
                P.op("scalar", lambda s_: s_.activation(stm[:], stm[:], AF.Sigmoid),
                     reads=[("t", "stm")], writes=[("t", "stm")], tag="sel.sig")
                calibrate(stm, ("t", "stm"), acc, ("t", "acc"), 32)
                argmax8(acc, ("t", "acc"), lidx, ("t", "lidx"))
                P.op("vector", lambda v: v.scalar_tensor_tensor(
                    cidx[:], cidx[:], 8.0, lidx[:], op0=ALU.mult, op1=ALU.add),
                    reads=[("t", "cidx"), ("t", "lidx")], writes=[("t", "cidx")],
                    tag="b.tidx")
                for st in range(4):
                    P.op("vector", lambda v, st=st: v.tensor_scalar(
                        ohtm[:, st * 64:(st + 1) * 64], C["iota64"][:],
                        cidx[:, st:st + 1], None, op0=ALU.is_equal),
                        reads=[("t", "cidx"), ("c", "iota64")],
                        writes=[("t", "ohtm")], tag="b.oh")
                for st in range(4):
                    P.op("tensor", lambda t, st=st: t.transpose(
                        pdb[0:64, 256 + st * 128: 256 + (st + 1) * 128],
                        ohtm[:, st * 64:(st + 1) * 64], C["idf_b"][:]),
                        reads=[("t", "ohtm"), ("c", "idf_b")],
                        writes=[("psum", "pdb")], tag="b.ohtp")
                P.op("vector", lambda v: v.tensor_copy(
                    ohT[:], pdb[0:64, 256:768]),
                    reads=[("psum", "pdb")], writes=[("t", "ohT")], tag="b.ohev")
                for m in range(2):
                    pban, kpb = (pa0, ("psum", "pa0")) if m == 0 else (pa1, ("psum", "pa1"))
                    for k in range(8):
                        P.op("tensor", lambda t, m=m, k=k, pban=pban: t.matmul(
                            pban[:, 0:NB],
                            cw1b[:, (k * 2 + m) * 128:(k * 2 + m + 1) * 128],
                            lnb[:, k * NB:(k + 1) * NB],
                            start=(k == 0), stop=(k == 7)),
                            reads=[("w", "cw1"), ln1.kxb], writes=[kpb], tag="cp.mm")
                    P.op("scalar", lambda s_, m=m, pban=pban: s_.activation(
                        hb[:, m * NB:(m + 1) * NB], pban[:, 0:NB], AF.Gelu,
                        bias=C["cb1"][:, m:m + 1]),
                        reads=[(("psum", "pa0") if m == 0 else ("psum", "pa1")), ("c", "cb1")],
                        writes=[("t", "hb")], tag="cp.ev")
                for m2 in range(2):
                    for k in range(2):
                        P.op("tensor", lambda t, k=k, m2=m2: t.matmul(
                            pg[0:1, 0:NB], C["cw2_b"][:, k * 2 + m2: k * 2 + m2 + 1],
                            hb[:, k * NB:(k + 1) * NB],
                            start=(k == 0), stop=(k == 1)),
                            reads=[("c", "cw2_b"), ("t", "hb")], writes=[("psum", "pg")],
                            tag="cp.mm2")
                    P.op("scalar", lambda s_, m2=m2: s_.activation(
                        comp[0:1, m2 * NB:(m2 + 1) * NB], pg[0:1, 0:NB], AF.Tanh,
                        bias=C["cb2"][0:1, m2:m2 + 1]),
                        reads=[("psum", "pg"), ("c", "cb2")], writes=[("t", "comp")],
                        tag="cp.tanh")
                for nm2, lo, hi in [("w1a", 0, 16), ("w1b", 16, 32), ("b1", 32, 48),
                                    ("w2", 48, 64)]:
                    P.op("tensor", lambda t, lo=lo, hi=hi: t.matmul(
                        pg[0:16, 0:NB], C["tabsp_b"][:, lo:hi], ohT[:],
                        start=True, stop=True),
                        reads=[("c", "tabsp_b"), ("t", "ohT")],
                        writes=[("psum", "pg")], tag="g.mm")
                    P.op("vector", lambda v, nm2=nm2: v.tensor_copy(
                        spl[nm2][:], pg[0:16, 0:NB]),
                        reads=[("psum", "pg")], writes=[("t", "spl" + nm2)], tag="g.ev")
                P.op("tensor", lambda t: t.matmul(
                    pg[0:1, 0:NB], C["tabsp_b"][:, 64:65], ohT[:],
                    start=True, stop=True),
                    reads=[("c", "tabsp_b"), ("t", "ohT")], writes=[("psum", "pg")],
                    tag="g.mm2")
                P.op("vector", lambda v: v.tensor_copy(splb2[:], pg[0:1, 0:NB]),
                     reads=[("psum", "pg")], writes=[("t", "splb2")], tag="g.ev2")
                P.op("tensor", lambda t: t.matmul(
                    prep[0:16, 0:NB], C["c1"][0:1, 0:16], comp[0:1, 0:NB],
                    start=True, stop=True),
                    reads=[("t", "comp"), ("c", "c1")], writes=[kpr], tag="sp.r0")
                P.op("vector", lambda v: v.tensor_tensor(
                    st1[:], spl["w1a"][:], prep[0:16, 0:NB], op=ALU.mult),
                    reads=[("t", "splw1a"), kpr], writes=[("t", "st1")], tag="sp.m0")
                P.op("tensor", lambda t: t.matmul(
                    prep[0:16, 0:NB], C["c1"][0:1, 0:16], comp[0:1, NB:2 * NB],
                    start=True, stop=True),
                    reads=[("t", "comp"), ("c", "c1")], writes=[kpr], tag="sp.r1")
                P.op("vector", lambda v: v.tensor_tensor(
                    st2[:], spl["w1b"][:], prep[0:16, 0:NB], op=ALU.mult),
                    reads=[("t", "splw1b"), kpr], writes=[("t", "st2")], tag="sp.m1")
                P.op("vector", lambda v: v.tensor_tensor(
                    st1[:], st1[:], st2[:], op=ALU.add),
                    reads=[("t", "st1"), ("t", "st2")], writes=[("t", "st1")], tag="sp.a")
                P.op("vector", lambda v: v.tensor_tensor(
                    st1[:], st1[:], spl["b1"][:], op=ALU.add),
                    reads=[("t", "st1"), ("t", "splb1")], writes=[("t", "st1")], tag="sp.b")
                P.op("vector", lambda v: v.tensor_scalar(
                    hhb[:], st1[:], 0.0, None, op0=ALU.max),
                    reads=[("t", "st1")], writes=[("t", "hhb")], tag="sp.relu")
                P.op("vector", lambda v: v.tensor_copy(w2bb[:], spl["w2"][:]),
                     reads=[("t", "splw2")], writes=[("t", "w2bb")], tag="sp.w2c")
                P.op("vector", lambda v: v.tensor_tensor(
                    hhb[:], hhb[:], w2bb[:], op=ALU.mult),
                    reads=[("t", "hhb"), ("t", "w2bb")], writes=[("t", "hhb")], tag="sp.hw")
                P.op("tensor", lambda t: t.matmul(
                    pg[0:1, 0:NB], onecol16[:], hhb[:], start=True, stop=True),
                    reads=[("t", "hhb"), ("t", "oc16")], writes=[("psum", "pg")],
                    tag="sp.sum")
                P.op("vector", lambda v: v.tensor_scalar(
                    splb2[:], splb2[:], oscb[0:1, 0:1], None, op0=ALU.mult),
                    reads=[("t", "splb2"), ("t", "oscb")], writes=[("t", "splb2")],
                    tag="sp.b2s")
                P.op("vector", lambda v: v.scalar_tensor_tensor(
                    splb2[:], pg[0:1, 0:NB], oscb[0:1, 0:1], splb2[:],
                    op0=ALU.mult, op1=ALU.add),
                    reads=[("psum", "pg"), ("t", "oscb"), ("t", "splb2")],
                    writes=[("t", "splb2")], tag="sp.mag")
                P.op("tensor", lambda t: t.matmul(
                    prep[:, 0:NB], C["c1"][:], splb2[:], start=True, stop=True),
                    reads=[("t", "splb2"), ("c", "c1")], writes=[kpr], tag="sp.mrep")
                P.op("scalar", lambda s_: s_.copy(magr[:], prep[:, 0:NB]),
                     reads=[kpr], writes=[("t", "magr")], tag="sp.mrev")
                for m in range(8):
                    P.op("tensor", lambda t, m=m: t.matmul(
                        pdf[:, 0:NB], C["dirs_b"][:, m * 128:(m + 1) * 128], ohT[:],
                        start=True, stop=True),
                        reads=[("c", "dirs_b"), ("t", "ohT")],
                        writes=[("psum", "pdf")], tag="y.dmm")
                    P.op("vector", lambda v, m=m: v.tensor_tensor(
                        dirt[:], pdf[:, 0:NB], magr[:], op=ALU.mult),
                        reads=[("psum", "pdf"), ("t", "magr")],
                        writes=[("t", "dirt")], tag="y.dm")
                    P.op("vector", lambda v, m=m: v.tensor_tensor(
                        ln1.xc[:, m * NB:(m + 1) * NB], dirt[:],
                        xfb[:, m * NB:(m + 1) * NB], op=ALU.add),
                        reads=[("t", "dirt"), ("t", "xfb")],
                        writes=[ln1.kxc], tag="y.add")
                ryrep = ln1.run_inplace(C, pstat, kps, prep, kpr)
                ln1.scale_inplace(ryrep, kpr)
                for m in range(8):
                    P.op("vector", lambda v, m=m: v.tensor_scalar(
                        ln1.xc[:, m * NB:(m + 1) * NB], ln1.xc[:, m * NB:(m + 1) * NB],
                        C["gamma"][:, m:m + 1], C["beta"][:, m:m + 1],
                        op0=ALU.mult, op1=ALU.add),
                        reads=[ln1.kxc, ("c", "gamma"), ("c", "beta")],
                        writes=[ln1.kxc], tag="y.gb")
                rzrep = ln1.run_inplace(C, pstat, kps, prep, kpr)
                ln1.add_into(rr(x3b, 8), [("t", "x3b")], rzrep, kpr)
                for tt in range(NB // 128):
                    for f in range(8):
                        pb_t = pdf if f < 4 else pdf2
                        P.op("tensor", lambda t, tt=tt, f=f, pb_t=pb_t: t.transpose(
                            pb_t[:, (f % 4) * 128:(f % 4 + 1) * 128],
                            x3b[:, f * NB + tt * 128: f * NB + (tt + 1) * 128],
                            C["idf"][:]),
                            reads=[("t", "x3b"), ("c", "idf")],
                            writes=[("psum", "pdf" if f < 4 else "pdf2")], tag="e.tp")
                    P.op("vector", lambda v: v.tensor_copy(otm[:, 0:512], pdf[:, 0:512]),
                         reads=[("psum", "pdf")], writes=[("t", "otm")], tag="e.ev")
                    P.op("vector", lambda v: v.tensor_copy(otm[:, 512:1024], pdf2[:, 0:512]),
                         reads=[("psum", "pdf2")], writes=[("t", "otm")], tag="e.ev2")
                    P.dma("sync", out_h[s, c0 + tt * 128: c0 + (tt + 1) * 128, :], otm[:],
                          reads=[("t", "otm")], tag="e.st")
        P.emit()
    return nc


def l1ffn_inputs(x3raw, weights, T=4096):
    w1 = np.asarray(weights["L1_ffn_spline_w1"], dtype=np.float32)
    b1 = np.asarray(weights["L1_ffn_spline_b1"], dtype=np.float32)
    w2 = np.asarray(weights["L1_ffn_spline_w2"], dtype=np.float32)[:, :, 0]
    b2 = np.asarray(weights["L1_ffn_spline_b2"], dtype=np.float32)
    tabsp = np.concatenate([w1[:, 0, :], w1[:, 1, :], b1, w2, b2], axis=1)
    cw2 = np.asarray(weights["L1_ffn_compress_w2"], dtype=np.float32)
    cw2t = cw2.reshape(2, 128, 2).transpose(1, 0, 2).reshape(128, 4)
    i8 = np.arange(8, dtype=np.float32)
    grp8 = np.zeros((64, 8), dtype=np.float32)
    for j in range(64):
        grp8[j, j // 8] = 1.0 / 8.0
    iota8x4 = np.tile(i8, (128, 4))
    d = {
        "x3": np.ascontiguousarray(x3raw, dtype=np.float32),
        "cw1": tile_w(np.asarray(weights["L1_ffn_compress_w1"], dtype=np.float32)),
        "sigraw": np.asarray(weights["L1_ffn_signatures_raw"], dtype=np.float32),
        "knots": np.asarray(weights["L1_ffn_knot_values"], dtype=np.float32).reshape(1, 8),
        "temp": np.asarray(weights["L1_ffn_temperature"], dtype=np.float32).reshape(1, 1),
        "osc": np.asarray(weights["L1_ffn_output_scale"], dtype=np.float32).reshape(1, 1),
        "cw2": cw2t,
        "cb1": np.ascontiguousarray(np.asarray(weights["L1_ffn_compress_b1"], dtype=np.float32).reshape(2, 128).T),
        "cb2": np.asarray(weights["L1_ffn_compress_b2"], dtype=np.float32).reshape(1, 2),
        "tabsp": tabsp,
        "dirs": np.asarray(weights["L1_ffn_directions"], dtype=np.float32),
        "gamma": np.ascontiguousarray(np.asarray(weights["L1_ffn_norm_gamma"], dtype=np.float32).reshape(8, 128).T),
        "beta": np.ascontiguousarray(np.asarray(weights["L1_ffn_norm_beta"], dtype=np.float32).reshape(8, 128).T),
        "grp8": grp8,
        "iota8x4": iota8x4.astype(np.float32),
        "iotaMB": (iota8x4 - 1.0e6).astype(np.float32),
        "iota64": np.tile(np.arange(64, dtype=np.float32), (128, 1)),
    }
    d.update(common_const_arrays())
    return d


def build_l0ffn_attn(nc, T=4096, SEQS=2, debug=False):
    es = contextlib.ExitStack()
    NB = 512
    nblk = T // NB
    NH = 16
    with es:
        P = Prog(nc, es)
        z_h = nc.dram_tensor("z", [SEQS, 128, 8 * T], F32, kind="ExternalInput").ap()
        zo_h = nc.dram_tensor("zo", [SEQS, 128, 8 * T], F32, kind="ExternalOutput").ap()
        if debug:
            dbg_h = nc.dram_tensor("dbg", [128, 1792], F32, kind="ExternalOutput").ap()
            dbg_sb = alloc(es, nc, "dbg_sb", [128, 1792], F32)
            P.op("vector", lambda v: v.memset(dbg_sb[:], 0.0), writes=[("t", "dbg")], tag="dbg.z")
        wqkv_h = nc.dram_tensor("wqkv", [128, 192 * 128], F32, kind="ExternalInput").ap()
        wo_h = nc.dram_tensor("wo", [128, 64 * 128], F32, kind="ExternalInput").ap()
        rot_h = {nm2: nc.dram_tensor(nm2, [128, T], F32, kind="ExternalInput").ap()
                 for nm2 in ["csq", "ssq", "csk", "ssk"]}
        C = load_consts(P, es, nc, COMMON_CONSTS + [
            ("qsc2", (128, 1)), ("ksc2", (128, 1)),
            ("qsrow", (1, 64)), ("ksrow", (1, 64)),
            ("blockhalf", (128, 2)), ("selhalf", (2, 128)),
            ("maskpack", (128, 1536)),
        ])
        register_fc(nc, C)
        cast_c(P, es, nc, C, "onescol", (128, 1))
        cast_c(P, es, nc, C, "idf", (128, 128))
        cast_c(P, es, nc, C, "blockhalf", (128, 2))
        cast_c(P, es, nc, C, "maskpack", (128, 1536))

        stage = alloc(es, nc, "stage", [128, 1024], F32)
        wqkvb = alloc(es, nc, "wqkvb", [128, 192 * 128], BF16)
        wob = alloc(es, nc, "wob", [128, 64 * 128], BF16)
        for r in range(24):
            P.dma("sync", stage[:], wqkv_h[:, r * 1024:(r + 1) * 1024],
                  writes=[("stage",)], tag="w.l")
            P.op("vector", lambda v, r=r: v.tensor_copy(
                wqkvb[:, r * 1024:(r + 1) * 1024], stage[:]),
                reads=[("stage",)], writes=[("w", "qkv")], tag="w.c")
        for r in range(8):
            P.dma("sync", stage[:], wo_h[:, r * 1024:(r + 1) * 1024],
                  writes=[("stage",)], tag="w.l2")
            P.op("vector", lambda v, r=r: v.tensor_copy(
                wob[:, r * 1024:(r + 1) * 1024], stage[:]),
                reads=[("stage",)], writes=[("w", "o")], tag="w.c2")
        rott = {}
        for nm2 in ["csq", "ssq", "csk", "ssk"]:
            rt = alloc(es, nc, "t_" + nm2, [128, T], BF16)
            rott[nm2] = rt
            for r0 in range(0, T, 1024):
                wdt = min(1024, T - r0)
                P.dma("sync", stage[:, 0:wdt], rot_h[nm2][:, r0:r0 + wdt],
                      writes=[("stage",)], tag="rt.l")
                P.op("vector", lambda v, rt=rt, r0=r0, wdt=wdt: v.tensor_copy(
                    rt[:, r0:r0 + wdt], stage[:, 0:wdt]),
                    reads=[("stage",)], writes=[("t", "rott")], tag="rt.c")
        sc1 = alloc(es, nc, "sc1", [1, 2], F32)
        negc = alloc(es, nc, "negc", [128, 1], F32)
        P.op("vector", lambda v: v.tensor_reduce(
            sc1[0:1, 0:1], C["qsrow"][:], axis=AX.X, op=ALU.max,
            apply_absolute_value=True),
            reads=[("c", "qsrow")], writes=[("t", "sc1")], tag="c.mq")
        P.op("vector", lambda v: v.tensor_reduce(
            sc1[0:1, 1:2], C["ksrow"][:], axis=AX.X, op=ALU.max,
            apply_absolute_value=True),
            reads=[("c", "ksrow")], writes=[("t", "sc1")], tag="c.mk")
        P.op("vector", lambda v: v.tensor_tensor(
            sc1[0:1, 0:1], sc1[0:1, 0:1], sc1[0:1, 1:2], op=ALU.mult),
            reads=[("t", "sc1")], writes=[("t", "sc1")], tag="c.mm")
        P.op("vector", lambda v: v.tensor_scalar(
            sc1[0:1, 0:1], sc1[0:1, 0:1], -8.0 * 3.47, None, op0=ALU.mult),
            reads=[("t", "sc1")], writes=[("t", "sc1")], tag="c.s")
        prep = psum1(es, nc, "prep")
        kpr = ("psum", "prep")
        P.op("tensor", lambda t: t.matmul(prep[:, 0:1], C["c1"][:], sc1[0:1, 0:1],
                                          start=True, stop=True),
             reads=[("t", "sc1"), ("c", "c1")], writes=[kpr], tag="c.rep")
        P.op("scalar", lambda s: s.copy(negc[:], prep[:, 0:1]),
             reads=[kpr], writes=[("t", "negc")], tag="c.repev")

        zblk = alloc(es, nc, "zblk", [128, 8 * NB], F32)
        shz = alloc(es, nc, "shz", [128, 8 * NB], BF16)
        qkvb = alloc(es, nc, "qkvb", [128, 24 * NB], BF16)
        sqb = alloc(es, nc, "sqb", [128, NB], BF16)
        nrm = alloc(es, nc, "nrm", [2, NB], F32)
        shq = alloc(es, nc, "shq", [128, NB], BF16)
        tq1 = alloc(es, nc, "tq1", [128, NB], BF16)
        kring = alloc(es, nc, "kring", [128, 8 * 6 * 128], BF16)
        vring = alloc(es, nc, "vring", [128, 6 * NH * 65], BF16)
        abp = alloc(es, nc, "abp", [128, 1536], BF16)
        dstripe = alloc(es, nc, "dstripe", [1, NB], F32)
        rrsb = alloc(es, nc, "rrsb", [64, NB], F32)
        aout = alloc(es, nc, "aout", [128, 8 * NB], BF16)
        bcol = alloc(es, nc, "bcol", [128, 8], F32)

        pstat = psum1(es, nc, "pstat")
        pa0 = psum1(es, nc, "pa0")
        pa1 = psum1(es, nc, "pa1")
        pS = es.enter_context(nc.psum_tensor("pS", [128, 1536], F32))
        pdb = es.enter_context(nc.psum_tensor("pdb", [128, 1024], BF16))
        kps = ("psum", "pstat")

        SLOT_W = [128, 256, 384, 384, 256, 128]
        SLOT_O = [0, 128, 384, 768, 1152, 1408]
        SLOT_Q = [0, 0, 0, 128, 256, 384]
        SPIECES = [(0, 0, 0, 128), (1, 128, 0, 256), (2, 384, 0, 128),
                   (2, 512, 128, 256), (3, 768, 128, 256), (3, 1024, 384, 128),
                   (4, 1152, 256, 256), (5, 1408, 384, 128)]

        for s in range(SEQS):
            P.op("vector", lambda v: v.memset(vring[:], 1.0),
                 writes=[("t", "vring")], tag="r.vz")
            P.op("vector", lambda v: v.memset(kring[:], 0.0),
                 writes=[("t", "kring")], tag="r.kz")
            P.op("vector", lambda v: v.memset(bcol[:], 0.0),
                 writes=[("t", "bcol"), ("t", "bcolc")], tag="r.bz")
            for blk in range(nblk):
                c0, c1_ = blk * NB, (blk + 1) * NB
                P.dma("sync", rr(zblk, 8),
                      z_h[s].rearrange("p (f t) -> p f t", f=8)[:, :, c0:c1_],
                      writes=[("t", "zblk")], tag="z.ld")
                P.op("vector", lambda v: v.tensor_copy(
                    bcol[:, 4:8], r3(zblk, 8, NB, NB - 1, NB)[:, 4:8, 0]),
                    reads=[("t", "zblk")], writes=[("t", "bcolc")], tag="sh.sv")
                P.op("vector", lambda v: v.tensor_copy(
                    rr(shz, 8)[:, 0:4, :], rr(zblk, 8)[:, 0:4, :]),
                    reads=[("t", "zblk")], writes=[("t", "shz")], tag="sh.a")
                P.op("vector", lambda v: v.tensor_copy(
                    rr(shz, 8)[:, 4:8, 1:NB], rr(zblk, 8)[:, 4:8, 0:NB - 1]),
                    reads=[("t", "zblk")], writes=[("t", "shz")], tag="sh.b")
                P.op("vector", lambda v: v.tensor_copy(
                    rr(shz, 8)[:, 4:8, 0], bcol[:, 0:4]),
                    reads=[("t", "bcol")], writes=[("t", "shz")], tag="sh.c")
                for m in range(24):
                    pban, kpb = (pa0, ("psum", "pa0")) if m % 2 == 0 else (pa1, ("psum", "pa1"))
                    for k in range(8):
                        P.op("tensor", lambda t, m=m, k=k, pban=pban: t.matmul(
                            pban[:, 0:NB],
                            wqkvb[:, (k * 24 + m) * 128:(k * 24 + m + 1) * 128],
                            shz[:, k * NB:(k + 1) * NB],
                            start=(k == 0), stop=(k == 7)),
                            reads=[("w", "qkv"), ("t", "shz")],
                            writes=[kpb], tag="p.mm")
                    P.op("scalar", lambda sc, m=m, pban=pban: sc.copy(
                        qkvb[:, m * NB:(m + 1) * NB], pban[:, 0:NB]),
                        reads=[("psum", "pa" + str(m % 2))],
                        writes=[("t", "qkv")], tag="p.ev")
                for part, scol, cst, sst in [("q", "qsc2", "csq", "ssq"),
                                             ("k", "ksc2", "csk", "ssk")]:
                    for f in range(8):
                        m = f if part == "q" else 8 + f
                        dbgk = debug and s == 0 and blk == 0 and part == "k" and f == 0
                        if dbgk:
                            P.op("vector", lambda v, m=m: v.tensor_copy(
                                dbg_sb[0:64, 0:128], qkvb[0:64, m * NB + 384: m * NB + 512]),
                                reads=[("t", "qkv")], writes=[("t", "dbg")], tag="dbg.s1")
                        P.op("scalar", lambda sc, m=m: sc.activation(
                            sqb[:], qkvb[:, m * NB:(m + 1) * NB], AF.Square),
                            reads=[("t", "qkv")], writes=[("t", "sqb")], tag="n.sq")
                        P.op("tensor", lambda t: t.matmul(
                            pstat[0:2, 0:NB], C["blockhalf_b"][:], sqb[:],
                            start=True, stop=True),
                            reads=[("t", "sqb"), ("c", "blockhalf_b")],
                            writes=[kps], tag="n.mm")
                        P.op("scalar", lambda sc: sc.activation(
                            nrm[:], pstat[0:2, 0:NB], AF.Sqrt),
                            reads=[kps], writes=[("t", "nrm")], tag="n.sqrt")
                        P.op("vector", lambda v: v.tensor_scalar(
                            nrm[:], nrm[:], 1e-12, None, op0=ALU.max),
                            reads=[("t", "nrm"), ("c", "fc")], writes=[("t", "nrm")], tag="n.max")
                        P.op("vector", lambda v: v.reciprocal(nrm[:], nrm[:]),
                             reads=[("t", "nrm")], writes=[("t", "nrm")], tag="n.rec")
                        P.op("tensor", lambda t: t.matmul(
                            prep[:, 0:NB], C["selhalf"][:], nrm[:],
                            start=True, stop=True),
                            reads=[("t", "nrm"), ("c", "selhalf")], writes=[kpr],
                            tag="n.rep")
                        P.op("vector", lambda v, m=m, scol=scol: v.scalar_tensor_tensor(
                            qkvb[:, m * NB:(m + 1) * NB], qkvb[:, m * NB:(m + 1) * NB],
                            C[scol][:], prep[:, 0:NB], op0=ALU.mult, op1=ALU.mult),
                            reads=[("t", "qkv"), ("c", scol), kpr],
                            writes=[("t", "qkv")], tag="n.app")
                        if dbgk:
                            P.op("vector", lambda v, m=m: v.tensor_copy(
                                dbg_sb[0:64, 128:256], qkvb[0:64, m * NB + 384: m * NB + 512]),
                                reads=[("t", "qkv")], writes=[("t", "dbg")], tag="dbg.s2")
                        for pp in range(4):
                            src_b = (pp + 1) % 2 * 32 + (pp // 2) * 64
                            dst_b = pp * 32
                            P.op("vector", lambda v, m=m, src_b=src_b, dst_b=dst_b: v.tensor_copy(
                                shq[dst_b:dst_b + 32, :],
                                qkvb[src_b:src_b + 32, m * NB:(m + 1) * NB]),
                                reads=[("t", "qkv")], writes=[("t", "shq")], tag="rt.sh")
                        P.op("vector", lambda v, m=m, cst=cst, c0=c0, c1_=c1_: v.tensor_tensor(
                            tq1[:], qkvb[:, m * NB:(m + 1) * NB],
                            rott[cst][:, c0:c1_], op=ALU.mult),
                            reads=[("t", "qkv"), ("t", "rott")], writes=[("t", "tq1")],
                            tag="rt.c")
                        P.op("vector", lambda v, sst=sst, c0=c0, c1_=c1_: v.tensor_tensor(
                            shq[:], shq[:], rott[sst][:, c0:c1_], op=ALU.mult),
                            reads=[("t", "shq"), ("t", "rott")], writes=[("t", "shq")],
                            tag="rt.s")
                        P.op("vector", lambda v, m=m: v.tensor_tensor(
                            qkvb[:, m * NB:(m + 1) * NB], tq1[:], shq[:], op=ALU.add),
                            reads=[("t", "tq1"), ("t", "shq")], writes=[("t", "qkv")],
                            tag="rt.a")
                        if dbgk:
                            P.op("vector", lambda v, m=m: v.tensor_copy(
                                dbg_sb[0:64, 256:384], qkvb[0:64, m * NB + 384: m * NB + 512]),
                                reads=[("t", "qkv")], writes=[("t", "dbg")], tag="dbg.s3")
                for j in range(4):
                    kt = 4 * blk + j
                    slot = kt % 6
                    P.op("vector", lambda v, j=j, slot=slot: v.tensor_copy(
                        kring.ap().rearrange("p (f q) -> p f q", f=8)[:, :, slot * 128:(slot + 1) * 128],
                        qkvb.ap().rearrange("p (m q) -> p m q", m=24)[:, 8:16, j * 128:(j + 1) * 128]),
                        reads=[("t", "qkv")], writes=[("t", "kring")], tag="r.kf")
                for j in range(4):
                    kt = 4 * blk + j
                    slot = kt % 6
                    for f in range(8):
                        P.op("tensor", lambda t, j=j, f=f: t.transpose(
                            pdb[:, f * 128:(f + 1) * 128],
                            qkvb[:, (16 + f) * NB + j * 128: (16 + f) * NB + (j + 1) * 128],
                            C["idf_b"][:]),
                            reads=[("t", "qkv"), ("c", "idf_b")],
                            writes=[("psum", "pdb")], tag="r.vtp")
                    for h in range(NH):
                        P.op("scalar", lambda sc, h=h, slot=slot: sc.copy(
                            vring[:, slot * NH * 65 + h * 65: slot * NH * 65 + h * 65 + 64],
                            pdb[:, h * 64:(h + 1) * 64]),
                            reads=[("psum", "pdb")], writes=[("t", "vring")], tag="r.vev")
                first = (blk == 0)
                slots_rel = range(2, 6) if first else range(6)
                for h in range(NH):
                    fb = h // 2
                    pb_ = (h % 2) * 64
                    pban, kpb = (pa0, ("psum", "pa0")) if h % 2 == 0 else (pa1, ("psum", "pa1"))
                    for (r_, po, qo, wd) in SPIECES:
                        if first and r_ < 2:
                            continue
                        kt = 4 * blk + r_ - 2
                        slot = kt % 6
                        P.op("tensor", lambda t, slot=slot, fb=fb, pb_=pb_, po=po, qo=qo, wd=wd: t.matmul(
                            pS[:, po:po + wd],
                            kring[pb_:pb_ + 64, fb * 768 + slot * 128: fb * 768 + (slot + 1) * 128],
                            qkvb[pb_:pb_ + 64, fb * NB + qo: fb * NB + qo + wd],
                            start=True, stop=True),
                            reads=[("t", "kring"), ("t", "qkv")],
                            writes=[("psum", "pS")], tag="a.smm")
                    lo = 384 if first else 0
                    P.op("scalar", lambda sc, lo=lo: sc.activation(
                        abp[:, lo:1536], pS[:, lo:1536], AF.Exp,
                        bias=negc[:], scale=8.0),
                        reads=[("psum", "pS"), ("t", "negc")],
                        writes=[("t", "abp")], tag="a.exp")
                    P.op("vector", lambda v, lo=lo: v.tensor_tensor(
                        abp[:, lo:1536], abp[:, lo:1536], C["maskpack_b"][:, lo:1536],
                        op=ALU.mult),
                        reads=[("t", "abp"), ("c", "maskpack_b")],
                        writes=[("t", "abp")], tag="a.mask")
                    for qs in range(4):
                        rs = [r_ for r_ in (qs, qs + 1, qs + 2)
                              if (not first) or r_ >= 2]
                        for ri, r_ in enumerate(rs):
                            kt = 4 * blk + r_ - 2
                            slot = kt % 6
                            off = SLOT_O[r_] + (qs * 128 - SLOT_Q[r_])
                            P.op("tensor", lambda t, slot=slot, h=h, pban=pban,
                                 qs=qs, off=off, st=(ri == 0), sp=(ri == len(rs) - 1): t.matmul(
                                pban[0:65, qs * 128:(qs + 1) * 128],
                                vring[:, slot * NH * 65 + h * 65: slot * NH * 65 + (h + 1) * 65],
                                abp[:, off:off + 128],
                                start=st, stop=sp),
                                reads=[("t", "vring"), ("t", "abp")],
                                writes=[kpb], tag="a.vmm")
                    if debug and s == 0 and blk == 0 and h == 0:
                        P.op("vector", lambda v: v.tensor_copy(dbg_sb[:, 384:1536], pS[:, 384:1536]),
                             reads=[("psum", "pS")], writes=[("t", "dbg")], tag="dbg.s")
                        P.op("vector", lambda v: v.tensor_copy(dbg_sb[0:64, 1536:1664],
                             qkvb[0:64, 8 * NB + 384: 8 * NB + 512]),
                             reads=[("t", "qkv")], writes=[("t", "dbg")], tag="dbg.k3")
                        P.op("vector", lambda v: v.tensor_copy(dbg_sb[0:64, 1664:1792],
                             kring[0:64, 3 * 128:4 * 128]),
                             reads=[("t", "kring")], writes=[("t", "dbg")], tag="dbg.k1")
                    P.op("scalar", lambda sc, pban=pban: sc.copy(
                        dstripe[:], pban[64:65, 0:NB]),
                        reads=[kpb], writes=[("t", "dstripe")], tag="a.den")
                    P.op("vector", lambda v: v.reciprocal(dstripe[:], dstripe[:]),
                         reads=[("t", "dstripe")], writes=[("t", "dstripe")], tag="a.rec")
                    P.op("tensor", lambda t: t.matmul(
                        prep[0:64, 0:NB], C["c1"][0:1, 0:64], dstripe[:],
                        start=True, stop=True),
                        reads=[("t", "dstripe"), ("c", "c1")], writes=[kpr], tag="a.rep")
                    P.op("scalar", lambda sc: sc.copy(rrsb[:], prep[0:64, 0:NB]),
                         reads=[kpr], writes=[("t", "rrsb")], tag="a.rev")
                    P.op("vector", lambda v, h=h, fb=fb, pb_=pb_, pban=pban: v.scalar_tensor_tensor(
                        aout[pb_:pb_ + 64, fb * NB:(fb + 1) * NB],
                        pban[0:64, 0:NB], 1.0, rrsb[:],
                        op0=ALU.mult, op1=ALU.mult),
                        reads=[kpb, ("t", "rrsb")], writes=[("t", "aout")], tag="a.div")
                for m in range(8):
                    pban, kpb = (pa0, ("psum", "pa0")) if m % 2 == 0 else (pa1, ("psum", "pa1"))
                    for k in range(8):
                        P.op("tensor", lambda t, m=m, k=k, pban=pban: t.matmul(
                            pban[:, 0:NB],
                            wob[:, (k * 8 + m) * 128:(k * 8 + m + 1) * 128],
                            aout[:, k * NB:(k + 1) * NB],
                            start=(k == 0), stop=(k == 7)),
                            reads=[("w", "o"), ("t", "aout")],
                            writes=[kpb], tag="wo.mm")
                    P.op("vector", lambda v, m=m, pban=pban: v.tensor_tensor(
                        zblk[:, m * NB:(m + 1) * NB], zblk[:, m * NB:(m + 1) * NB],
                        pban[:, 0:NB], op=ALU.add),
                        reads=[("psum", "pa" + str(m % 2)), ("t", "zblk")],
                        writes=[("t", "zblk")], tag="wo.ev")
                P.op("vector", lambda v: v.tensor_copy(bcol[:, 0:4], bcol[:, 4:8]),
                     reads=[("t", "bcolc")], writes=[("t", "bcol")], tag="sh.rot")
                P.dma("sync", zo_h[s].rearrange("p (f t) -> p f t", f=8)[:, :, c0:c1_],
                      rr(zblk, 8), reads=[("t", "zblk")], tag="z.st")
        if debug:
            P.dma("sync", dbg_h, dbg_sb[:], reads=[("t", "dbg")], tag="dbg.st")
        P.emit()
    return nc


class LNHelperB(LNHelper):
    def run_from_bf16(self, C, pstat, kps, prep, kpr, src_b, src_key):
        P, NB = self.P, self.NB
        xc = self.xc
        for f in range(8):
            P.op("tensor", lambda t, f=f: t.matmul(
                pstat[0:1, 0:NB], C["onescol_b"][:], src_b[:, f * NB:(f + 1) * NB],
                start=(f == 0), stop=(f == 7)),
                reads=[src_key, ("c", "onescol_b")], writes=[kps], tag="lnb.sum")
        P.op("scalar", lambda s: s.copy(self.stat1[:], pstat[0:1, 0:NB]),
             reads=[kps], writes=[self.kst1], tag="lnb.sumev")
        P.op("tensor", lambda t: t.matmul(
            prep[:, 0:NB], C["c1024"][:], self.stat1[:], start=True, stop=True),
            reads=[self.kst1, ("c", "c1024")], writes=[kpr], tag="lnb.murep")
        P.op("vector", lambda v: v.tensor_tensor(
            rr(xc, 8), src_b.ap().rearrange("p (f t) -> p f t", f=8),
            prep[:, 0:NB].unsqueeze(1).broadcast_to([128, 8, NB]),
            op=ALU.subtract),
            reads=[src_key, kpr], writes=[self.kxc], tag="lnb.xc")
        P.op("scalar", lambda s: s.activation(self.xb[:], xc[:], AF.Square),
             reads=[self.kxc], writes=[self.kxb], tag="lnb.sq")
        for f in range(8):
            P.op("tensor", lambda t, f=f: t.matmul(
                pstat[0:1, 0:NB], C["onescol_b"][:], self.xb[:, f * NB:(f + 1) * NB],
                start=(f == 0), stop=(f == 7)),
                reads=[self.kxb, ("c", "onescol_b")], writes=[kps], tag="lnb.ssq")
        P.op("scalar", lambda s: s.activation(
            self.stat1[:], pstat[0:1, 0:NB], AF.Sqrt, bias=1e-5, scale=1.0 / 1024.0),
            reads=[kps, ("c", "fc")], writes=[self.kst1], tag="lnb.std")
        P.op("vector", lambda v: v.reciprocal(self.stat2[:], self.stat1[:]),
             reads=[self.kst1], writes=[self.kst2], tag="lnb.rstd")
        P.op("tensor", lambda t: t.matmul(
            prep[:, 0:NB], C["c1"][:], self.stat2[:], start=True, stop=True),
            reads=[self.kst2, ("c", "c1")], writes=[kpr], tag="lnb.rrep")
        return prep[:, 0:NB].unsqueeze(1).broadcast_to([128, 8, NB])


def build_l0ffn_ffn(nc, T=4096, SEQS=2, final=False):
    es = contextlib.ExitStack()
    NB = 512
    nblk = T // NB
    with es:
        P = Prog(nc, es)
        z_h = nc.dram_tensor("z", [SEQS, 128, 8 * T], F32, kind="ExternalInput").ap()
        zo_h = nc.dram_tensor("zo", [SEQS, 128, 8 * T], F32, kind="ExternalOutput").ap()
        if final:
            x1_h = nc.dram_tensor("x1", [SEQS, 128, 8 * T], F32, kind="ExternalInput").ap()
            x2_h = nc.dram_tensor("x2", [SEQS, 128, 8 * T], F32, kind="ExternalOutput").ap()
        w1_h = nc.dram_tensor("w1", [128, 352 * 128], F32, kind="ExternalInput").ap()
        w2_h = nc.dram_tensor("w2", [128, 176 * 128], F32, kind="ExternalInput").ap()
        C = load_consts(P, es, nc, COMMON_CONSTS + [
            ("lng", (128, 8)), ("lnb_", (128, 8)),
            ("b1c", (128, 44)), ("b2c", (128, 8)),
        ])
        register_fc(nc, C)
        cast_c(P, es, nc, C, "onescol", (128, 1))

        stage = alloc(es, nc, "stage", [128, 1024], F32)
        w1b = alloc(es, nc, "w1b", [128, 352 * 128], BF16)
        w2b = alloc(es, nc, "w2b", [128, 176 * 128], BF16)
        for r in range(44):
            P.dma("sync", stage[:], w1_h[:, r * 1024:(r + 1) * 1024],
                  writes=[("stage",)], tag="w.l")
            P.op("vector", lambda v, r=r: v.tensor_copy(
                w1b[:, r * 1024:(r + 1) * 1024], stage[:]),
                reads=[("stage",)], writes=[("w", "w1")], tag="w.c")
        for r in range(22):
            P.dma("sync", stage[:], w2_h[:, r * 1024:(r + 1) * 1024],
                  writes=[("stage",)], tag="w.l2")
            P.op("vector", lambda v, r=r: v.tensor_copy(
                w2b[:, r * 1024:(r + 1) * 1024], stage[:]),
                reads=[("stage",)], writes=[("w", "w2")], tag="w.c2")

        zblk = alloc(es, nc, "zblk", [128, 8 * NB], F32)
        shz = alloc(es, nc, "shz", [128, 8 * NB], BF16)
        h2b = alloc(es, nc, "h2b", [128, 22 * NB], BF16)
        gel = alloc(es, nc, "gel", [128, NB], BF16)
        av = alloc(es, nc, "av", [128, NB], BF16)
        bcol = alloc(es, nc, "bcol", [128, 8], F32)
        if final:
            x1b = alloc(es, nc, "x1b", [128, 8 * NB], F32)
        ln1 = LNHelperB(P, es, nc, NB, tag="1", xc_dtype=BF16)
        lnx = ln1.xb

        pstat = psum1(es, nc, "pstat")
        prep = psum1(es, nc, "prep")
        pa0 = psum1(es, nc, "pa0")
        pa1 = psum1(es, nc, "pa1")
        kps, kpr = ("psum", "pstat"), ("psum", "prep")

        for s in range(SEQS):
            P.op("vector", lambda v: v.memset(bcol[:], 0.0),
                 writes=[("t", "bcol"), ("t", "bcolc")], tag="r.bz")
            for blk in range(nblk):
                c0, c1_ = blk * NB, (blk + 1) * NB
                P.dma("sync", rr(zblk, 8),
                      z_h[s].rearrange("p (f t) -> p f t", f=8)[:, :, c0:c1_],
                      writes=[("t", "zblk")], tag="z.ld")
                P.op("vector", lambda v: v.tensor_copy(
                    bcol[:, 4:8], r3(zblk, 8, NB, NB - 1, NB)[:, 4:8, 0]),
                    reads=[("t", "zblk")], writes=[("t", "bcolc")], tag="sh.sv")
                P.op("vector", lambda v: v.tensor_copy(
                    rr(shz, 8)[:, 0:4, :], rr(zblk, 8)[:, 0:4, :]),
                    reads=[("t", "zblk")], writes=[("t", "shz")], tag="sh.a")
                P.op("vector", lambda v: v.tensor_copy(
                    rr(shz, 8)[:, 4:8, 1:NB], rr(zblk, 8)[:, 4:8, 0:NB - 1]),
                    reads=[("t", "zblk")], writes=[("t", "shz")], tag="sh.b")
                P.op("vector", lambda v: v.tensor_copy(
                    rr(shz, 8)[:, 4:8, 0], bcol[:, 0:4]),
                    reads=[("t", "bcol")], writes=[("t", "shz")], tag="sh.c")
                rrep = ln1.run_from_bf16(C, pstat, kps, prep, kpr, shz, ("t", "shz"))
                ln1.to_bf16(rr(lnx, 8), [ln1.kxb], rrep, kpr)
                for m in range(8):
                    P.op("vector", lambda v, m=m: v.tensor_scalar(
                        lnx[:, m * NB:(m + 1) * NB], lnx[:, m * NB:(m + 1) * NB],
                        C["lng"][:, m:m + 1], C["lnb_"][:, m:m + 1],
                        op0=ALU.mult, op1=ALU.add),
                        reads=[ln1.kxb, ("c", "lng"), ("c", "lnb_")],
                        writes=[ln1.kxb], tag="f.gb")
                for j in range(22):
                    for which, mm_ in [("a", j), ("g", 22 + j)]:
                        pban, kpb = (pa0, ("psum", "pa0")) if which == "a" else (pa1, ("psum", "pa1"))
                        for k in range(8):
                            P.op("tensor", lambda t, mm_=mm_, k=k, pban=pban: t.matmul(
                                pban[:, 0:NB],
                                w1b[:, (k * 44 + mm_) * 128:(k * 44 + mm_ + 1) * 128],
                                lnx[:, k * NB:(k + 1) * NB],
                                start=(k == 0), stop=(k == 7)),
                                reads=[("w", "w1"), ln1.kxb],
                                writes=[kpb], tag="f.mm1")
                    P.op("scalar", lambda sc, j=j: sc.activation(
                        av[:], pa0[:, 0:NB], AF.Identity, bias=C["b1c"][:, j:j + 1]),
                        reads=[("psum", "pa0"), ("c", "b1c")], writes=[("t", "av")],
                        tag="f.a")
                    P.op("scalar", lambda sc, j=j: sc.activation(
                        gel[:], pa1[:, 0:NB], AF.Gelu, bias=C["b1c"][:, 22 + j:23 + j]),
                        reads=[("psum", "pa1"), ("c", "b1c")], writes=[("t", "gel")],
                        tag="f.g")
                    P.op("vector", lambda v, j=j: v.tensor_tensor(
                        h2b[:, j * NB:(j + 1) * NB], av[:], gel[:], op=ALU.mult),
                        reads=[("t", "av"), ("t", "gel")], writes=[("t", "h2")],
                        tag="f.h2")
                for m in range(8):
                    pban, kpb = (pa0, ("psum", "pa0")) if m % 2 == 0 else (pa1, ("psum", "pa1"))
                    for k in range(22):
                        P.op("tensor", lambda t, m=m, k=k, pban=pban: t.matmul(
                            pban[:, 0:NB],
                            w2b[:, (k * 8 + m) * 128:(k * 8 + m + 1) * 128],
                            h2b[:, k * NB:(k + 1) * NB],
                            start=(k == 0), stop=(k == 21)),
                            reads=[("w", "w2"), ("t", "h2")],
                            writes=[kpb], tag="f.mm2")
                    P.op("vector", lambda v, m=m, pban=pban: v.scalar_tensor_tensor(
                        zblk[:, m * NB:(m + 1) * NB], pban[:, 0:NB],
                        C["b2c"][:, m:m + 1], zblk[:, m * NB:(m + 1) * NB],
                        op0=ALU.add, op1=ALU.add),
                        reads=[("psum", "pa" + str(m % 2)), ("c", "b2c"), ("t", "zblk")],
                        writes=[("t", "zblk")], tag="f.ev")
                P.op("vector", lambda v: v.tensor_copy(bcol[:, 0:4], bcol[:, 4:8]),
                     reads=[("t", "bcolc")], writes=[("t", "bcol")], tag="sh.rot")
                P.dma("sync", zo_h[s].rearrange("p (f t) -> p f t", f=8)[:, :, c0:c1_],
                      rr(zblk, 8), reads=[("t", "zblk")], tag="z.st")
                if final:
                    P.dma("sync", rr(x1b, 8),
                          x1_h[s].rearrange("p (f t) -> p f t", f=8)[:, :, c0:c1_],
                          writes=[("t", "x1b")], tag="x1.ld")
                    rz = ln1.run(C, pstat, kps, prep, kpr, rr(zblk, 8), [("t", "zblk")])
                    ln1.add_into(rr(x1b, 8), [("t", "x1b")], rz, kpr)
                    P.dma("sync", x2_h[s].rearrange("p (f t) -> p f t", f=8)[:, :, c0:c1_],
                          rr(x1b, 8), reads=[("t", "x1b")], tag="x2.st")
        P.emit()
    return nc


def _xpos_tables(T):
    DIM_HEAD, WINDOW = 64, 256
    half = np.arange(0, DIM_HEAD, 2, dtype=np.float64)
    inv_freq = 1.0 / 10000.0 ** (half / DIM_HEAD)
    t = np.arange(T, dtype=np.float64)
    freqs = t[:, None] * inv_freq[None, :]
    freqs = np.concatenate([freqs, freqs], -1)
    base_scale = (half + 0.4 * DIM_HEAD) / (1.4 * DIM_HEAD)
    power = (t - T // 2) / (WINDOW // 2)
    scale = np.concatenate([base_scale[None, :] ** power[:, None]] * 2, -1)
    cos, sin = np.cos(freqs), np.sin(freqs)
    sgn = np.concatenate([-np.ones(32), np.ones(32)])
    csq = (cos * scale).T
    ssq = (sin * scale).T * sgn[:, None]
    inv = 1.0 / scale
    csk = (cos * inv).T
    ssk = (sin * inv).T * sgn[:, None]
    out = {}
    for nm2, a in [("csq", csq), ("ssq", ssq), ("csk", csk), ("ssk", ssk)]:
        out[nm2] = np.ascontiguousarray(np.tile(a, (2, 1)).astype(np.float32))
    return out


def _maskpack():
    SLOT_W = [128, 256, 384, 384, 256, 128]
    SLOT_O = [0, 128, 384, 768, 1152, 1408]
    SLOT_Q = [0, 0, 0, 128, 256, 384]
    oj = np.arange(128)[:, None]
    oi = np.arange(128)[None, :]
    tri_c = (oj <= oi).astype(np.float32)
    tri_a = (oi <= oj).astype(np.float32)
    ones = np.ones((128, 128), dtype=np.float32)
    mp = np.zeros((128, 1536), dtype=np.float32)
    for r_ in range(6):
        rel = r_ - 2
        qs_list = [qs for qs in range(4) if rel <= qs <= rel + 2]
        for idx, qs in enumerate(qs_list):
            d = qs - rel
            blkm = tri_c if d == 0 else (ones if d == 1 else tri_a)
            mp[:, SLOT_O[r_] + idx * 128: SLOT_O[r_] + (idx + 1) * 128] = blkm
    return mp


def l0ffn_attn_inputs(zraw, weights, li, T=4096):
    Wqkv = np.asarray(weights["L0_ffn_Wqkv"], dtype=np.float32)[li]
    Wo = np.asarray(weights["L0_ffn_Wo"], dtype=np.float32)[li]
    qs = np.asarray(weights["L0_ffn_q_scale"], dtype=np.float32)[li]
    ks = np.asarray(weights["L0_ffn_k_scale"], dtype=np.float32)[li]
    d = {
        "z": np.ascontiguousarray(zraw, dtype=np.float32),
        "wqkv": tile_w(Wqkv), "wo": tile_w(Wo),
        "qsc2": np.tile(qs, 2).reshape(128, 1),
        "ksc2": np.tile(ks, 2).reshape(128, 1),
        "qsrow": qs.reshape(1, 64), "ksrow": ks.reshape(1, 64),
        "blockhalf": np.concatenate([
            np.concatenate([np.ones((64, 1)), np.zeros((64, 1))], 1),
            np.concatenate([np.zeros((64, 1)), np.ones((64, 1))], 1)], 0).astype(np.float32),
        "selhalf": np.concatenate([
            np.concatenate([np.ones((1, 64)), np.zeros((1, 64))], 1),
            np.concatenate([np.zeros((1, 64)), np.ones((1, 64))], 1)], 0).astype(np.float32),
        "maskpack": _maskpack(),
    }
    d.update(_xpos_tables(T))
    d.update(common_const_arrays())
    return d


def l0ffn_ffn_inputs(zraw, weights, li, T=4096, final=False, x1raw=None):
    W1 = np.asarray(weights["L0_ffn_W1"], dtype=np.float32)[li]
    b1 = np.asarray(weights["L0_ffn_b1"], dtype=np.float32)[li]
    W2 = np.asarray(weights["L0_ffn_W2"], dtype=np.float32)[li]
    b2 = np.asarray(weights["L0_ffn_b2"], dtype=np.float32)[li]
    g = np.asarray(weights["L0_ffn_ln_g"], dtype=np.float32)[li]
    b = np.asarray(weights["L0_ffn_ln_b"], dtype=np.float32)[li]
    W1p = np.zeros((1024, 5632), dtype=np.float32)
    W1p[:, 0:2730] = W1[:, 0:2730]
    W1p[:, 2816:2816 + 2730] = W1[:, 2730:5460]
    b1p = np.zeros(5632, dtype=np.float32)
    b1p[0:2730] = b1[0:2730]
    b1p[2816:2816 + 2730] = b1[2730:5460]
    W2p = np.zeros((2816, 1024), dtype=np.float32)
    W2p[0:2730] = W2
    d = {
        "z": np.ascontiguousarray(zraw, dtype=np.float32),
        "w1": tile_w(W1p), "w2": tile_w(W2p),
        "b1c": np.ascontiguousarray(b1p.reshape(44, 128).T),
        "b2c": np.ascontiguousarray(b2.reshape(8, 128).T),
        "lng": np.ascontiguousarray(g.reshape(8, 128).T),
        "lnb_": np.ascontiguousarray(b.reshape(8, 128).T),
    }
    if final:
        d["x1"] = np.ascontiguousarray(x1raw, dtype=np.float32)
    d.update(common_const_arrays())
    return d


def build_l0fin(nc, T=4096, SEQS=2):
    es = contextlib.ExitStack()
    NB = 512
    nblk = T // NB
    with es:
        P = Prog(nc, es)
        z_h = nc.dram_tensor("z", [SEQS, 128, 8 * T], F32, kind="ExternalInput").ap()
        x1_h = nc.dram_tensor("x1", [SEQS, 128, 8 * T], F32, kind="ExternalInput").ap()
        x2_h = nc.dram_tensor("x2", [SEQS, 128, 8 * T], F32, kind="ExternalOutput").ap()
        C = load_consts(P, es, nc, COMMON_CONSTS)
        register_fc(nc, C)
        cast_c(P, es, nc, C, "onescol", (128, 1))
        zblk = alloc(es, nc, "zblk", [128, 8 * NB], F32)
        x1b = alloc(es, nc, "x1b", [128, 8 * NB], F32)
        ln1 = LNHelper(P, es, nc, NB, tag="1")
        pstat = psum1(es, nc, "pstat")
        prep = psum1(es, nc, "prep")
        kps, kpr = ("psum", "pstat"), ("psum", "prep")
        for s in range(SEQS):
            for blk in range(nblk):
                c0, c1_ = blk * NB, (blk + 1) * NB
                P.dma("sync", rr(zblk, 8),
                      z_h[s].rearrange("p (f t) -> p f t", f=8)[:, :, c0:c1_],
                      writes=[("t", "zblk")], tag="z.ld")
                P.dma("sync", rr(x1b, 8),
                      x1_h[s].rearrange("p (f t) -> p f t", f=8)[:, :, c0:c1_],
                      writes=[("t", "x1b")], tag="x1.ld")
                rz = ln1.run(C, pstat, kps, prep, kpr, rr(zblk, 8), [("t", "zblk")])
                ln1.add_into(rr(x1b, 8), [("t", "x1b")], rz, kpr)
                P.dma("sync", x2_h[s].rearrange("p (f t) -> p f t", f=8)[:, :, c0:c1_],
                      rr(x1b, 8), reads=[("t", "x1b")], tag="x2.st")
        P.emit()
    return nc


"""Fused single-launch megakernel builder."""

import contextlib


def alloc(es, nc, name, shape, dtype):
    return es.enter_context(nc.sbuf_tensor(uname(name), shape, dtype,
                                           unsafe_allow_realloc=True))




def psum1(es, nc, name, dtype=F32):
    n = 512 if dtype == F32 else 1024
    return es.enter_context(nc.psum_tensor(uname(name), [128, n], dtype,
                                           unsafe_allow_realloc=True))




def load_consts_f(P, es, nc, specs, arrays_key="c"):
    C = {}
    for name, shape in specs:
        hbm = nc.dram_tensor(name, list(shape), F32, kind="ExternalInput").ap()
        sb = alloc(es, nc, "c_" + name, list(shape), F32)
        P.dma("sync", sb[:], hbm, writes=[("c", name)], tag=f"load.{name}")
        C[name] = sb
    return C


def cast_cf(P, es, nc, C, name, shape):
    sb = alloc(es, nc, "cb_" + name, list(shape), BF16)
    P.op("vector", lambda v: v.tensor_copy(sb[:], C[name][:]),
         reads=[("c", name)], writes=[("c", name + "_b")], tag=f"cast.{name}")
    C[name + "_b"] = sb
    return sb


def build_fused(nc, T=4096, SEQS=2, phases="12345", loop_n=6):
    es = contextlib.ExitStack()
    NB = 512
    nblk = T // NB
    CH = 128
    nch = NB // CH
    NH = 16
    with es:
        P = Prog(nc, es, loop_n=loop_n)
        x_h = nc.dram_tensor("x", [SEQS, T, 1024], F32, kind="ExternalInput").ap()
        out_h = nc.dram_tensor("out", [SEQS, T, 1024], F32, kind="ExternalOutput").ap()
        z_h = nc.dram_tensor("zscr", [SEQS, 128, 8 * T], F32).ap()
        x1_h = nc.dram_tensor("x1scr", [SEQS, 128, 8 * T], F32).ap()
        x2_h = nc.dram_tensor("x2scr", [SEQS, 128, 8 * T], F32).ap()

        C0 = load_consts_f(P, es, nc, COMMON_CONSTS)
        register_fc(nc, C0)
        cast_cf(P, es, nc, C0, "onescol", (128, 1))
        cast_cf(P, es, nc, C0, "idf", (128, 128))

        if "1" in phases:
            es1 = contextlib.ExitStack()
            with es1:
                emit_l0attn(P, es1, nc, C0, x_h, x1_h, z_h, T, SEQS, NB, CH, nch, nblk)
            P.barrier("bar.p1")

        if "2" in phases:
            P.begin_loop()
            es2a = contextlib.ExitStack()
            if "a" not in phases or "2a" in phases:
                with es2a:
                    emit_l0fa(P, es2a, nc, C0, z_h, T, SEQS, NB, NH, nblk)
                P.barrier("bar.p2a")
            es2b = contextlib.ExitStack()
            if "b" not in phases or "2b" in phases:
                with es2b:
                    emit_l0ff(P, es2b, nc, C0, z_h, T, SEQS, NB, nblk)
            P.end_loop()

        if "3" in phases:
            es3 = contextlib.ExitStack()
            with es3:
                emit_l0fin(P, es3, nc, C0, z_h, x1_h, x2_h, T, SEQS, NB, nblk)
            P.barrier("bar.p3")

        if "4" in phases:
            es4 = contextlib.ExitStack()
            with es4:
                emit_l1attn(P, es4, nc, C0, x2_h, x1_h, T, SEQS, NB, CH, nch, NH, nblk)
            P.barrier("bar.p4")

        if "5" in phases:
            es5 = contextlib.ExitStack()
            with es5:
                emit_l1ffn(P, es5, nc, C0, x1_h, out_h, T, SEQS, NB, nblk)
        else:
            P.dma("sync", out_h[0, 0:128, 0:1024],
                  x1_h[0][:, 0:1024], tag="dbg.out")

        P.emit()
    return nc




def fused_inputs(x_shard, w, T=4096):
    d = {}
    a = l0attn_inputs(x_shard, w, T=T)
    for k in ["x", "basisT", "qcT", "kcT", "vcT", "ocT", "dlog", "oscale",
              "ET", "M01T", "powq2", "powk2"]:
        d[k] = a[k]
    d.update(common_const_arrays())
    Wqkv = np.asarray(w["L0_ffn_Wqkv"], dtype=np.float32)
    Wo = np.asarray(w["L0_ffn_Wo"], dtype=np.float32)
    d["wqkv6"] = np.concatenate([tile_w(Wqkv[i]) for i in range(6)], axis=1)
    d["wo6"] = np.concatenate([tile_w(Wo[i]) for i in range(6)], axis=1)
    qs = np.asarray(w["L0_ffn_q_scale"], dtype=np.float32)
    ks = np.asarray(w["L0_ffn_k_scale"], dtype=np.float32)
    qsc = np.stack([np.tile(qs[i], 2) for i in range(6)])
    ksc = np.stack([np.tile(ks[i], 2) for i in range(6)])
    falc = np.zeros((6, 128, 8), dtype=np.float32)
    falc[:, :, 0:4] = qsc[:, :, None]
    falc[:, :, 4:8] = ksc[:, :, None]
    d["falc6"] = falc.reshape(6 * 128, 8)
    d["farow6"] = np.concatenate([qs, ks], axis=1).astype(np.float32)
    fa = l0ffn_attn_inputs(x_shard[:, :1, :1] * 0, w, 0, T=T)
    for k in ["blockhalf", "selhalf", "maskpack", "csq", "ssq", "csk", "ssk"]:
        d[k] = fa[k]
    W1s, W2s, b1s, b2s, lngs, lnbs = [], [], [], [], [], []
    for li in range(6):
        fi = l0ffn_ffn_inputs(x_shard[:, :1, :1] * 0, w, li, T=T)
        W1s.append(fi["w1"]); W2s.append(fi["w2"])
        b1s.append(fi["b1c"]); b2s.append(fi["b2c"])
        lngs.append(fi["lng"]); lnbs.append(fi["lnb_"])
    d["w16"] = np.concatenate(W1s, axis=1)
    d["w26"] = np.concatenate(W2s, axis=1)
    fflc = np.zeros((6, 128, 68), dtype=np.float32)
    for li in range(6):
        fflc[li, :, 0:8] = lngs[li]
        fflc[li, :, 8:16] = lnbs[li]
        fflc[li, :, 16:60] = b1s[li]
        fflc[li, :, 60:68] = b2s[li]
    d["fflc6"] = fflc.reshape(6 * 128, 68)
    la = l1attn_inputs(np.zeros((1, 1, 1), np.float32), w, T=T)
    for k in ["wq", "wk", "wv", "wo", "bq", "bk", "bv", "bo", "cosr", "sinr", "MINC"]:
        d[k] = la[k]
    lf = l1ffn_inputs(np.zeros((1, 1, 1), np.float32), w, T=T)
    for k in ["cw1", "sigraw", "knots", "temp", "osc", "cw2", "cb1", "cb2",
              "tabsp", "dirs", "gamma", "beta", "grp8", "iota8x4", "iotaMB", "iota64"]:
        d[k] = lf[k]
    return d




def emit_l0attn(P, es, nc, C0, x_h, x1_h, z0_h, T, SEQS, NB, CH, nch, nblk):
    if True:
        C = dict(C0)
        C.update(load_consts_f(P, es, nc, [
            ("dlog", (1, 1)), ("oscale", (1, 1)),
            ("ET", (128, 128)), ("M01T", (128, 128)),
            ("powq2", (1, 256)), ("powk2", (1, 256)),
        ]))
        stage = alloc(es, nc, "stage", [128, 1024], F32)
        for nm, sh in [("basisT", (128, 1024)), ("qcT", (128, 256)),
                       ("kcT", (128, 256)), ("vcT", (128, 256)),
                       ("ocT", (128, 256))]:
            load_bf16_via_stage(P, es, nc, C, stage, nm, sh)

        xt = alloc(es, nc, "xt", [128, 8 * T], F32)
        xtm = [stage, stage]
        qw = alloc(es, nc, "qw", [128, 8 * 256], BF16)
        kw = alloc(es, nc, "kw", [128, 8 * 256], BF16)
        vw = alloc(es, nc, "vw", [128, 8 * 256], BF16)
        owT = alloc(es, nc, "owT", [128, 2 * 1024], BF16)
        DTt = alloc(es, nc, "DTt", [128, 128], F32)
        Etmp = alloc(es, nc, "Etmp", [128, 128], F32)
        lamq = alloc(es, nc, "lamq", [128, 256], BF16)
        lamk = alloc(es, nc, "lamk", [128, 256], BF16)
        lamc = alloc(es, nc, "lamc", [128, 1], F32)
        osbc = alloc(es, nc, "osbc", [128, 1], F32)
        srow = alloc(es, nc, "srow", [1, 256], F32)
        srow2 = alloc(es, nc, "srow2", [1, 256], F32)
        lam_t = alloc(es, nc, "lam_t", [1, 4], F32)
        lnl_bc = alloc(es, nc, "lnl_bc", [128, 4], F32)
        qT = alloc(es, nc, "qT", [128, 2 * NB], BF16)
        kT = alloc(es, nc, "kT", [128, 2 * NB], BF16)
        vT = alloc(es, nc, "vT", [128, 2 * NB], BF16)
        qs = alloc(es, nc, "qs", [128, 256], BF16)
        ks = alloc(es, nc, "ks", [128, 256], BF16)
        ab = alloc(es, nc, "ab", [128, 128], BF16)
        vtm = alloc(es, nc, "vtm", [128, 256], BF16)
        ktm = alloc(es, nc, "ktm", [128, 256], BF16)
        state = alloc(es, nc, "state", [128, 512], F32)
        stateb = alloc(es, nc, "stateb", [128, 512], BF16)
        oT = alloc(es, nc, "oT", [128, 2 * NB], BF16)
        ln1 = LNHelper(P, es, nc, NB, tag="1")

        pstat = psum1(es, nc, "pstat")
        prep = psum1(es, nc, "prep")
        pa0 = psum1(es, nc, "pa0")
        pa1 = psum1(es, nc, "pa1")
        pbt = es.enter_context(nc.psum_tensor(uname("pbt"), [128, 1024], F32, unsafe_allow_realloc=True))
        pc = psum1(es, nc, "pc")
        pdb = es.enter_context(nc.psum_tensor(uname("pdb"), [128, 1024], BF16, unsafe_allow_realloc=True))
        kps, kpr = ("psum", "pstat"), ("psum", "prep")

        for nm, dst, kk in [("qcT_b", qw, "q"), ("kcT_b", kw, "k"), ("vcT_b", vw, "v")]:
            for dk in range(8):
                P.op("tensor", lambda t, dk=dk, nm=nm: t.matmul(
                    pa0[:, 0:256], C["basisT_b"][:, dk * 128:(dk + 1) * 128],
                    C[nm][:], start=True, stop=True),
                    reads=[("c", "basisT_b"), ("c", nm)],
                    writes=[("psum", "pa0")], tag="w.mm")
                P.op("scalar", lambda s, dk=dk, dst=dst: s.copy(
                    dst[:, dk * 256:(dk + 1) * 256], pa0[:, 0:256]),
                    reads=[("psum", "pa0")], writes=[("w", kk)], tag="w.evac")
        for ck in range(2):
            for half in range(2):
                P.op("tensor", lambda t, ck=ck, half=half: t.matmul(
                    pa1[:, 0:512], C["ocT_b"][:, ck * 128:(ck + 1) * 128],
                    C["basisT_b"][:, half * 512:(half + 1) * 512],
                    start=True, stop=True),
                    reads=[("c", "basisT_b"), ("c", "ocT_b")],
                    writes=[("psum", "pa1")], tag="w.omm")
                P.op("scalar", lambda s, ck=ck, half=half: s.copy(
                    owT[:, ck * 1024 + half * 512: ck * 1024 + (half + 1) * 512],
                    pa1[:, 0:512]),
                    reads=[("psum", "pa1")], writes=[("w", "o")], tag="w.oevac")

        P.op("vector", lambda v: v.memset(lam_t[:], 0.0),
             writes=[("t", "lam")], tag="d.z")
        P.op("scalar", lambda s: s.activation(lam_t[:, 0:1], C["dlog"][:], AF.Sigmoid),
             reads=[("c", "dlog"), ("t", "lam")], writes=[("t", "lam")], tag="d.sig")
        P.op("scalar", lambda s: s.activation(lam_t[:, 1:2], lam_t[:, 0:1], AF.Ln),
             reads=[("t", "lam")], writes=[("t", "lam")], tag="d.ln")
        P.op("scalar", lambda s: s.activation(lam_t[:, 2:3], lam_t[:, 1:2], AF.Exp,
                                              scale=float(CH)),
             reads=[("t", "lam")], writes=[("t", "lam")], tag="d.lamc")
        P.op("tensor", lambda t: t.matmul(prep[:, 0:4], C["c1"][:], lam_t[:],
                                          start=True, stop=True),
             reads=[("t", "lam"), ("c", "c1")], writes=[kpr], tag="d.rep")
        P.op("scalar", lambda s: s.copy(lnl_bc[:], prep[:, 0:4]),
             reads=[kpr], writes=[("t", "lnl_bc")], tag="d.repev")
        P.op("vector", lambda v: v.tensor_copy(lamc[:], lnl_bc[:, 2:3]),
             reads=[("t", "lnl_bc")], writes=[("t", "lamcbc")], tag="d.lamcbc")
        P.op("tensor", lambda t: t.matmul(prep[:, 0:1], C["c1"][:], C["oscale"][:],
                                          start=True, stop=True),
             reads=[("c", "oscale"), ("c", "c1")], writes=[kpr], tag="d.osrep")
        P.op("scalar", lambda s: s.copy(osbc[:], prep[:, 0:1]),
             reads=[kpr], writes=[("t", "osbc")], tag="d.osev")
        P.op("vector", lambda v: v.tensor_scalar(
            Etmp[:], C["ET"][:], lnl_bc[:, 1:2], None, op0=ALU.mult),
            reads=[("c", "ET"), ("t", "lnl_bc")], writes=[("t", "Etmp")], tag="d.emul")
        P.op("scalar", lambda s: s.activation(Etmp[:], Etmp[:], AF.Exp),
             reads=[("t", "Etmp")], writes=[("t", "Etmp")], tag="d.eexp")
        P.op("vector", lambda v: v.tensor_tensor(DTt[:], Etmp[:], C["M01T"][:],
                                                 op=ALU.mult),
             reads=[("t", "Etmp"), ("c", "M01T")], writes=[("t", "DT")], tag="d.dmask")
        for dstt, powname, kk in [(lamq, "powq2", "lamq"), (lamk, "powk2", "lamk")]:
            P.op("vector", lambda v, powname=powname: v.tensor_scalar(
                srow[:], C[powname][:], lnl_bc[0:1, 1:2], None, op0=ALU.mult),
                reads=[("c", powname), ("t", "lnl_bc")], writes=[("t", "srow")],
                tag="d.pmul")
            P.op("scalar", lambda s: s.activation(srow2[:], srow[:], AF.Exp),
                 reads=[("t", "srow")], writes=[("t", "srow2")], tag="d.pexp")
            P.op("tensor", lambda t: t.matmul(prep[:, 0:256], C["c1"][:], srow2[:],
                                              start=True, stop=True),
                 reads=[("t", "srow2"), ("c", "c1")], writes=[kpr], tag="d.prep")
            P.op("scalar", lambda s, dstt=dstt, kk=kk: s.copy(dstt[:], prep[:, 0:256]),
                 reads=[kpr], writes=[("t", kk)], tag="d.pev")

        for s in range(SEQS):
            emit_entry_transpose(P, C, x_h, s, xt, T, xtm, pbt, NB)
            P.op("vector", lambda v: v.memset(state[:], 0.0),
                 writes=[("t", "state")], tag="st.zero")
            P.op("vector", lambda v: v.memset(stateb[:], 0.0),
                 writes=[("t", "stateb")], tag="st.zero16")

            for blk in reversed(range(nblk)):
                c0, c1_ = blk * NB, (blk + 1) * NB
                kxt = ("xt", blk)
                rrep = ln1.run(C, pstat, kps, prep, kpr, r3(xt, 8, T, c0, c1_), [kxt])
                lnb = ln1.xb
                ln1.to_bf16(rr(lnb, 8), [ln1.kxb], rrep, kpr)
                for wname, wt, dstt, kk in [("q", qw, qT, "qT"), ("k", kw, kT, "kT"),
                                            ("v", vw, vT, "vT")]:
                    for cm in range(2):
                        pban = pa0 if cm == 0 else pa1
                        for dk in range(8):
                            P.op("tensor", lambda t, wt=wt, cm=cm, dk=dk, pban=pban: t.matmul(
                                pban[:, 0:NB],
                                wt[:, dk * 256 + cm * 128: dk * 256 + cm * 128 + 128],
                                lnb[:, dk * NB:(dk + 1) * NB],
                                start=(dk == 0), stop=(dk == 7)),
                                reads=[("w", wname), ln1.kxb],
                                writes=[("psum", "pa" + str(cm))], tag="p.mm")
                        P.op("scalar", lambda sc, dstt=dstt, cm=cm, pban=pban: sc.copy(
                            dstt[:, cm * NB:(cm + 1) * NB], pban[:, 0:NB]),
                            reads=[("psum", "pa" + str(cm))],
                            writes=[("t", kk)], tag="p.evac")
                for ch in reversed(range(nch)):
                    t0 = ch * CH
                    P.op("vector", lambda v, t0=t0: v.tensor_tensor(
                        rr(qs, 2), rr(qT, 2)[:, :, t0:t0 + CH], rr(lamq, 2),
                        op=ALU.mult),
                        reads=[("t", "qT"), ("t", "lamq")], writes=[("t", "qs")],
                        tag="r.qs")
                    P.op("vector", lambda v, t0=t0: v.tensor_tensor(
                        rr(ks, 2), rr(kT, 2)[:, :, t0:t0 + CH], rr(lamk, 2),
                        op=ALU.mult),
                        reads=[("t", "kT"), ("t", "lamk")], writes=[("t", "ks")],
                        tag="r.ks")
                    for ck in range(2):
                        P.op("tensor", lambda t, ck=ck, t0=t0: t.matmul(
                            pc[:, 0:CH],
                            kT[:, ck * NB + t0: ck * NB + t0 + CH],
                            qT[:, ck * NB + t0: ck * NB + t0 + CH],
                            start=(ck == 0), stop=(ck == 1)),
                            reads=[("t", "kT"), ("t", "qT")],
                            writes=[("psum", "pc")], tag="r.amm")
                    P.op("vector", lambda v: v.tensor_tensor(
                        ab[:], pc[:, 0:CH], DTt[:], op=ALU.mult),
                        reads=[("psum", "pc"), ("t", "DT")], writes=[("t", "ab")],
                        tag="r.amask")
                    for ck in range(2):
                        P.op("tensor", lambda t, ck=ck, t0=t0: t.transpose(
                            pdb[:, ck * 128:(ck + 1) * 128],
                            vT[:, ck * NB + t0: ck * NB + t0 + CH],
                            C["idf_b"][:]),
                            reads=[("t", "vT"), ("c", "idf_b")],
                            writes=[("psum", "pdb")], tag="r.vtp")
                    P.op("vector", lambda v: v.tensor_copy(vtm[:], pdb[:, 0:256]),
                         reads=[("psum", "pdb")], writes=[("t", "vtm")], tag="r.vtev")
                    for ck in range(2):
                        P.op("tensor", lambda t, ck=ck: t.transpose(
                            pdb[:, 256 + ck * 128: 256 + (ck + 1) * 128],
                            ks[:, ck * 128:(ck + 1) * 128],
                            C["idf_b"][:]),
                            reads=[("t", "ks"), ("c", "idf_b")],
                            writes=[("psum", "pdb")], tag="r.ktp")
                    P.op("vector", lambda v: v.tensor_copy(ktm[:], pdb[:, 256:512]),
                         reads=[("psum", "pdb")], writes=[("t", "ktm")], tag="r.ktev")
                    for m in range(2):
                        pban = pa0 if m == 0 else pa1
                        P.op("tensor", lambda t, m=m, pban=pban: t.matmul(
                            pban[:, 0:CH],
                            vtm[:, m * 128:(m + 1) * 128], ab[:],
                            start=True, stop=False),
                            reads=[("t", "vtm"), ("t", "ab")],
                            writes=[("psum", "pa" + str(m))], tag="r.omm")
                        for ck in range(2):
                            P.op("tensor", lambda t, m=m, ck=ck, pban=pban: t.matmul(
                                pban[:, 0:CH],
                                stateb[:, ck * 256 + m * 128: ck * 256 + (m + 1) * 128],
                                qs[:, ck * 128:(ck + 1) * 128],
                                start=False, stop=(ck == 1)),
                                reads=[("t", "stateb"), ("t", "qs")],
                                writes=[("psum", "pa" + str(m))], tag="r.cmm")
                    for m in range(2):
                        pban = pa0 if m == 0 else pa1
                        P.op("scalar", lambda sc, t0=t0, m=m, pban=pban: sc.copy(
                            oT[:, m * NB + t0: m * NB + t0 + CH], pban[:, 0:CH]),
                            reads=[("psum", "pa" + str(m))],
                            writes=[("t", "oT")], tag="r.oev")
                    for ck in range(2):
                        P.op("tensor", lambda t, ck=ck: t.matmul(
                            pc[:, ck * 256:(ck + 1) * 256],
                            ktm[:, ck * 128:(ck + 1) * 128], vtm[:],
                            start=True, stop=True),
                            reads=[("t", "ktm"), ("t", "vtm")],
                            writes=[("psum", "pc")], tag="r.smm")
                    P.op("vector", lambda v: v.scalar_tensor_tensor(
                        state[:], state[:], lamc[:], pc[:],
                        op0=ALU.mult, op1=ALU.add),
                        reads=[("t", "state"), ("t", "lamcbc"), ("psum", "pc")],
                        writes=[("t", "state")], tag="r.supd")
                    P.op("vector", lambda v: v.tensor_copy(stateb[:], state[:]),
                         reads=[("t", "state")], writes=[("t", "stateb")], tag="r.s16")
                for dm in range(8):
                    pban = pa0 if dm % 2 == 0 else pa1
                    for ck in range(2):
                        P.op("tensor", lambda t, dm=dm, ck=ck, pban=pban: t.matmul(
                            pban[:, 0:NB],
                            owT[:, ck * 1024 + dm * 128: ck * 1024 + (dm + 1) * 128],
                            oT[:, ck * NB:(ck + 1) * NB],
                            start=(ck == 0), stop=(ck == 1)),
                            reads=[("w", "o"), ("t", "oT")],
                            writes=[("psum", "pa" + str(dm % 2))], tag="o.mm")
                    P.op("vector", lambda v, dm=dm, pban=pban: v.tensor_scalar(
                        ln1.xc[:, dm * NB:(dm + 1) * NB], pban[:, 0:NB],
                        osbc[:], None, op0=ALU.mult),
                        reads=[("psum", "pa" + str(dm % 2)), ("t", "osbc")],
                        writes=[ln1.kxc], tag="o.ev")
                ryrep = ln1.run_inplace(C, pstat, kps, prep, kpr)
                ln1.add_into(r3(xt, 8, T, c0, c1_), [kxt], ryrep, kpr)
                P.dma("sync", x1_h[s].rearrange("p (f t) -> p f t", f=8)[:, :, c0:c1_],
                      r3(xt, 8, T, c0, c1_), reads=[kxt], tag="x1.st")
                rzrep = ln1.run(C, pstat, kps, prep, kpr, r3(xt, 8, T, c0, c1_), [kxt])
                ln1.scale_inplace(rzrep, kpr)
                P.dma("sync", z0_h[s].rearrange("p (f t) -> p f t", f=8)[:, :, c0:c1_],
                      rr(ln1.xc, 8), reads=[ln1.kxc], tag="z0.st")



def emit_l0fa(P, es, nc, C0, z_h, T, SEQS, NB, NH, nblk):
    zo_h = z_h
    if True:
        wqkv_h = nc.dram_tensor("wqkv6", [128, 6 * 192 * 128], F32, kind="ExternalInput").ap().tensor
        wo_h = nc.dram_tensor("wo6", [128, 6 * 64 * 128], F32, kind="ExternalInput").ap().tensor
        rot_h = {nm2: nc.dram_tensor(nm2, [128, T], F32, kind="ExternalInput").ap()
                 for nm2 in ["csq", "ssq", "csk", "ssk"]}
        falc_h = nc.dram_tensor("falc6", [6 * 128, 8], F32, kind="ExternalInput").ap().tensor
        farow_h = nc.dram_tensor("farow6", [6, 128], F32, kind="ExternalInput").ap().tensor
        C = dict(C0)
        C.update(load_consts_f(P, es, nc, [
            ("blockhalf", (128, 2)), ("selhalf", (2, 128)),
            ("maskpack", (128, 1536)),
        ]))
        cast_cf(P, es, nc, C, "blockhalf", (128, 2))
        cast_cf(P, es, nc, C, "maskpack", (128, 1536))
        falc = alloc(es, nc, "falc", [128, 8], F32)
        farow = alloc(es, nc, "farow", [1, 128], F32)
        C["qsc2"] = falc.ap()[:, 0:4]
        C["ksc2"] = falc.ap()[:, 4:8]
        C["qsrow"] = farow.ap()[0:1, 0:64]
        C["ksrow"] = farow.ap()[0:1, 64:128]

        stage = alloc(es, nc, "stage", [128, 1024], F32)
        wqkvb = alloc(es, nc, "wqkvb", [128, 192 * 128], BF16)
        wob = alloc(es, nc, "wob", [128, 64 * 128], BF16)
        wqkv_st = nc.dram_tensor(uname("wqkv_st"), [128, 192 * 128], F32).ap()
        wo_st = nc.dram_tensor(uname("wo_st"), [128, 64 * 128], F32).ap()
        dyn_dma(P, "sync", wqkv_st[:, :], wqkv_h, 0, 192 * 128,
                [[6 * 192 * 128, 128], [1, 192 * 128]],
                writes=[("st", "qkv")], tag="w.st")
        dyn_dma(P, "sync", wo_st[:, :], wo_h, 0, 64 * 128,
                [[6 * 64 * 128, 128], [1, 64 * 128]],
                writes=[("st", "o")], tag="w.st2")
        dyn_dma(P, "sync", falc[:], falc_h, 0, 8 * 128,
                [[8, 128], [1, 8]], writes=[("c", "qsc2"), ("c", "ksc2")], tag="lc.f")
        dyn_dma(P, "sync", farow[:], farow_h, 0, 128,
                [[128, 1], [1, 128]], writes=[("c", "qsrow"), ("c", "ksrow")], tag="lc.r")
        for r in range(24):
            P.dma("sync", stage[:], wqkv_st[:, r * 1024:(r + 1) * 1024],
                  reads=[("st", "qkv")], writes=[("stage",)], tag="w.l")
            P.op("vector", lambda v, r=r: v.tensor_copy(
                wqkvb[:, r * 1024:(r + 1) * 1024], stage[:]),
                reads=[("stage",)], writes=[("w", "qkv")], tag="w.c")
        for r in range(8):
            P.dma("sync", stage[:], wo_st[:, r * 1024:(r + 1) * 1024],
                  reads=[("st", "o")], writes=[("stage",)], tag="w.l2")
            P.op("vector", lambda v, r=r: v.tensor_copy(
                wob[:, r * 1024:(r + 1) * 1024], stage[:]),
                reads=[("stage",)], writes=[("w", "o")], tag="w.c2")
        rott = {}
        for nm2 in ["csq", "ssq", "csk", "ssk"]:
            rt = alloc(es, nc, "t_" + nm2, [128, T], BF16)
            rott[nm2] = rt
            for r0 in range(0, T, 1024):
                wdt = min(1024, T - r0)
                P.dma("sync", stage[:, 0:wdt], rot_h[nm2][:, r0:r0 + wdt],
                      writes=[("stage",)], tag="rt.l")
                P.op("vector", lambda v, rt=rt, r0=r0, wdt=wdt: v.tensor_copy(
                    rt[:, r0:r0 + wdt], stage[:, 0:wdt]),
                    reads=[("stage",)], writes=[("t", "rott")], tag="rt.c")
        sc1 = alloc(es, nc, "sc1", [1, 2], F32)
        negc = alloc(es, nc, "negc", [128, 1], F32)
        P.op("vector", lambda v: v.tensor_reduce(
            sc1[0:1, 0:1], C["qsrow"][:], axis=AX.X, op=ALU.max,
            apply_absolute_value=True),
            reads=[("c", "qsrow")], writes=[("t", "sc1")], tag="c.mq")
        P.op("vector", lambda v: v.tensor_reduce(
            sc1[0:1, 1:2], C["ksrow"][:], axis=AX.X, op=ALU.max,
            apply_absolute_value=True),
            reads=[("c", "ksrow")], writes=[("t", "sc1")], tag="c.mk")
        P.op("vector", lambda v: v.tensor_tensor(
            sc1[0:1, 0:1], sc1[0:1, 0:1], sc1[0:1, 1:2], op=ALU.mult),
            reads=[("t", "sc1")], writes=[("t", "sc1")], tag="c.mm")
        P.op("vector", lambda v: v.tensor_scalar(
            sc1[0:1, 0:1], sc1[0:1, 0:1], -8.0 * 3.47, None, op0=ALU.mult),
            reads=[("t", "sc1")], writes=[("t", "sc1")], tag="c.s")
        prep = psum1(es, nc, "prep")
        kpr = ("psum", "prep")
        P.op("tensor", lambda t: t.matmul(prep[:, 0:1], C["c1"][:], sc1[0:1, 0:1],
                                          start=True, stop=True),
             reads=[("t", "sc1"), ("c", "c1")], writes=[kpr], tag="c.rep")
        P.op("scalar", lambda s: s.copy(negc[:], prep[:, 0:1]),
             reads=[kpr], writes=[("t", "negc")], tag="c.repev")

        zblk = alloc(es, nc, "zblk", [128, 8 * NB], F32)
        shz = alloc(es, nc, "shz", [128, 8 * NB], BF16)
        qkvb = alloc(es, nc, "qkvb", [128, 24 * NB], BF16)
        sqb = alloc(es, nc, "sqb", [128, NB], BF16)
        nrm = alloc(es, nc, "nrm", [2, NB], F32)
        shq = alloc(es, nc, "shq", [128, NB], BF16)
        tq1 = alloc(es, nc, "tq1", [128, NB], BF16)
        kring = alloc(es, nc, "kring", [128, 8 * 6 * 128], BF16)
        vring = alloc(es, nc, "vring", [128, 6 * NH * 65], BF16)
        abp = alloc(es, nc, "abp", [128, 1536], BF16)
        dstripe = alloc(es, nc, "dstripe", [1, NB], F32)
        rrsb = alloc(es, nc, "rrsb", [64, NB], F32)
        aout = alloc(es, nc, "aout", [128, 8 * NB], BF16)
        bcol = alloc(es, nc, "bcol", [128, 8], F32)

        pstat = psum1(es, nc, "pstat")
        pa0 = psum1(es, nc, "pa0")
        pa1 = psum1(es, nc, "pa1")
        pS = es.enter_context(nc.psum_tensor(uname("pS"), [128, 1536], F32, unsafe_allow_realloc=True))
        pdb = es.enter_context(nc.psum_tensor(uname("pdb"), [128, 1024], BF16, unsafe_allow_realloc=True))
        kps = ("psum", "pstat")

        SLOT_W = [128, 256, 384, 384, 256, 128]
        SLOT_O = [0, 128, 384, 768, 1152, 1408]
        SLOT_Q = [0, 0, 0, 128, 256, 384]
        SPIECES = [(0, 0, 0, 128), (1, 128, 0, 256), (2, 384, 0, 128),
                   (2, 512, 128, 256), (3, 768, 128, 256), (3, 1024, 384, 128),
                   (4, 1152, 256, 256), (5, 1408, 384, 128)]

        for s in range(SEQS):
            P.op("vector", lambda v: v.memset(vring[:], 1.0),
                 writes=[("t", "vring")], tag="r.vz")
            P.op("vector", lambda v: v.memset(kring[:], 0.0),
                 writes=[("t", "kring")], tag="r.kz")
            P.op("vector", lambda v: v.memset(bcol[:], 0.0),
                 writes=[("t", "bcol"), ("t", "bcolc")], tag="r.bz")
            for blk in range(nblk):
                c0, c1_ = blk * NB, (blk + 1) * NB
                P.dma("sync", rr(zblk, 8),
                      z_h[s].rearrange("p (f t) -> p f t", f=8)[:, :, c0:c1_],
                      writes=[("t", "zblk")], tag="z.ld")
                P.op("vector", lambda v: v.tensor_copy(
                    bcol[:, 4:8], r3(zblk, 8, NB, NB - 1, NB)[:, 4:8, 0]),
                    reads=[("t", "zblk")], writes=[("t", "bcolc")], tag="sh.sv")
                P.op("vector", lambda v: v.tensor_copy(
                    rr(shz, 8)[:, 0:4, :], rr(zblk, 8)[:, 0:4, :]),
                    reads=[("t", "zblk")], writes=[("t", "shz")], tag="sh.a")
                P.op("vector", lambda v: v.tensor_copy(
                    rr(shz, 8)[:, 4:8, 1:NB], rr(zblk, 8)[:, 4:8, 0:NB - 1]),
                    reads=[("t", "zblk")], writes=[("t", "shz")], tag="sh.b")
                P.op("vector", lambda v: v.tensor_copy(
                    rr(shz, 8)[:, 4:8, 0], bcol[:, 0:4]),
                    reads=[("t", "bcol")], writes=[("t", "shz")], tag="sh.c")
                for m in range(24):
                    pban, kpb = (pa0, ("psum", "pa0")) if m % 2 == 0 else (pa1, ("psum", "pa1"))
                    for k in range(8):
                        P.op("tensor", lambda t, m=m, k=k, pban=pban: t.matmul(
                            pban[:, 0:NB],
                            wqkvb[:, (k * 24 + m) * 128:(k * 24 + m + 1) * 128],
                            shz[:, k * NB:(k + 1) * NB],
                            start=(k == 0), stop=(k == 7)),
                            reads=[("w", "qkv"), ("t", "shz")],
                            writes=[kpb], tag="p.mm")
                    P.op("scalar", lambda sc, m=m, pban=pban: sc.copy(
                        qkvb[:, m * NB:(m + 1) * NB], pban[:, 0:NB]),
                        reads=[("psum", "pa" + str(m % 2))],
                        writes=[("t", "qkv")], tag="p.ev")
                for part, scol, cst, sst in [("q", "qsc2", "csq", "ssq"),
                                             ("k", "ksc2", "csk", "ssk")]:
                    for f in range(8):
                        m = f if part == "q" else 8 + f
                        dbgk = False
                        if dbgk:
                            P.op("vector", lambda v, m=m: v.tensor_copy(
                                dbg_sb[0:64, 0:128], qkvb[0:64, m * NB + 384: m * NB + 512]),
                                reads=[("t", "qkv")], writes=[("t", "dbg")], tag="dbg.s1")
                        P.op("scalar", lambda sc, m=m: sc.activation(
                            sqb[:], qkvb[:, m * NB:(m + 1) * NB], AF.Square),
                            reads=[("t", "qkv")], writes=[("t", "sqb")], tag="n.sq")
                        P.op("tensor", lambda t: t.matmul(
                            pstat[0:2, 0:NB], C["blockhalf_b"][:], sqb[:],
                            start=True, stop=True),
                            reads=[("t", "sqb"), ("c", "blockhalf_b")],
                            writes=[kps], tag="n.mm")
                        P.op("scalar", lambda sc: sc.activation(
                            nrm[:], pstat[0:2, 0:NB], AF.Sqrt),
                            reads=[kps], writes=[("t", "nrm")], tag="n.sqrt")
                        P.op("vector", lambda v: v.tensor_scalar(
                            nrm[:], nrm[:], 1e-12, None, op0=ALU.max),
                            reads=[("t", "nrm"), ("c", "fc")], writes=[("t", "nrm")], tag="n.max")
                        P.op("vector", lambda v: v.reciprocal(nrm[:], nrm[:]),
                             reads=[("t", "nrm")], writes=[("t", "nrm")], tag="n.rec")
                        P.op("tensor", lambda t: t.matmul(
                            prep[:, 0:NB], C["selhalf"][:], nrm[:],
                            start=True, stop=True),
                            reads=[("t", "nrm"), ("c", "selhalf")], writes=[kpr],
                            tag="n.rep")
                        P.op("vector", lambda v, m=m, scol=scol: v.scalar_tensor_tensor(
                            qkvb[:, m * NB:(m + 1) * NB], qkvb[:, m * NB:(m + 1) * NB],
                            C[scol][:, 0:1], prep[:, 0:NB], op0=ALU.mult, op1=ALU.mult),
                            reads=[("t", "qkv"), ("c", scol), kpr],
                            writes=[("t", "qkv")], tag="n.app")
                        if dbgk:
                            P.op("vector", lambda v, m=m: v.tensor_copy(
                                dbg_sb[0:64, 128:256], qkvb[0:64, m * NB + 384: m * NB + 512]),
                                reads=[("t", "qkv")], writes=[("t", "dbg")], tag="dbg.s2")
                        for pp in range(4):
                            src_b = (pp + 1) % 2 * 32 + (pp // 2) * 64
                            dst_b = pp * 32
                            P.op("vector", lambda v, m=m, src_b=src_b, dst_b=dst_b: v.tensor_copy(
                                shq[dst_b:dst_b + 32, :],
                                qkvb[src_b:src_b + 32, m * NB:(m + 1) * NB]),
                                reads=[("t", "qkv")], writes=[("t", "shq")], tag="rt.sh")
                        P.op("vector", lambda v, m=m, cst=cst, c0=c0, c1_=c1_: v.tensor_tensor(
                            tq1[:], qkvb[:, m * NB:(m + 1) * NB],
                            rott[cst][:, c0:c1_], op=ALU.mult),
                            reads=[("t", "qkv"), ("t", "rott")], writes=[("t", "tq1")],
                            tag="rt.c")
                        P.op("vector", lambda v, sst=sst, c0=c0, c1_=c1_: v.tensor_tensor(
                            shq[:], shq[:], rott[sst][:, c0:c1_], op=ALU.mult),
                            reads=[("t", "shq"), ("t", "rott")], writes=[("t", "shq")],
                            tag="rt.s")
                        P.op("vector", lambda v, m=m: v.tensor_tensor(
                            qkvb[:, m * NB:(m + 1) * NB], tq1[:], shq[:], op=ALU.add),
                            reads=[("t", "tq1"), ("t", "shq")], writes=[("t", "qkv")],
                            tag="rt.a")
                        if dbgk:
                            P.op("vector", lambda v, m=m: v.tensor_copy(
                                dbg_sb[0:64, 256:384], qkvb[0:64, m * NB + 384: m * NB + 512]),
                                reads=[("t", "qkv")], writes=[("t", "dbg")], tag="dbg.s3")
                for j in range(4):
                    kt = 4 * blk + j
                    slot = kt % 6
                    P.op("vector", lambda v, j=j, slot=slot: v.tensor_copy(
                        kring.ap().rearrange("p (f q) -> p f q", f=8)[:, :, slot * 128:(slot + 1) * 128],
                        qkvb.ap().rearrange("p (m q) -> p m q", m=24)[:, 8:16, j * 128:(j + 1) * 128]),
                        reads=[("t", "qkv")], writes=[("t", "kring")], tag="r.kf")
                for j in range(4):
                    kt = 4 * blk + j
                    slot = kt % 6
                    for f in range(8):
                        P.op("tensor", lambda t, j=j, f=f: t.transpose(
                            pdb[:, f * 128:(f + 1) * 128],
                            qkvb[:, (16 + f) * NB + j * 128: (16 + f) * NB + (j + 1) * 128],
                            C["idf_b"][:]),
                            reads=[("t", "qkv"), ("c", "idf_b")],
                            writes=[("psum", "pdb")], tag="r.vtp")
                    for h in range(NH):
                        P.op("scalar", lambda sc, h=h, slot=slot: sc.copy(
                            vring[:, slot * NH * 65 + h * 65: slot * NH * 65 + h * 65 + 64],
                            pdb[:, h * 64:(h + 1) * 64]),
                            reads=[("psum", "pdb")], writes=[("t", "vring")], tag="r.vev")
                first = (blk == 0)
                slots_rel = range(2, 6) if first else range(6)
                for h in range(NH):
                    fb = h // 2
                    pb_ = (h % 2) * 64
                    pban, kpb = (pa0, ("psum", "pa0")) if h % 2 == 0 else (pa1, ("psum", "pa1"))
                    for (r_, po, qo, wd) in SPIECES:
                        if first and r_ < 2:
                            continue
                        kt = 4 * blk + r_ - 2
                        slot = kt % 6
                        P.op("tensor", lambda t, slot=slot, fb=fb, pb_=pb_, po=po, qo=qo, wd=wd: t.matmul(
                            pS[:, po:po + wd],
                            kring[pb_:pb_ + 64, fb * 768 + slot * 128: fb * 768 + (slot + 1) * 128],
                            qkvb[pb_:pb_ + 64, fb * NB + qo: fb * NB + qo + wd],
                            start=True, stop=True),
                            reads=[("t", "kring"), ("t", "qkv")],
                            writes=[("psum", "pS")], tag="a.smm")
                    lo = 384 if first else 0
                    P.op("scalar", lambda sc, lo=lo: sc.activation(
                        abp[:, lo:1536], pS[:, lo:1536], AF.Exp,
                        bias=negc[:], scale=8.0),
                        reads=[("psum", "pS"), ("t", "negc")],
                        writes=[("t", "abp")], tag="a.exp")
                    P.op("vector", lambda v, lo=lo: v.tensor_tensor(
                        abp[:, lo:1536], abp[:, lo:1536], C["maskpack_b"][:, lo:1536],
                        op=ALU.mult),
                        reads=[("t", "abp"), ("c", "maskpack_b")],
                        writes=[("t", "abp")], tag="a.mask")
                    for qs in range(4):
                        rs = [r_ for r_ in (qs, qs + 1, qs + 2)
                              if (not first) or r_ >= 2]
                        for ri, r_ in enumerate(rs):
                            kt = 4 * blk + r_ - 2
                            slot = kt % 6
                            off = SLOT_O[r_] + (qs * 128 - SLOT_Q[r_])
                            P.op("tensor", lambda t, slot=slot, h=h, pban=pban,
                                 qs=qs, off=off, st=(ri == 0), sp=(ri == len(rs) - 1): t.matmul(
                                pban[0:65, qs * 128:(qs + 1) * 128],
                                vring[:, slot * NH * 65 + h * 65: slot * NH * 65 + (h + 1) * 65],
                                abp[:, off:off + 128],
                                start=st, stop=sp),
                                reads=[("t", "vring"), ("t", "abp")],
                                writes=[kpb], tag="a.vmm")
                    if False:
                        P.op("vector", lambda v: v.tensor_copy(dbg_sb[:, 384:1536], pS[:, 384:1536]),
                             reads=[("psum", "pS")], writes=[("t", "dbg")], tag="dbg.s")
                        P.op("vector", lambda v: v.tensor_copy(dbg_sb[0:64, 1536:1664],
                             qkvb[0:64, 8 * NB + 384: 8 * NB + 512]),
                             reads=[("t", "qkv")], writes=[("t", "dbg")], tag="dbg.k3")
                        P.op("vector", lambda v: v.tensor_copy(dbg_sb[0:64, 1664:1792],
                             kring[0:64, 3 * 128:4 * 128]),
                             reads=[("t", "kring")], writes=[("t", "dbg")], tag="dbg.k1")
                    P.op("scalar", lambda sc, pban=pban: sc.copy(
                        dstripe[:], pban[64:65, 0:NB]),
                        reads=[kpb], writes=[("t", "dstripe")], tag="a.den")
                    P.op("vector", lambda v: v.reciprocal(dstripe[:], dstripe[:]),
                         reads=[("t", "dstripe")], writes=[("t", "dstripe")], tag="a.rec")
                    P.op("tensor", lambda t: t.matmul(
                        prep[0:64, 0:NB], C["c1"][0:1, 0:64], dstripe[:],
                        start=True, stop=True),
                        reads=[("t", "dstripe"), ("c", "c1")], writes=[kpr], tag="a.rep")
                    P.op("scalar", lambda sc: sc.copy(rrsb[:], prep[0:64, 0:NB]),
                         reads=[kpr], writes=[("t", "rrsb")], tag="a.rev")
                    P.op("vector", lambda v, h=h, fb=fb, pb_=pb_, pban=pban: v.scalar_tensor_tensor(
                        aout[pb_:pb_ + 64, fb * NB:(fb + 1) * NB],
                        pban[0:64, 0:NB], 1.0, rrsb[:],
                        op0=ALU.mult, op1=ALU.mult),
                        reads=[kpb, ("t", "rrsb")], writes=[("t", "aout")], tag="a.div")
                for m in range(8):
                    pban, kpb = (pa0, ("psum", "pa0")) if m % 2 == 0 else (pa1, ("psum", "pa1"))
                    for k in range(8):
                        P.op("tensor", lambda t, m=m, k=k, pban=pban: t.matmul(
                            pban[:, 0:NB],
                            wob[:, (k * 8 + m) * 128:(k * 8 + m + 1) * 128],
                            aout[:, k * NB:(k + 1) * NB],
                            start=(k == 0), stop=(k == 7)),
                            reads=[("w", "o"), ("t", "aout")],
                            writes=[kpb], tag="wo.mm")
                    P.op("vector", lambda v, m=m, pban=pban: v.tensor_tensor(
                        zblk[:, m * NB:(m + 1) * NB], zblk[:, m * NB:(m + 1) * NB],
                        pban[:, 0:NB], op=ALU.add),
                        reads=[("psum", "pa" + str(m % 2)), ("t", "zblk")],
                        writes=[("t", "zblk")], tag="wo.ev")
                P.op("vector", lambda v: v.tensor_copy(bcol[:, 0:4], bcol[:, 4:8]),
                     reads=[("t", "bcolc")], writes=[("t", "bcol")], tag="sh.rot")
                P.dma("sync", zo_h[s].rearrange("p (f t) -> p f t", f=8)[:, :, c0:c1_],
                      rr(zblk, 8), reads=[("t", "zblk")], tag="z.st")


class LNHelperB(LNHelper):
    def run_from_bf16(self, C, pstat, kps, prep, kpr, src_b, src_key):
        P, NB = self.P, self.NB
        xc = self.xc
        for f in range(8):
            P.op("tensor", lambda t, f=f: t.matmul(
                pstat[0:1, 0:NB], C["onescol_b"][:], src_b[:, f * NB:(f + 1) * NB],
                start=(f == 0), stop=(f == 7)),
                reads=[src_key, ("c", "onescol_b")], writes=[kps], tag="lnb.sum")
        P.op("scalar", lambda s: s.copy(self.stat1[:], pstat[0:1, 0:NB]),
             reads=[kps], writes=[self.kst1], tag="lnb.sumev")
        P.op("tensor", lambda t: t.matmul(
            prep[:, 0:NB], C["c1024"][:], self.stat1[:], start=True, stop=True),
            reads=[self.kst1, ("c", "c1024")], writes=[kpr], tag="lnb.murep")
        P.op("vector", lambda v: v.tensor_tensor(
            rr(xc, 8), src_b.ap().rearrange("p (f t) -> p f t", f=8),
            prep[:, 0:NB].unsqueeze(1).broadcast_to([128, 8, NB]),
            op=ALU.subtract),
            reads=[src_key, kpr], writes=[self.kxc], tag="lnb.xc")
        P.op("scalar", lambda s: s.activation(self.xb[:], xc[:], AF.Square),
             reads=[self.kxc], writes=[self.kxb], tag="lnb.sq")
        for f in range(8):
            P.op("tensor", lambda t, f=f: t.matmul(
                pstat[0:1, 0:NB], C["onescol_b"][:], self.xb[:, f * NB:(f + 1) * NB],
                start=(f == 0), stop=(f == 7)),
                reads=[self.kxb, ("c", "onescol_b")], writes=[kps], tag="lnb.ssq")
        P.op("scalar", lambda s: s.activation(
            self.stat1[:], pstat[0:1, 0:NB], AF.Sqrt, bias=1e-5, scale=1.0 / 1024.0),
            reads=[kps, ("c", "fc")], writes=[self.kst1], tag="lnb.std")
        P.op("vector", lambda v: v.reciprocal(self.stat2[:], self.stat1[:]),
             reads=[self.kst1], writes=[self.kst2], tag="lnb.rstd")
        P.op("tensor", lambda t: t.matmul(
            prep[:, 0:NB], C["c1"][:], self.stat2[:], start=True, stop=True),
            reads=[self.kst2, ("c", "c1")], writes=[kpr], tag="lnb.rrep")
        return prep[:, 0:NB].unsqueeze(1).broadcast_to([128, 8, NB])



def emit_l0ff(P, es, nc, C0, z_h, T, SEQS, NB, nblk):
    zo_h = z_h
    final = False
    if True:
        w1_h = nc.dram_tensor("w16", [128, 6 * 352 * 128], F32, kind="ExternalInput").ap().tensor
        w2_h = nc.dram_tensor("w26", [128, 6 * 176 * 128], F32, kind="ExternalInput").ap().tensor
        fflc_h = nc.dram_tensor("fflc6", [6 * 128, 68], F32, kind="ExternalInput").ap().tensor
        C = dict(C0)
        fflc = alloc(es, nc, "fflc", [128, 68], F32)
        C["lng"] = fflc.ap()[:, 0:8]
        C["lnb_"] = fflc.ap()[:, 8:16]
        C["b1c"] = fflc.ap()[:, 16:60]
        C["b2c"] = fflc.ap()[:, 60:68]
        dyn_dma(P, "sync", fflc[:], fflc_h, 0, 68 * 128,
                [[68, 128], [1, 68]],
                writes=[("c", "lng"), ("c", "lnb_"), ("c", "b1c"), ("c", "b2c")],
                tag="lc.ff")

        stage = alloc(es, nc, "stage", [128, 1024], F32)
        w1b = alloc(es, nc, "w1b", [128, 352 * 128], BF16)
        w2b = alloc(es, nc, "w2b", [128, 176 * 128], BF16)
        w1_st = nc.dram_tensor(uname("w1_st"), [128, 352 * 128], F32).ap()
        w2_st = nc.dram_tensor(uname("w2_st"), [128, 176 * 128], F32).ap()
        dyn_dma(P, "sync", w1_st[:, :], w1_h, 0, 352 * 128,
                [[6 * 352 * 128, 128], [1, 352 * 128]],
                writes=[("st", "w1")], tag="w.st")
        dyn_dma(P, "sync", w2_st[:, :], w2_h, 0, 176 * 128,
                [[6 * 176 * 128, 128], [1, 176 * 128]],
                writes=[("st", "w2")], tag="w.st2")
        for r in range(44):
            P.dma("sync", stage[:], w1_st[:, r * 1024:(r + 1) * 1024],
                  reads=[("st", "w1")], writes=[("stage",)], tag="w.l")
            P.op("vector", lambda v, r=r: v.tensor_copy(
                w1b[:, r * 1024:(r + 1) * 1024], stage[:]),
                reads=[("stage",)], writes=[("w", "w1")], tag="w.c")
        for r in range(22):
            P.dma("sync", stage[:], w2_st[:, r * 1024:(r + 1) * 1024],
                  reads=[("st", "w2")], writes=[("stage",)], tag="w.l2")
            P.op("vector", lambda v, r=r: v.tensor_copy(
                w2b[:, r * 1024:(r + 1) * 1024], stage[:]),
                reads=[("stage",)], writes=[("w", "w2")], tag="w.c2")

        zblk = alloc(es, nc, "zblk", [128, 8 * NB], F32)
        shz = alloc(es, nc, "shz", [128, 8 * NB], BF16)
        h2b = alloc(es, nc, "h2b", [128, 22 * NB], BF16)
        gel = alloc(es, nc, "gel", [128, NB], BF16)
        av = alloc(es, nc, "av", [128, NB], BF16)
        bcol = alloc(es, nc, "bcol", [128, 8], F32)
        if final:
            x1b = alloc(es, nc, "x1b", [128, 8 * NB], F32)
        ln1 = LNHelperB(P, es, nc, NB, tag="1", xc_dtype=BF16)
        lnx = ln1.xb

        pstat = psum1(es, nc, "pstat")
        prep = psum1(es, nc, "prep")
        pa0 = psum1(es, nc, "pa0")
        pa1 = psum1(es, nc, "pa1")
        kps, kpr = ("psum", "pstat"), ("psum", "prep")

        for s in range(SEQS):
            P.op("vector", lambda v: v.memset(bcol[:], 0.0),
                 writes=[("t", "bcol"), ("t", "bcolc")], tag="r.bz")
            for blk in range(nblk):
                c0, c1_ = blk * NB, (blk + 1) * NB
                P.dma("sync", rr(zblk, 8),
                      z_h[s].rearrange("p (f t) -> p f t", f=8)[:, :, c0:c1_],
                      writes=[("t", "zblk")], tag="z.ld")
                P.op("vector", lambda v: v.tensor_copy(
                    bcol[:, 4:8], r3(zblk, 8, NB, NB - 1, NB)[:, 4:8, 0]),
                    reads=[("t", "zblk")], writes=[("t", "bcolc")], tag="sh.sv")
                P.op("vector", lambda v: v.tensor_copy(
                    rr(shz, 8)[:, 0:4, :], rr(zblk, 8)[:, 0:4, :]),
                    reads=[("t", "zblk")], writes=[("t", "shz")], tag="sh.a")
                P.op("vector", lambda v: v.tensor_copy(
                    rr(shz, 8)[:, 4:8, 1:NB], rr(zblk, 8)[:, 4:8, 0:NB - 1]),
                    reads=[("t", "zblk")], writes=[("t", "shz")], tag="sh.b")
                P.op("vector", lambda v: v.tensor_copy(
                    rr(shz, 8)[:, 4:8, 0], bcol[:, 0:4]),
                    reads=[("t", "bcol")], writes=[("t", "shz")], tag="sh.c")
                rrep = ln1.run_from_bf16(C, pstat, kps, prep, kpr, shz, ("t", "shz"))
                ln1.to_bf16(rr(lnx, 8), [ln1.kxb], rrep, kpr)
                for m in range(8):
                    P.op("vector", lambda v, m=m: v.tensor_scalar(
                        lnx[:, m * NB:(m + 1) * NB], lnx[:, m * NB:(m + 1) * NB],
                        C["lng"][:, m:m + 1], C["lnb_"][:, m:m + 1],
                        op0=ALU.mult, op1=ALU.add),
                        reads=[ln1.kxb, ("c", "lng"), ("c", "lnb_")],
                        writes=[ln1.kxb], tag="f.gb")
                for j in range(22):
                    for which, mm_ in [("a", j), ("g", 22 + j)]:
                        pban, kpb = (pa0, ("psum", "pa0")) if which == "a" else (pa1, ("psum", "pa1"))
                        for k in range(8):
                            P.op("tensor", lambda t, mm_=mm_, k=k, pban=pban: t.matmul(
                                pban[:, 0:NB],
                                w1b[:, (k * 44 + mm_) * 128:(k * 44 + mm_ + 1) * 128],
                                lnx[:, k * NB:(k + 1) * NB],
                                start=(k == 0), stop=(k == 7)),
                                reads=[("w", "w1"), ln1.kxb],
                                writes=[kpb], tag="f.mm1")
                    P.op("scalar", lambda sc, j=j: sc.activation(
                        av[:], pa0[:, 0:NB], AF.Identity, bias=C["b1c"][:, j:j + 1]),
                        reads=[("psum", "pa0"), ("c", "b1c")], writes=[("t", "av")],
                        tag="f.a")
                    P.op("scalar", lambda sc, j=j: sc.activation(
                        gel[:], pa1[:, 0:NB], AF.Gelu, bias=C["b1c"][:, 22 + j:23 + j]),
                        reads=[("psum", "pa1"), ("c", "b1c")], writes=[("t", "gel")],
                        tag="f.g")
                    P.op("vector", lambda v, j=j: v.tensor_tensor(
                        h2b[:, j * NB:(j + 1) * NB], av[:], gel[:], op=ALU.mult),
                        reads=[("t", "av"), ("t", "gel")], writes=[("t", "h2")],
                        tag="f.h2")
                for m in range(8):
                    pban, kpb = (pa0, ("psum", "pa0")) if m % 2 == 0 else (pa1, ("psum", "pa1"))
                    for k in range(22):
                        P.op("tensor", lambda t, m=m, k=k, pban=pban: t.matmul(
                            pban[:, 0:NB],
                            w2b[:, (k * 8 + m) * 128:(k * 8 + m + 1) * 128],
                            h2b[:, k * NB:(k + 1) * NB],
                            start=(k == 0), stop=(k == 21)),
                            reads=[("w", "w2"), ("t", "h2")],
                            writes=[kpb], tag="f.mm2")
                    P.op("vector", lambda v, m=m, pban=pban: v.scalar_tensor_tensor(
                        zblk[:, m * NB:(m + 1) * NB], pban[:, 0:NB],
                        C["b2c"][:, m:m + 1], zblk[:, m * NB:(m + 1) * NB],
                        op0=ALU.add, op1=ALU.add),
                        reads=[("psum", "pa" + str(m % 2)), ("c", "b2c"), ("t", "zblk")],
                        writes=[("t", "zblk")], tag="f.ev")
                P.op("vector", lambda v: v.tensor_copy(bcol[:, 0:4], bcol[:, 4:8]),
                     reads=[("t", "bcolc")], writes=[("t", "bcol")], tag="sh.rot")
                P.dma("sync", zo_h[s].rearrange("p (f t) -> p f t", f=8)[:, :, c0:c1_],
                      rr(zblk, 8), reads=[("t", "zblk")], tag="z.st")
                if final:
                    P.dma("sync", rr(x1b, 8),
                          x1_h[s].rearrange("p (f t) -> p f t", f=8)[:, :, c0:c1_],
                          writes=[("t", "x1b")], tag="x1.ld")
                    rz = ln1.run(C, pstat, kps, prep, kpr, rr(zblk, 8), [("t", "zblk")])
                    ln1.add_into(rr(x1b, 8), [("t", "x1b")], rz, kpr)
                    P.dma("sync", x2_h[s].rearrange("p (f t) -> p f t", f=8)[:, :, c0:c1_],
                          rr(x1b, 8), reads=[("t", "x1b")], tag="x2.st")



def emit_l0fin(P, es, nc, C0, z_h, x1_h, x2_h, T, SEQS, NB, nblk):
    if True:
        C = C0
        zblk = alloc(es, nc, "zblk", [128, 8 * NB], F32)
        x1b = alloc(es, nc, "x1b", [128, 8 * NB], F32)
        ln1 = LNHelper(P, es, nc, NB, tag="1")
        pstat = psum1(es, nc, "pstat")
        prep = psum1(es, nc, "prep")
        kps, kpr = ("psum", "pstat"), ("psum", "prep")
        for s in range(SEQS):
            for blk in range(nblk):
                c0, c1_ = blk * NB, (blk + 1) * NB
                P.dma("sync", rr(zblk, 8),
                      z_h[s].rearrange("p (f t) -> p f t", f=8)[:, :, c0:c1_],
                      writes=[("t", "zblk")], tag="z.ld")
                P.dma("sync", rr(x1b, 8),
                      x1_h[s].rearrange("p (f t) -> p f t", f=8)[:, :, c0:c1_],
                      writes=[("t", "x1b")], tag="x1.ld")
                rz = ln1.run(C, pstat, kps, prep, kpr, rr(zblk, 8), [("t", "zblk")])
                ln1.add_into(rr(x1b, 8), [("t", "x1b")], rz, kpr)
                P.dma("sync", x2_h[s].rearrange("p (f t) -> p f t", f=8)[:, :, c0:c1_],
                      rr(x1b, 8), reads=[("t", "x1b")], tag="x2.st")


def emit_l1attn(P, es, nc, C0, x2_h, x3_h, T, SEQS, NB, CH, nch, NH, nblk):
    if True:
        w_h = {}
        for nm_ in ["wq", "wk", "wv", "wo"]:
            w_h[nm_] = nc.dram_tensor(nm_, [128, 8192], F32, kind="ExternalInput").ap()
        C = dict(C0)
        C.update(load_consts_f(P, es, nc, [
            ("bq", (128, 8)), ("bk", (128, 8)), ("bv", (128, 8)), ("bo", (128, 8)),
            ("MINC", (128, 128)),
        ]))
        cosr_h = nc.dram_tensor("cosr", [1, T], F32, kind="ExternalInput").ap()
        sinr_h = nc.dram_tensor("sinr", [1, T], F32, kind="ExternalInput").ap()

        stage = alloc(es, nc, "stage", [128, 2048], F32)
        wsb = {nm_: alloc(es, nc, nm_ + "b", [128, 8192], BF16)
               for nm_ in ["wq", "wk", "wv", "wo"]}
        for nm_ in ["wq", "wk", "wv", "wo"]:
            for r in range(4):
                P.dma("sync", stage[:], w_h[nm_][:, r * 2048:(r + 1) * 2048],
                      writes=[("stage",)], tag="w.load")
                P.op("vector", lambda v, nm_=nm_, r=r: v.tensor_copy(
                    wsb[nm_][:, r * 2048:(r + 1) * 2048], stage[:]),
                    reads=[("stage",)], writes=[("w", nm_)], tag="w.cast")
        cosw = alloc(es, nc, "cosw", [128, T], BF16)
        sinw = alloc(es, nc, "sinw", [128, T], BF16)
        prep = psum1(es, nc, "prep")
        kpr = ("psum", "prep")
        for tab, rowh, kk in [(cosw, cosr_h, "cosw"), (sinw, sinr_h, "sinw")]:
            for b in range(T // 512):
                P.dma("sync", stage[0:1, 0:512], rowh[0:1, b * 512:(b + 1) * 512],
                      writes=[("stage",)], tag="tab.ld")
                P.op("tensor", lambda t, b=b: t.matmul(
                    prep[:, 0:512], C["c1"][:], stage[0:1, 0:512],
                    start=True, stop=True),
                    reads=[("stage",), ("c", "c1")], writes=[kpr], tag="tab.rep")
                P.op("scalar", lambda s, tab=tab, b=b, kk=kk: s.copy(
                    tab[:, b * 512:(b + 1) * 512], prep[:, 0:512]),
                    reads=[kpr], writes=[("t", kk)], tag="tab.ev")

        x2b = alloc(es, nc, "x2b", [128, 8 * NB], F32)
        qp = alloc(es, nc, "qp", [128, 8 * NB], BF16)
        kp = alloc(es, nc, "kp", [128, 8 * NB], BF16)
        vv = alloc(es, nc, "vv", [128, 8 * NB], BF16)
        qc = alloc(es, nc, "qc", [64, 16 * CH], BF16)
        qs_ = alloc(es, nc, "qs", [64, 16 * CH], BF16)
        kc = alloc(es, nc, "kc", [64, 16 * CH], BF16)
        ks_ = alloc(es, nc, "ks", [64, 16 * CH], BF16)
        vtma = alloc(es, nc, "vtma", [128, NH * 65], BF16)
        kctm = alloc(es, nc, "kctm", [128, 1024], BF16)
        kstm = alloc(es, nc, "kstm", [128, 1024], BF16)
        ab = alloc(es, nc, "ab", [128, CH], BF16)
        state = alloc(es, nc, "state", [64, NH * 130], F32)
        stateb = alloc(es, nc, "stateb", [64, NH * 130], BF16)
        dstripe = alloc(es, nc, "dstripe", [1, CH], F32)
        denr = alloc(es, nc, "denr", [1, CH], F32)
        rrsb = alloc(es, nc, "rrsb", [64, CH], F32)
        outT = alloc(es, nc, "outT", [128, 8 * NB], BF16)
        ln1 = LNHelper(P, es, nc, NB, tag="1")

        pstat = psum1(es, nc, "pstat")
        pa0 = psum1(es, nc, "pa0")
        pa1 = psum1(es, nc, "pa1")
        pA = psum1(es, nc, "pA")
        pS = psum1(es, nc, "pS")
        pdb = es.enter_context(nc.psum_tensor(uname("pdb"), [128, 1024], BF16, unsafe_allow_realloc=True))
        kps = ("psum", "pstat")

        P.op("vector", lambda v: v.memset(vtma[:], 1.0), writes=[("t", "vtma")],
             tag="va.ones")

        for s in range(SEQS):
            P.op("vector", lambda v: v.memset(state[:], 0.0),
                 writes=[("t", "state")], tag="st.z")
            P.op("vector", lambda v: v.memset(stateb[:], 0.0),
                 writes=[("t", "stateb")], tag="st.z16")
            for blk in range(nblk):
                c0, c1_ = blk * NB, (blk + 1) * NB
                P.dma("sync", rr(x2b, 8),
                      x2_h[s].rearrange("p (f t) -> p f t", f=8)[:, :, c0:c1_],
                      writes=[("t", "x2b")], tag="x2.ld")
                rrep = ln1.run(C, pstat, kps, prep, kpr, rr(x2b, 8), [("t", "x2b")])
                lnb = ln1.xb
                ln1.to_bf16(rr(lnb, 8), [ln1.kxb], rrep, kpr)
                for wname, bias, dstt, kk, act in [
                        ("wq", "bq", qp, "qp", AF.Relu), ("wk", "bk", kp, "kp", AF.Relu),
                        ("wv", "bv", vv, "vv", AF.Identity)]:
                    for m in range(8):
                        pban, kpb = (pa0, ("psum", "pa0")) if m % 2 == 0 else (pa1, ("psum", "pa1"))
                        for k in range(8):
                            P.op("tensor", lambda t, wname=wname, m=m, k=k, pban=pban: t.matmul(
                                pban[:, 0:NB],
                                wsb[wname][:, (k * 8 + m) * 128:(k * 8 + m + 1) * 128],
                                lnb[:, k * NB:(k + 1) * NB],
                                start=(k == 0), stop=(k == 7)),
                                reads=[("w", wname), ln1.kxb],
                                writes=[kpb], tag="p.mm")
                        P.op("scalar", lambda sc, m=m, dstt=dstt, bias=bias, act=act, kk=kk, pban=pban: sc.activation(
                            dstt[:, m * NB:(m + 1) * NB], pban[:, 0:NB], act,
                            bias=C[bias][:, m:m + 1]),
                            reads=[kpb, ("psum", "pa" + str(m % 2)), ("c", bias)],
                            writes=[("t", kk)], tag="p.ev")
                for ch in range(nch):
                    t0 = ch * CH
                    g0 = c0 + t0
                    for srctab, tabk, dstt, kk, src, ksrc in [
                            (cosw, "cosw", qc, "qc", qp, "qp"), (sinw, "sinw", qs_, "qs", qp, "qp"),
                            (cosw, "cosw", kc, "kc", kp, "kp"), (sinw, "sinw", ks_, "ks", kp, "kp")]:
                        for par in range(2):
                            P.op("vector", lambda v, srctab=srctab, dstt=dstt, src=src, t0=t0, g0=g0, par=par: v.tensor_tensor(
                                dstt.ap().rearrange("p (f q) -> p f q", f=8)[0:64, :, par * CH:(par + 1) * CH],
                                rr(src, 8)[par * 64:par * 64 + 64, :, t0:t0 + CH],
                                srctab[par * 64:par * 64 + 64, g0:g0 + CH].unsqueeze(1).broadcast_to([64, 8, CH]),
                                op=ALU.mult),
                                reads=[("t", ksrc), ("t", tabk)],
                                writes=[("t", kk)], tag="c.qk")
                    for f in range(8):
                        P.op("tensor", lambda t, f=f, t0=t0: t.transpose(
                            pdb[:, f * 128:(f + 1) * 128],
                            vv[:, f * NB + t0: f * NB + t0 + CH],
                            C["idf_b"][:]),
                            reads=[("t", "vv"), ("c", "idf_b")],
                            writes=[("psum", "pdb")], tag="c.vtp")
                    for h in range(NH):
                        P.op("scalar", lambda sc, h=h: sc.copy(
                            vtma[:, h * 65: h * 65 + 64],
                            pdb[:, h * 64:(h + 1) * 64]),
                            reads=[("psum", "pdb")], writes=[("t", "vtma")], tag="c.vtev")
                    for src, srck, dstt, kk in [(kc, "kc", kctm, "kctm"), (ks_, "ks", kstm, "kstm")]:
                        for h in range(NH):
                            P.op("tensor", lambda t, h=h, src=src, srck=srck: t.transpose(
                                pdb[:, h * 64:(h + 1) * 64],
                                src[0:64, h * CH:(h + 1) * CH],
                                C["idf_b"][0:64, 0:64]),
                                reads=[("t", srck), ("c", "idf_b")],
                                writes=[("psum", "pdb")], tag="c.ktp")
                        P.op("vector", lambda v, dstt=dstt, kk=kk: v.tensor_copy(dstt[:], pdb[:, 0:1024]),
                             reads=[("psum", "pdb")], writes=[("t", kk)], tag="c.ktev")
                    for h in range(NH):
                        fb = h // 2
                        pb_ = (h % 2) * 64
                        pban, kpb = (pa0, ("psum", "pa0")) if h % 2 == 0 else (pa1, ("psum", "pa1"))
                        P.op("tensor", lambda t, h=h: t.matmul(
                            pA[:, 0:CH],
                            kc[0:64, h * CH:(h + 1) * CH],
                            qc[0:64, h * CH:(h + 1) * CH],
                            start=True, stop=False),
                            reads=[("t", "kc"), ("t", "qc")],
                            writes=[("psum", "pA")], tag="h.amm")
                        P.op("tensor", lambda t, h=h: t.matmul(
                            pA[:, 0:CH],
                            ks_[0:64, h * CH:(h + 1) * CH],
                            qs_[0:64, h * CH:(h + 1) * CH],
                            start=False, stop=True),
                            reads=[("t", "ks"), ("t", "qs")],
                            writes=[("psum", "pA")], tag="h.amm2")
                        P.op("vector", lambda v: v.tensor_tensor(
                            ab[:], pA[:, 0:CH], C["MINC"][:], op=ALU.mult),
                            reads=[("psum", "pA"), ("c", "MINC")],
                            writes=[("t", "ab")], tag="h.mask")
                        P.op("tensor", lambda t, h=h, pban=pban: t.matmul(
                            pban[0:65, 0:CH], vtma[:, h * 65:(h + 1) * 65], ab[:],
                            start=True, stop=False),
                            reads=[("t", "vtma"), ("t", "ab")],
                            writes=[kpb], tag="h.omm")
                        P.op("tensor", lambda t, h=h, pban=pban: t.matmul(
                            pban[0:65, 0:CH],
                            stateb[0:64, h * 130: h * 130 + 65],
                            qc[0:64, h * CH:(h + 1) * CH],
                            start=False, stop=False),
                            reads=[("t", "stateb"), ("t", "qc")],
                            writes=[kpb], tag="h.cmm")
                        P.op("tensor", lambda t, h=h, pban=pban: t.matmul(
                            pban[0:65, 0:CH],
                            stateb[0:64, h * 130 + 65: h * 130 + 130],
                            qs_[0:64, h * CH:(h + 1) * CH],
                            start=False, stop=True),
                            reads=[("t", "stateb"), ("t", "qs")],
                            writes=[kpb], tag="h.cmm2")
                        P.op("scalar", lambda sc, pban=pban: sc.copy(
                            dstripe[:], pban[64:65, 0:CH]),
                            reads=[kpb], writes=[("t", "dstripe")], tag="h.den")
                        P.op("vector", lambda v: v.tensor_scalar(
                            denr[:], dstripe[:], 1e-6, None, op0=ALU.max),
                            reads=[("t", "dstripe")], writes=[("t", "denr")], tag="h.dmax")
                        P.op("vector", lambda v: v.reciprocal(denr[:], denr[:]),
                             reads=[("t", "denr")], writes=[("t", "denr")], tag="h.drec")
                        P.op("tensor", lambda t: t.matmul(
                            prep[0:64, 0:CH], C["c1"][0:1, 0:64], denr[:],
                            start=True, stop=True),
                            reads=[("t", "denr"), ("c", "c1")], writes=[kpr], tag="h.drep")
                        P.op("scalar", lambda sc: sc.copy(rrsb[:], prep[0:64, 0:CH]),
                             reads=[kpr], writes=[("t", "rrsb")], tag="h.drev")
                        P.op("vector", lambda v, h=h, fb=fb, pb_=pb_, t0=t0, pban=pban: v.scalar_tensor_tensor(
                            outT[pb_:pb_ + 64, fb * NB + t0: fb * NB + t0 + CH],
                            pban[0:64, 0:CH], 1.0, rrsb[:],
                            op0=ALU.mult, op1=ALU.mult),
                            reads=[kpb, ("t", "rrsb")],
                            writes=[("t", "outT")], tag="h.div")
                        if False:
                            P.op("vector", lambda v: v.tensor_copy(dbg_sb[:, 0:128], ab[:]),
                                 reads=[("t", "ab")], writes=[("t", "dbg")], tag="dbg.ab")
                            P.op("vector", lambda v: v.tensor_copy(dbg_sb[0:64, 128:256], qc[0:64, 0:128]),
                                 reads=[("t", "qc")], writes=[("t", "dbg")], tag="dbg.qc")
                            P.op("vector", lambda v: v.tensor_copy(dbg_sb[:, 256:384], vtma[:, 0:128]),
                                 reads=[("t", "vtma")], writes=[("t", "dbg")], tag="dbg.vt")
                            P.op("vector", lambda v: v.tensor_copy(dbg_sb[0:64, 384:512], kc[0:64, 0:128]),
                                 reads=[("t", "kc")], writes=[("t", "dbg")], tag="dbg.kc")
                            P.op("vector", lambda v, pban=pban: v.tensor_copy(dbg_sb[0:65, 512:640], pban[0:65, 0:CH]),
                                 reads=[("psum", "pa0")], writes=[("t", "dbg")], tag="dbg.num")
                            P.op("vector", lambda v: v.tensor_copy(dbg_sb[0:64, 640:704], qs_[0:64, 0:64]),
                                 reads=[("t", "qs")], writes=[("t", "dbg")], tag="dbg.qs")
                            P.op("vector", lambda v: v.tensor_copy(dbg_sb[0:64, 704:768], ks_[0:64, 0:64]),
                                 reads=[("t", "ks")], writes=[("t", "dbg")], tag="dbg.ks")
                        P.op("tensor", lambda t, h=h: t.matmul(
                            pS[0:64, 0:65], kctm[:, h * 64:(h + 1) * 64],
                            vtma[:, h * 65:(h + 1) * 65],
                            start=True, stop=True),
                            reads=[("t", "kctm"), ("t", "vtma")],
                            writes=[("psum", "pS")], tag="h.smm")
                        P.op("tensor", lambda t, h=h: t.matmul(
                            pS[0:64, 65:130], kstm[:, h * 64:(h + 1) * 64],
                            vtma[:, h * 65:(h + 1) * 65],
                            start=True, stop=True),
                            reads=[("t", "kstm"), ("t", "vtma")],
                            writes=[("psum", "pS")], tag="h.smm2")
                        P.op("vector", lambda v, h=h: v.tensor_tensor(
                            state[0:64, h * 130:(h + 1) * 130],
                            state[0:64, h * 130:(h + 1) * 130],
                            pS[0:64, 0:130], op=ALU.add),
                            reads=[("t", "state"), ("psum", "pS")],
                            writes=[("t", "state")], tag="h.supd")
                    P.op("vector", lambda v: v.tensor_copy(stateb[:], state[:]),
                         reads=[("t", "state")], writes=[("t", "stateb")], tag="c.s16")
                for m in range(8):
                    pban, kpb = (pa0, ("psum", "pa0")) if m % 2 == 0 else (pa1, ("psum", "pa1"))
                    for k in range(8):
                        P.op("tensor", lambda t, m=m, k=k, pban=pban: t.matmul(
                            pban[:, 0:NB],
                            wsb["wo"][:, (k * 8 + m) * 128:(k * 8 + m + 1) * 128],
                            outT[:, k * NB:(k + 1) * NB],
                            start=(k == 0), stop=(k == 7)),
                            reads=[("w", "wo"), ("t", "outT")],
                            writes=[kpb], tag="wo.mm")
                    P.op("vector", lambda v, m=m, pban=pban: v.tensor_scalar(
                        ln1.xc[:, m * NB:(m + 1) * NB], pban[:, 0:NB],
                        C["bo"][:, m:m + 1], None, op0=ALU.add),
                        reads=[("psum", "pa" + str(m % 2)), ("c", "bo")],
                        writes=[ln1.kxc], tag="wo.ev")
                ryrep = ln1.run_inplace(C, pstat, kps, prep, kpr)
                ln1.add_into(rr(x2b, 8), [("t", "x2b")], ryrep, kpr)
                P.dma("sync", x3_h[s].rearrange("p (f t) -> p f t", f=8)[:, :, c0:c1_],
                      rr(x2b, 8), reads=[("t", "x2b")], tag="x3.st")



def emit_l1ffn(P, es, nc, C0, x3_h, out_h, T, SEQS, NB, nblk):
    BIG = 1.0e6
    if True:
        cw1_h = nc.dram_tensor("cw1", [128, 16 * 128], F32, kind="ExternalInput").ap()
        C = dict(C0)
        C.update(load_consts_f(P, es, nc, [
            ("sigraw", (64, 1024)),
            ("knots", (1, 8)), ("temp", (1, 1)), ("osc", (1, 1)),
            ("cw2", (128, 4)), ("cb1", (128, 2)), ("cb2", (1, 2)),
            ("tabsp", (64, 65)), ("dirs", (64, 1024)),
            ("gamma", (128, 8)), ("beta", (128, 8)),
            ("grp8", (64, 8)), ("iota8x4", (128, 32)), ("iotaMB", (128, 32)),
            ("iota64", (128, 64)),
        ]))
        cast_cf(P, es, nc, C, "cw2", (128, 4))
        cast_cf(P, es, nc, C, "tabsp", (64, 65))
        cast_cf(P, es, nc, C, "dirs", (64, 1024))
        cast_cf(P, es, nc, C, "grp8", (64, 8))

        stage = alloc(es, nc, "stage", [128, 2048], F32)
        cw1b = alloc(es, nc, "cw1b", [128, 16 * 128], BF16)
        P.dma("sync", stage[:], cw1_h[:, 0:2048], writes=[("stage",)], tag="w.l")
        P.op("vector", lambda v: v.tensor_copy(cw1b[:], stage[:]),
             reads=[("stage",)], writes=[("w", "cw1")], tag="w.c")

        pstat = psum1(es, nc, "pstat")
        prep = psum1(es, nc, "prep")
        pa0 = psum1(es, nc, "pa0")
        pa1 = psum1(es, nc, "pa1")
        pg = psum1(es, nc, "pg")
        pdb = es.enter_context(nc.psum_tensor(uname("pdb"), [128, 1024], BF16, unsafe_allow_realloc=True))
        pdf = psum1(es, nc, "pdf")
        pdf2 = psum1(es, nc, "pdf2")
        kps, kpr = ("psum", "pstat"), ("psum", "prep")

        sigq = alloc(es, nc, "sigq", [64, 1024], F32)
        sigt = alloc(es, nc, "sigt", [64, 1024], F32)
        sigb = alloc(es, nc, "sigb", [64, 1024], BF16)
        csig = alloc(es, nc, "csig", [8, 1024], F32)
        csigb = alloc(es, nc, "csigb", [8, 1024], BF16)
        sigsT = alloc(es, nc, "sigsT", [128, 8 * 64], BF16)
        csigT = alloc(es, nc, "csigT", [128, 8 * 8], BF16)
        P.op("vector", lambda v: v.tensor_scalar(
            sigq[:], C["sigraw"][:], 0.3, None, op0=ALU.is_gt),
            reads=[("c", "sigraw")], writes=[("t", "sigq")], tag="s.gt")
        P.op("vector", lambda v: v.tensor_scalar(
            sigt[:], C["sigraw"][:], -0.3, None, op0=ALU.is_lt),
            reads=[("c", "sigraw")], writes=[("t", "sigt")], tag="s.lt")
        P.op("vector", lambda v: v.tensor_tensor(
            sigq[:], sigq[:], sigt[:], op=ALU.subtract),
            reads=[("t", "sigq"), ("t", "sigt")], writes=[("t", "sigq")], tag="s.sub")
        P.op("vector", lambda v: v.tensor_copy(sigb[:], sigq[:]),
             reads=[("t", "sigq")], writes=[("t", "sigb")], tag="s.cast")
        for half in range(2):
            P.op("tensor", lambda t, half=half: t.matmul(
                pg[0:8, half * 256:half * 256 + 256] if False else pg[0:8, 0:512],
                C["grp8_b"][:], sigb[:, half * 512:(half + 1) * 512],
                start=True, stop=True),
                reads=[("t", "sigb"), ("c", "grp8_b")], writes=[("psum", "pg")],
                tag="s.cmm")
            P.op("scalar", lambda s, half=half: s.activation(
                csig[:, half * 512:(half + 1) * 512], pg[0:8, 0:512], AF.Sign),
                reads=[("psum", "pg")], writes=[("t", "csig")], tag="s.sign")
        P.op("vector", lambda v: v.tensor_copy(csigb[:], csig[:]),
             reads=[("t", "csig")], writes=[("t", "csigb")], tag="s.ccast")
        for f in range(8):
            P.op("tensor", lambda t, f=f: t.transpose(
                pdb[:, f * 64:(f + 1) * 64],
                sigb[:, f * 128:(f + 1) * 128], C["idf_b"][0:64, 0:64]),
                reads=[("t", "sigb"), ("c", "idf_b")], writes=[("psum", "pdb")],
                tag="s.tp")
        P.op("vector", lambda v: v.tensor_copy(sigsT[:], pdb[:, 0:512]),
             reads=[("psum", "pdb")], writes=[("t", "sigsT")], tag="s.tpev")
        for f in range(8):
            P.op("tensor", lambda t, f=f: t.transpose(
                pdb[:, 512 + f * 8: 512 + (f + 1) * 8],
                csigb[:, f * 128:(f + 1) * 128], C["idf_b"][0:8, 0:8]),
                reads=[("t", "csigb"), ("c", "idf_b")], writes=[("psum", "pdb")],
                tag="s.ctp")
        P.op("vector", lambda v: v.tensor_copy(csigT[:], pdb[:, 512:576]),
             reads=[("psum", "pdb")], writes=[("t", "csigT")], tag="s.ctpev")
        slp = alloc(es, nc, "slp", [1, 8], F32)
        slpb = alloc(es, nc, "slpb", [128, 8], F32)
        P.op("vector", lambda v: v.tensor_tensor(
            slp[0:1, 0:7], C["knots"][0:1, 1:8], C["knots"][0:1, 0:7],
            op=ALU.subtract),
            reads=[("c", "knots")], writes=[("t", "slp")], tag="k.slope")
        P.op("vector", lambda v: v.tensor_copy(slp[0:1, 7:8], C["knots"][0:1, 0:1]),
             reads=[("c", "knots")], writes=[("t", "slp")], tag="k.k0")
        P.op("tensor", lambda t: t.matmul(prep[:, 0:8], C["c1"][:], slp[:],
                                          start=True, stop=True),
             reads=[("t", "slp"), ("c", "c1")], writes=[kpr], tag="k.rep")
        P.op("scalar", lambda s: s.copy(slpb[:], prep[:, 0:8]),
             reads=[kpr], writes=[("t", "slpb")], tag="k.repev")
        itp = alloc(es, nc, "itp", [1, 1], F32)
        itpb = alloc(es, nc, "itpb", [128, 1], F32)
        oscb = alloc(es, nc, "oscb", [128, 1], F32)
        P.op("scalar", lambda s: s.activation(itp[:], C["temp"][:], AF.Identity,
                                              bias=1e-6),
             reads=[("c", "temp"), ("c", "fc")], writes=[("t", "itp")], tag="t.add")
        P.op("vector", lambda v: v.reciprocal(itp[:], itp[:]),
             reads=[("t", "itp")], writes=[("t", "itp")], tag="t.rec")
        P.op("tensor", lambda t: t.matmul(prep[:, 0:1], C["c1"][:], itp[:],
                                          start=True, stop=True),
             reads=[("t", "itp"), ("c", "c1")], writes=[kpr], tag="t.rep")
        P.op("scalar", lambda s: s.copy(itpb[:], prep[:, 0:1]),
             reads=[kpr], writes=[("t", "itpb")], tag="t.repev")
        P.op("tensor", lambda t: t.matmul(prep[:, 0:1], C["c1"][:], C["osc"][:],
                                          start=True, stop=True),
             reads=[("c", "osc"), ("c", "c1")], writes=[kpr], tag="o.rep")
        P.op("scalar", lambda s: s.copy(oscb[:], prep[:, 0:1]),
             reads=[kpr], writes=[("t", "oscb")], tag="o.repev")

        x3b = alloc(es, nc, "x3b", [128, 8 * NB], F32)
        xfb = alloc(es, nc, "xfb", [128, 8 * NB], F32)
        csc = alloc(es, nc, "csc", [8, NB], F32)
        tscb = alloc(es, nc, "tscb", [64, NB], BF16)
        gtm = alloc(es, nc, "gtm", [128, 32], F32)
        stm = alloc(es, nc, "stm", [128, 32], F32)
        htmp = alloc(es, nc, "htmp", [128, 32], F32)
        htmp2 = alloc(es, nc, "htmp2", [128, 32], F32)
        acc = alloc(es, nc, "acc", [128, 32], F32)
        mx = alloc(es, nc, "mx", [128, 4], F32)
        cidx = alloc(es, nc, "cidx", [128, 4], F32)
        lidx = alloc(es, nc, "lidx", [128, 4], F32)
        tstm = alloc(es, nc, "tstm", [128, 4 * 64], F32)
        ohtm = alloc(es, nc, "ohtm", [128, 4 * 64], BF16)
        ohT = alloc(es, nc, "ohT", [64, NB], BF16)
        hb = alloc(es, nc, "hb", [128, 2 * NB], BF16)
        comp = alloc(es, nc, "comp", [1, 2 * NB], F32)
        spl = {nm2: alloc(es, nc, "spl_" + nm2, [16, NB], F32)
               for nm2 in ["w1a", "w1b", "b1", "w2"]}
        splb2 = alloc(es, nc, "splb2", [1, NB], F32)
        st1 = alloc(es, nc, "st1", [16, NB], F32)
        st2 = alloc(es, nc, "st2", [16, NB], F32)
        hhb = alloc(es, nc, "hhb", [16, NB], BF16)
        w2bb = alloc(es, nc, "w2bb", [16, NB], BF16)
        magr = alloc(es, nc, "magr", [128, NB], F32)
        dirt = alloc(es, nc, "dirt", [128, NB], F32)
        otm = alloc(es, nc, "otm", [128, 1024], F32)
        onecol16 = alloc(es, nc, "oc16", [16, 1], BF16)
        P.op("vector", lambda v: v.memset(onecol16[:], 1.0),
             writes=[("t", "oc16")], tag="oc16")
        ln1 = LNHelper(P, es, nc, NB, tag="1")

        for s in range(SEQS):
            for blk in range(nblk):
                c0, c1_ = blk * NB, (blk + 1) * NB
                P.dma("sync", rr(x3b, 8),
                      x3_h[s].rearrange("p (f t) -> p f t", f=8)[:, :, c0:c1_],
                      writes=[("t", "x3b")], tag="x3.ld")
                rrep = ln1.run(C, pstat, kps, prep, kpr, rr(x3b, 8), [("t", "x3b")])
                lnb = ln1.xb
                ln1.to_bf16(rr(lnb, 8), [ln1.kxb], rrep, kpr)
                ln1.mul_to(rr(xfb, 8), [("t", "xfb")], rrep, kpr)
                for k in range(8):
                    P.op("tensor", lambda t, k=k: t.matmul(
                        pg[0:8, 0:NB], csigT[:, k * 8:(k + 1) * 8],
                        lnb[:, k * NB:(k + 1) * NB],
                        start=(k == 0), stop=(k == 7)),
                        reads=[("t", "csigT"), ln1.kxb], writes=[("psum", "pg")],
                        tag="b.cmm")
                P.op("scalar", lambda s_: s_.copy(csc[:], pg[0:8, 0:NB]),
                     reads=[("psum", "pg")], writes=[("t", "csc")], tag="b.cev")
                for k in range(8):
                    P.op("tensor", lambda t, k=k: t.matmul(
                        pg[0:64, 0:NB], sigsT[:, k * 64:(k + 1) * 64],
                        lnb[:, k * NB:(k + 1) * NB],
                        start=(k == 0), stop=(k == 7)),
                        reads=[("t", "sigsT"), ln1.kxb], writes=[("psum", "pg")],
                        tag="b.tmm")
                P.op("scalar", lambda s_: s_.copy(tscb[:], pg[0:64, 0:NB]),
                     reads=[("psum", "pg")], writes=[("t", "tscb")], tag="b.tev")
                P.op("vector", lambda v: v.tensor_scalar(
                    csc[:], csc[:], itpb[0:8, 0:1], None, op0=ALU.mult),
                    reads=[("t", "csc"), ("t", "itpb")], writes=[("t", "csc")],
                    tag="b.it")
                P.op("scalar", lambda s_: s_.activation(csc[:], csc[:], AF.Sigmoid),
                     reads=[("t", "csc")], writes=[("t", "csc")], tag="b.sig")
                for st in range(4):
                    P.op("tensor", lambda t, st=st: t.transpose(
                        pdf[:, st * 8:(st + 1) * 8],
                        csc[:, st * 128:(st + 1) * 128], C["idf"][0:8, 0:8]),
                        reads=[("t", "csc"), ("c", "idf")], writes=[("psum", "pdf")],
                        tag="b.gtp")
                P.op("vector", lambda v: v.tensor_copy(gtm[:], pdf[:, 0:32]),
                     reads=[("psum", "pdf")], writes=[("t", "gtm")], tag="b.gtpev")

                def calibrate(src, srck, dst, dstk, width):
                    for j in range(7):
                        P.op("vector", lambda v, j=j: v.tensor_scalar(
                            htmp[:, 0:width], src[:, 0:width], 7.0, float(-j),
                            op0=ALU.mult, op1=ALU.add),
                            reads=[srck], writes=[("t", "htmp")], tag="cal.h")
                        P.op("vector", lambda v: v.tensor_scalar(
                            htmp[:, 0:width], htmp[:, 0:width], 0.0, 1.0,
                            op0=ALU.max, op1=ALU.min),
                            reads=[("t", "htmp")], writes=[("t", "htmp")], tag="cal.c")
                        if j == 0:
                            P.op("vector", lambda v: v.tensor_scalar(
                                dst[:, 0:width], htmp[:, 0:width],
                                slpb[:, 0:1], slpb[:, 7:8],
                                op0=ALU.mult, op1=ALU.add),
                                reads=[("t", "htmp"), ("t", "slpb")],
                                writes=[dstk], tag="cal.a0")
                        else:
                            P.op("vector", lambda v, j=j: v.scalar_tensor_tensor(
                                dst[:, 0:width], htmp[:, 0:width],
                                slpb[:, j:j + 1], dst[:, 0:width],
                                op0=ALU.mult, op1=ALU.add),
                                reads=[("t", "htmp"), ("t", "slpb"), dstk],
                                writes=[dstk], tag="cal.a")

                calibrate(gtm, ("t", "gtm"), acc, ("t", "acc"), 32)

                def argmax8(gsrc, gsrck, idxdst, idxk):
                    P.op("vector", lambda v: v.tensor_reduce(
                        mx[:], gsrc.ap().rearrange("p (s e) -> p s e", s=4),
                        axis=AX.X, op=ALU.max),
                        reads=[gsrck], writes=[("t", "mx")], tag="am.max")
                    P.op("vector", lambda v: v.tensor_tensor(
                        htmp.ap().rearrange("p (s e) -> p s e", s=4),
                        gsrc.ap().rearrange("p (s e) -> p s e", s=4),
                        mx.ap().unsqueeze(2).broadcast_to([128, 4, 8]),
                        op=ALU.is_ge),
                        reads=[gsrck, ("t", "mx")], writes=[("t", "htmp")], tag="am.eq")
                    P.op("vector", lambda v: v.tensor_tensor(
                        htmp2[:], htmp[:], C["iotaMB"][:], op=ALU.mult),
                        reads=[("t", "htmp"), ("c", "iotaMB")],
                        writes=[("t", "htmp2")], tag="am.mb")
                    P.op("vector", lambda v: v.tensor_scalar(
                        htmp2[:], htmp2[:], BIG, None, op0=ALU.add),
                        reads=[("t", "htmp2")], writes=[("t", "htmp2")], tag="am.big")
                    P.op("vector", lambda v: v.tensor_reduce(
                        idxdst[:], htmp2.ap().rearrange("p (s e) -> p s e", s=4),
                        axis=AX.X, op=ALU.min),
                        reads=[("t", "htmp2")], writes=[idxk], tag="am.min")

                argmax8(acc, ("t", "acc"), cidx, ("t", "cidx"))
                for st in range(4):
                    P.op("tensor", lambda t, st=st: t.transpose(
                        pdb[:, st * 64:(st + 1) * 64],
                        tscb[:, st * 128:(st + 1) * 128], C["idf_b"][0:64, 0:64]),
                        reads=[("t", "tscb"), ("c", "idf_b")],
                        writes=[("psum", "pdb")], tag="b.ttp")
                P.op("vector", lambda v: v.tensor_copy(tstm[:], pdb[:, 0:256]),
                     reads=[("psum", "pdb")], writes=[("t", "tstm")], tag="b.ttpev")
                for c in range(8):
                    P.op("vector", lambda v, c=c: v.tensor_scalar(
                        htmp[:, 0:4], cidx[:], float(c), None, op0=ALU.is_equal),
                        reads=[("t", "cidx")], writes=[("t", "htmp")], tag="sel.eq")
                    P.op("vector", lambda v, c=c: v.tensor_tensor(
                        htmp2.ap().rearrange("p (s e) -> p s e", s=4),
                        tstm.ap().rearrange("p (s e) -> p s e", s=4)[:, :, c * 8:(c + 1) * 8],
                        htmp[:, 0:4].unsqueeze(2).broadcast_to([128, 4, 8]),
                        op=ALU.mult),
                        reads=[("t", "tstm"), ("t", "htmp")],
                        writes=[("t", "htmp2")], tag="sel.m")
                    if c == 0:
                        P.op("vector", lambda v: v.tensor_copy(stm[:], htmp2[:]),
                             reads=[("t", "htmp2")], writes=[("t", "stm")], tag="sel.i")
                    else:
                        P.op("vector", lambda v: v.tensor_tensor(
                            stm[:], stm[:], htmp2[:], op=ALU.add),
                            reads=[("t", "stm"), ("t", "htmp2")],
                            writes=[("t", "stm")], tag="sel.a")
                P.op("vector", lambda v: v.tensor_scalar(
                    stm[:], stm[:], itpb[:, 0:1], None, op0=ALU.mult),
                    reads=[("t", "stm"), ("t", "itpb")], writes=[("t", "stm")],
                    tag="sel.it")
                P.op("scalar", lambda s_: s_.activation(stm[:], stm[:], AF.Sigmoid),
                     reads=[("t", "stm")], writes=[("t", "stm")], tag="sel.sig")
                calibrate(stm, ("t", "stm"), acc, ("t", "acc"), 32)
                argmax8(acc, ("t", "acc"), lidx, ("t", "lidx"))
                P.op("vector", lambda v: v.scalar_tensor_tensor(
                    cidx[:], cidx[:], 8.0, lidx[:], op0=ALU.mult, op1=ALU.add),
                    reads=[("t", "cidx"), ("t", "lidx")], writes=[("t", "cidx")],
                    tag="b.tidx")
                for st in range(4):
                    P.op("vector", lambda v, st=st: v.tensor_scalar(
                        ohtm[:, st * 64:(st + 1) * 64], C["iota64"][:],
                        cidx[:, st:st + 1], None, op0=ALU.is_equal),
                        reads=[("t", "cidx"), ("c", "iota64")],
                        writes=[("t", "ohtm")], tag="b.oh")
                for st in range(4):
                    P.op("tensor", lambda t, st=st: t.transpose(
                        pdb[0:64, 256 + st * 128: 256 + (st + 1) * 128],
                        ohtm[:, st * 64:(st + 1) * 64], C["idf_b"][:]),
                        reads=[("t", "ohtm"), ("c", "idf_b")],
                        writes=[("psum", "pdb")], tag="b.ohtp")
                P.op("vector", lambda v: v.tensor_copy(
                    ohT[:], pdb[0:64, 256:768]),
                    reads=[("psum", "pdb")], writes=[("t", "ohT")], tag="b.ohev")
                for m in range(2):
                    pban, kpb = (pa0, ("psum", "pa0")) if m == 0 else (pa1, ("psum", "pa1"))
                    for k in range(8):
                        P.op("tensor", lambda t, m=m, k=k, pban=pban: t.matmul(
                            pban[:, 0:NB],
                            cw1b[:, (k * 2 + m) * 128:(k * 2 + m + 1) * 128],
                            lnb[:, k * NB:(k + 1) * NB],
                            start=(k == 0), stop=(k == 7)),
                            reads=[("w", "cw1"), ln1.kxb], writes=[kpb], tag="cp.mm")
                    P.op("scalar", lambda s_, m=m, pban=pban: s_.activation(
                        hb[:, m * NB:(m + 1) * NB], pban[:, 0:NB], AF.Gelu,
                        bias=C["cb1"][:, m:m + 1]),
                        reads=[(("psum", "pa0") if m == 0 else ("psum", "pa1")), ("c", "cb1")],
                        writes=[("t", "hb")], tag="cp.ev")
                for m2 in range(2):
                    for k in range(2):
                        P.op("tensor", lambda t, k=k, m2=m2: t.matmul(
                            pg[0:1, 0:NB], C["cw2_b"][:, k * 2 + m2: k * 2 + m2 + 1],
                            hb[:, k * NB:(k + 1) * NB],
                            start=(k == 0), stop=(k == 1)),
                            reads=[("c", "cw2_b"), ("t", "hb")], writes=[("psum", "pg")],
                            tag="cp.mm2")
                    P.op("scalar", lambda s_, m2=m2: s_.activation(
                        comp[0:1, m2 * NB:(m2 + 1) * NB], pg[0:1, 0:NB], AF.Tanh,
                        bias=C["cb2"][0:1, m2:m2 + 1]),
                        reads=[("psum", "pg"), ("c", "cb2")], writes=[("t", "comp")],
                        tag="cp.tanh")
                for nm2, lo, hi in [("w1a", 0, 16), ("w1b", 16, 32), ("b1", 32, 48),
                                    ("w2", 48, 64)]:
                    P.op("tensor", lambda t, lo=lo, hi=hi: t.matmul(
                        pg[0:16, 0:NB], C["tabsp_b"][:, lo:hi], ohT[:],
                        start=True, stop=True),
                        reads=[("c", "tabsp_b"), ("t", "ohT")],
                        writes=[("psum", "pg")], tag="g.mm")
                    P.op("vector", lambda v, nm2=nm2: v.tensor_copy(
                        spl[nm2][:], pg[0:16, 0:NB]),
                        reads=[("psum", "pg")], writes=[("t", "spl" + nm2)], tag="g.ev")
                P.op("tensor", lambda t: t.matmul(
                    pg[0:1, 0:NB], C["tabsp_b"][:, 64:65], ohT[:],
                    start=True, stop=True),
                    reads=[("c", "tabsp_b"), ("t", "ohT")], writes=[("psum", "pg")],
                    tag="g.mm2")
                P.op("vector", lambda v: v.tensor_copy(splb2[:], pg[0:1, 0:NB]),
                     reads=[("psum", "pg")], writes=[("t", "splb2")], tag="g.ev2")
                P.op("tensor", lambda t: t.matmul(
                    prep[0:16, 0:NB], C["c1"][0:1, 0:16], comp[0:1, 0:NB],
                    start=True, stop=True),
                    reads=[("t", "comp"), ("c", "c1")], writes=[kpr], tag="sp.r0")
                P.op("vector", lambda v: v.tensor_tensor(
                    st1[:], spl["w1a"][:], prep[0:16, 0:NB], op=ALU.mult),
                    reads=[("t", "splw1a"), kpr], writes=[("t", "st1")], tag="sp.m0")
                P.op("tensor", lambda t: t.matmul(
                    prep[0:16, 0:NB], C["c1"][0:1, 0:16], comp[0:1, NB:2 * NB],
                    start=True, stop=True),
                    reads=[("t", "comp"), ("c", "c1")], writes=[kpr], tag="sp.r1")
                P.op("vector", lambda v: v.tensor_tensor(
                    st2[:], spl["w1b"][:], prep[0:16, 0:NB], op=ALU.mult),
                    reads=[("t", "splw1b"), kpr], writes=[("t", "st2")], tag="sp.m1")
                P.op("vector", lambda v: v.tensor_tensor(
                    st1[:], st1[:], st2[:], op=ALU.add),
                    reads=[("t", "st1"), ("t", "st2")], writes=[("t", "st1")], tag="sp.a")
                P.op("vector", lambda v: v.tensor_tensor(
                    st1[:], st1[:], spl["b1"][:], op=ALU.add),
                    reads=[("t", "st1"), ("t", "splb1")], writes=[("t", "st1")], tag="sp.b")
                P.op("vector", lambda v: v.tensor_scalar(
                    hhb[:], st1[:], 0.0, None, op0=ALU.max),
                    reads=[("t", "st1")], writes=[("t", "hhb")], tag="sp.relu")
                P.op("vector", lambda v: v.tensor_copy(w2bb[:], spl["w2"][:]),
                     reads=[("t", "splw2")], writes=[("t", "w2bb")], tag="sp.w2c")
                P.op("vector", lambda v: v.tensor_tensor(
                    hhb[:], hhb[:], w2bb[:], op=ALU.mult),
                    reads=[("t", "hhb"), ("t", "w2bb")], writes=[("t", "hhb")], tag="sp.hw")
                P.op("tensor", lambda t: t.matmul(
                    pg[0:1, 0:NB], onecol16[:], hhb[:], start=True, stop=True),
                    reads=[("t", "hhb"), ("t", "oc16")], writes=[("psum", "pg")],
                    tag="sp.sum")
                P.op("vector", lambda v: v.tensor_scalar(
                    splb2[:], splb2[:], oscb[0:1, 0:1], None, op0=ALU.mult),
                    reads=[("t", "splb2"), ("t", "oscb")], writes=[("t", "splb2")],
                    tag="sp.b2s")
                P.op("vector", lambda v: v.scalar_tensor_tensor(
                    splb2[:], pg[0:1, 0:NB], oscb[0:1, 0:1], splb2[:],
                    op0=ALU.mult, op1=ALU.add),
                    reads=[("psum", "pg"), ("t", "oscb"), ("t", "splb2")],
                    writes=[("t", "splb2")], tag="sp.mag")
                P.op("tensor", lambda t: t.matmul(
                    prep[:, 0:NB], C["c1"][:], splb2[:], start=True, stop=True),
                    reads=[("t", "splb2"), ("c", "c1")], writes=[kpr], tag="sp.mrep")
                P.op("scalar", lambda s_: s_.copy(magr[:], prep[:, 0:NB]),
                     reads=[kpr], writes=[("t", "magr")], tag="sp.mrev")
                for m in range(8):
                    P.op("tensor", lambda t, m=m: t.matmul(
                        pdf[:, 0:NB], C["dirs_b"][:, m * 128:(m + 1) * 128], ohT[:],
                        start=True, stop=True),
                        reads=[("c", "dirs_b"), ("t", "ohT")],
                        writes=[("psum", "pdf")], tag="y.dmm")
                    P.op("vector", lambda v, m=m: v.tensor_tensor(
                        dirt[:], pdf[:, 0:NB], magr[:], op=ALU.mult),
                        reads=[("psum", "pdf"), ("t", "magr")],
                        writes=[("t", "dirt")], tag="y.dm")
                    P.op("vector", lambda v, m=m: v.tensor_tensor(
                        ln1.xc[:, m * NB:(m + 1) * NB], dirt[:],
                        xfb[:, m * NB:(m + 1) * NB], op=ALU.add),
                        reads=[("t", "dirt"), ("t", "xfb")],
                        writes=[ln1.kxc], tag="y.add")
                ryrep = ln1.run_inplace(C, pstat, kps, prep, kpr)
                ln1.scale_inplace(ryrep, kpr)
                for m in range(8):
                    P.op("vector", lambda v, m=m: v.tensor_scalar(
                        ln1.xc[:, m * NB:(m + 1) * NB], ln1.xc[:, m * NB:(m + 1) * NB],
                        C["gamma"][:, m:m + 1], C["beta"][:, m:m + 1],
                        op0=ALU.mult, op1=ALU.add),
                        reads=[ln1.kxc, ("c", "gamma"), ("c", "beta")],
                        writes=[ln1.kxc], tag="y.gb")
                rzrep = ln1.run_inplace(C, pstat, kps, prep, kpr)
                ln1.add_into(rr(x3b, 8), [("t", "x3b")], rzrep, kpr)
                for tt in range(NB // 128):
                    for f in range(8):
                        pb_t = pdf if f < 4 else pdf2
                        P.op("tensor", lambda t, tt=tt, f=f, pb_t=pb_t: t.transpose(
                            pb_t[:, (f % 4) * 128:(f % 4 + 1) * 128],
                            x3b[:, f * NB + tt * 128: f * NB + (tt + 1) * 128],
                            C["idf"][:]),
                            reads=[("t", "x3b"), ("c", "idf")],
                            writes=[("psum", "pdf" if f < 4 else "pdf2")], tag="e.tp")
                    P.op("vector", lambda v: v.tensor_copy(otm[:, 0:512], pdf[:, 0:512]),
                         reads=[("psum", "pdf")], writes=[("t", "otm")], tag="e.ev")
                    P.op("vector", lambda v: v.tensor_copy(otm[:, 512:1024], pdf2[:, 0:512]),
                         reads=[("psum", "pdf2")], writes=[("t", "otm")], tag="e.ev2")
                    P.dma("sync", out_h[s, c0 + tt * 128: c0 + (tt + 1) * 128, :], otm[:],
                          reads=[("t", "otm")], tag="e.st")



from concourse.bass_utils import run_bass_kernel_spmd

NCORES = 8
T_FULL = 4096
SEQS_PER_CORE = 2

_prog_cache = {}


def kernel(**inputs):
    w = inputs
    T = T_FULL
    x = np.asarray(w["L0_attn_x"], dtype=np.float32)
    shards = [np.ascontiguousarray(x[2 * i:2 * i + 2]) for i in range(NCORES)]
    if "fused" not in _prog_cache:
        nc_ = bass.Bass("TRN2", target_bir_lowering=False)
        build_fused(nc_, T=T, SEQS=SEQS_PER_CORE)
        _prog_cache["fused"] = nc_
    nc_ = _prog_cache["fused"]
    in_maps = [fused_inputs(s, w, T=T) for s in shards]
    res = run_bass_kernel_spmd(nc_, in_maps, core_ids=list(range(NCORES)))
    return np.concatenate(
        [np.asarray(r["out"], dtype=np.float32) for r in res.results], axis=0)
